# version 01
# baseline (speedup 1.0000x reference)
import math
import numpy as np
from contextlib import ExitStack

import concourse.bass as bass
import concourse.mybir as mybir
from concourse.bass_utils import run_bass_kernel_spmd

F32 = mybir.dt.float32
BF16 = mybir.dt.bfloat16
I32 = mybir.dt.int32
AF = mybir.ActivationFunctionType
ALU = mybir.AluOpType
AX = mybir.AxisListType.X

B, T, E, H, D = 2, 8192, 2048, 16, 128
HID = 5632
NE = E // 128
NH = HID // 128
TQ = 2048
TKV = 4096
ROPE_BASE = 1e6
LN_EPS = 1e-5
Q_EPS = 1e-8
SCALE = 1.0 / math.sqrt(D)
QCNT = float(HID * E)

ENGINES = ("pe", "act", "dve", "gp", "sp")
CLASS_OF_L = {0: 0, 2: 1, 1: 2, 3: 3}
OTHER = {0: 1, 1: 0, 2: 3, 3: 2}


def qblocks(j):
    return [4 * i + j for i in range(16)]


def local_blocks(j):
    return qblocks(j) + qblocks(OTHER[j])


def token_ids(j):
    return np.concatenate([np.arange(g * 128, (g + 1) * 128) for g in local_blocks(j)])


class Plan:
    def __init__(self):
        self.prog = {e: [] for e in ENGINES}
        self.sem = {}
        self.sem_static = {}

    def cnt(self, s):
        return self.sem.get(s, 0)

    def bump(self, s, n, static_n=None):
        self.sem[s] = self.cnt(s) + n
        self.sem_static[s] = self.sem_static.get(s, 0) + (n if static_n is None else static_n)
        return self.sem[s]

    def op(self, eng, fn, waits=(), inc=None, n=1):
        if inc is not None:
            self.bump(inc, n)
        self.prog[eng].append((tuple(x for x in waits if x is not None), fn, inc, n))


class Stream:

    def __init__(self, plan, queue, lanes):
        self.p, self.q, self.lanes, self.i = plan, queue, lanes, 0

    def load(self, fn, free=()):
        lane = self.lanes[self.i % 2]
        self.i += 1
        waits = []
        if self.p.cnt(lane) > 0:
            waits.append((lane, self.p.cnt(lane)))
        waits += [f for f in free if f is not None]
        self.p.op(self.q, fn, waits=waits, inc=lane, n=16)
        return (lane, self.p.cnt(lane))


class Rot:
    def __init__(self, n):
        self.n = n
        self.rel = [None] * n
        self.i = 0

    def acquire(self):
        lane = self.i % self.n
        self.i += 1
        return lane, self.rel[lane]

    def release(self, lane, sem, val):
        self.rel[lane] = (sem, val)



class MK:
    def __init__(self, debug=0):
        self.debug = debug
        nc = self.nc = bass.Bass("TRN2", target_bir_lowering=False, debug=False)
        p = self.p = Plan()
        self.st = ExitStack()
        self.cell = {}

        def din(name, shape, dt=F32):
            return nc.dram_tensor(name, shape, dt, kind="ExternalInput").ap()

        self.x_in = din("x", [E, TKV])
        self.wqT_t = din("wqT_t", [NE, E, 128])
        self.wkT_t = din("wkT_t", [NE, E, 128])
        self.woT_t = din("woT_t", [NE, E, 128])
        self.wvT = din("wvT", [E, E])
        self.w1T_t = din("w1T_t", [NH, E, 128])
        self.w2T_t = din("w2T_t", [NH, E, 128])
        self.w3T_t = din("w3T_t", [NE, HID, 128])
        self.cos_in = din("cosT", [D, TKV])
        self.sin_in = din("sinT", [D, TKV])
        self.masks_in = din("masks", [128, 16 * 512], BF16)
        self.cls_in = din("cls", [1, 1], I32)
        self.nonce_in = din("nonce", [1, 1], I32)

        self.out_d = nc.dram_tensor("out", [E, TQ], F32, kind="ExternalOutput").ap()
        self.q_dram = nc.dram_tensor("q_dram", [E, TQ], BF16).ap()
        self.attn_dram = nc.dram_tensor("attn_dram", [E, TQ], BF16).ap()
        self.xmid_dram = nc.dram_tensor("xmid_dram", [E, TQ], F32).ap()
        self.attn2_dram = nc.dram_tensor("attn2_dram", [E, TQ], BF16).ap()
        self.qw1_dram = nc.dram_tensor("qw1_dram", [NH, 128, E], BF16).ap()
        self.qw2_dram = nc.dram_tensor("qw2_dram", [NH, 128, E], BF16).ap()
        self.qw3_dram = nc.dram_tensor("qw3_dram", [NE, 128, HID], BF16).ap()
        self.KG = nc.dram_tensor("KG", [E, T], BF16, addr_space="Shared").ap()
        self.VG = nc.dram_tensor("VG", [T, E], BF16, addr_space="Shared").ap()
        self.GP = nc.dram_tensor("GP", [1, 128], F32, addr_space="Shared").ap()
        self.FLAG = nc.dram_tensor("FLAG", [1, 2], I32, addr_space="Shared").ap()

        self.dbg = {}
        if debug == 1:
            self.dbg["kg"] = nc.dram_tensor("dbg_kg", [E, T], BF16, kind="ExternalOutput").ap()
            self.dbg["vg"] = nc.dram_tensor("dbg_vg", [T, E], BF16, kind="ExternalOutput").ap()
            self.dbg["q"] = nc.dram_tensor("dbg_q", [E, TQ], BF16, kind="ExternalOutput").ap()
            self.dbg["gp"] = nc.dram_tensor("dbg_gp", [1, 128], F32, kind="ExternalOutput").ap()
        if debug == 2:
            self.dbg["at"] = nc.dram_tensor("dbg_at", [E, TQ], BF16, kind="ExternalOutput").ap()
        if debug == 3:
            self.dbg["xm"] = nc.dram_tensor("dbg_xm", [E, TQ], F32, kind="ExternalOutput").ap()
        if debug == 4:
            self.dbg["xn2"] = nc.dram_tensor("dbg_xn2", [E, 1024], BF16, kind="ExternalOutput").ap()
            self.dbg["y3"] = nc.dram_tensor("dbg_y3", [E, 1024], BF16, kind="ExternalOutput").ap()
            self.dbg["sig"] = nc.dram_tensor("dbg_sig", [128, 1024], BF16, kind="ExternalOutput").ap()
            self.dbg["hid"] = nc.dram_tensor("dbg_hid", [128, 1024], BF16, kind="ExternalOutput").ap()

        def sbt(name, shape, dt):
            return self.st.enter_context(nc.sbuf_tensor(name, shape, dt))

        self.A = sbt("A", [128, 32768], BF16)
        self.Bb = sbt("Bb", [128, 45056], BF16)
        self.C = sbt("C", [128, 4096], F32)
        self.Eb = sbt("Eb", [128, 4096], BF16)
        self.Fb = sbt("Fb", [128, 5632], BF16)
        self.rows = sbt("rows", [1, 4096], F32)
        self.misc_f = sbt("misc_f", [128, 64], F32)
        self.mrow = sbt("mrow", [1, 128], F32)
        self.gstat = sbt("gstat", [128, 72], F32)
        self.gprow = sbt("gprow", [1, 128], F32)
        self.onecol_b = sbt("onecol_b", [128, 1], BF16)
        self.onecol_f = sbt("onecol_f", [128, 1], F32)
        self.onerow_f = sbt("onerow_f", [1, 128], F32)
        self.cidt = sbt("cidt", [1, 2], I32)
        self.flagt = sbt("flagt", [1, 2], I32)
        self.sig4 = sbt("sig4", [128, 1024], BF16)

        A, Bb, C, Eb = self.A, self.Bb, self.C, self.Eb
        self.xb = [A[:, c * 2048:(c + 1) * 2048] for c in range(NE)]
        self.wv_sp = [Bb[:, i * 8192:(i + 1) * 8192] for i in range(2)]
        self.wt_b = [Bb[:, 16384 + i * 2048:16384 + (i + 1) * 2048] for i in range(2)]
        self.ksb = [Bb[:, 20480 + i * 2048:20480 + (i + 1) * 2048] for i in range(2)]
        self.sqs = [Bb[:, 24576 + i * 2048:24576 + (i + 1) * 2048] for i in range(2)]
        self.ab_A = Bb[:, 28672:30720]
        self.ab_B = Bb[:, 30720:32768]
        self.t12 = [Bb[:, 32768:34816], Bb[:, 39936:41984]]
        self.evacb = [Bb[:, 34816 + i * 2048:34816 + (i + 1) * 2048] for i in range(2)]
        self.vtile = [Bb[:, 38912 + i * 512:38912 + (i + 1) * 512] for i in range(2)]
        self.cosb = Eb[:, 0:2048]
        self.sinb = Eb[:, 2048:4096]
        self.gbuf = C[:, 0:2048]
        self.rrow = self.rows[0:1, 0:512]
        self.rowX = self.rows[0:1, 0:2048]
        self.rowQ = self.rows[0:1, 2048:4096]
        self.rowM = C[0:1, 2048:4096]

        self.kh = [A[:, i * 8192:(i + 1) * 8192] for i in range(2)]
        self.vh = [A[:, 16384 + i * 8192:16384 + (i + 1) * 8192] for i in range(2)]
        self.masksb = Bb[:, 0:8192]
        self.qhb = [Bb[:, 8192 + i * 2048:8192 + (i + 1) * 2048] for i in range(2)]
        self.pbuf = [Bb[:, 12288 + i * 1024:12288 + (i + 1) * 1024] for i in range(2)]
        self.at_t = [Bb[:, 14336 + i * 512:14336 + (i + 1) * 512] for i in range(2)]
        self.rbig = C[:, 0:512]

        self.worhs = [Bb[:, i * 8192:(i + 1) * 8192] for i in range(2)]
        self.wt3_b = [Bb[:, 16384 + i * 2048:16384 + (i + 1) * 2048] for i in range(2)]
        self.asq3 = [Bb[:, 20480 + i * 2048:20480 + (i + 1) * 2048] for i in range(2)]
        self.xre = C[:, 0:2048]
        self.xmo = C[:, 2048:4096]

        self.xn2 = [A[:, c * 1024:(c + 1) * 1024] for c in range(NE)]
        self.y3b = [A[:, 16384 + c * 1024:16384 + (c + 1) * 1024] for c in range(NE)]
        self.hidv = [Bb[:, h * 1024:(h + 1) * 1024] for h in range(NH)]
        self.qw1v = Eb[:, 0:2048]
        self.qw2v = Eb[:, 2048:4096]
        self.qw3v = self.Fb[:, 0:5632]
        self.raw1 = C[:, 0:2048]
        self.raw2 = C[:, 2048:4096]
        self.ab4_A = self.rows[0:1, 0:1024]
        self.ab4_B = self.rows[0:1, 1024:2048]
        self.row4X = self.rows[0:1, 2048:3072]
        self.row4Q = self.rows[0:1, 3072:4096]

        psume = self.st.enter_context
        self.psS0 = psume(nc.psum_tensor("psS0", [128, 1024], F32))
        self.psS1 = psume(nc.psum_tensor("psS1", [128, 1024], F32))
        self.psA = psume(nc.psum_tensor("psA", [128, 512], F32))
        self.psB = psume(nc.psum_tensor("psB", [128, 512], F32))
        self.psC = psume(nc.psum_tensor("psC", [128, 512], F32))
        self.psD = psume(nc.psum_tensor("psD", [128, 512], F32))
        self.ps_acc = [self.psA, self.psB]
        self.psS = [self.psS0, self.psS1]
        self.statrows = [self.psS0[0:1, 0:512], self.psS0[0:1, 512:1024],
                         self.psS1[0:1, 0:512], self.psS1[0:1, 512:1024],
                         self.psA[0:1, 0:512], self.psB[0:1, 0:512],
                         self.psC[0:1, 0:512], self.psD[0:1, 0:512]]

        self.sems = {}
        for s in ["gl0", "gl1", "gl2", "gl3",
                  "sk0", "sk1", "sv0", "sv1", "sq0", "sq1", "sm0", "sm1",
                  "sr0", "sr1", "cdv", "cac",
                  "kw0", "kw1", "vw0", "vw1", "qw0", "qw1", "aw0", "aw1",
                  "xw0", "xw1", "ww0", "ww1", "ow0", "ow1",
                  "sw", "sw0", "sw1", "poll", "bar",
                  "pe", "ac", "dv", "pc", "ow"]:
            self.sems[s] = self.st.enter_context(nc.semaphore(s))

        self.gp_st = Stream(p, "gp", ["gl0", "gl1"])
        self.gp_st2 = Stream(p, "gp", ["gl2", "gl3"])
        self.sc_k = Stream(p, "act", ["sk0", "sk1"])
        self.sc_v = Stream(p, "act", ["sv0", "sv1"])
        self.sc_q = Stream(p, "act", ["sq0", "sq1"])
        self.sc_m = Stream(p, "act", ["sm0", "sm1"])
        self.t12_rot = Rot(2)

        self.acc_rot = Rot(2)
        self.wt_rot = Rot(2)
        self.ev_rot = Rot(2)
        self.ksb_rot = Rot(2)
        self.sqs_rot = Rot(2)
        self.vt_rot = Rot(2)
        self.stat_free = [None] * 8
        self.psS1row_free = None

    def w(self, eng, fn, waits=(), inc=None, n=1):
        self.p.op(eng, fn, waits, inc, n)

    def chain(self, eng, cell, instr, key="cdv"):
        cell[key] = cell.get(key, 0) + 1
        instr.then_inc(self.sems[key], 1)
        eng.wait_ge(self.sems[key], cell[key])

    def branch4(self, make, inc_sem, n=16, waits=(), eng="sp", reg="s_cls",
                nparts_hint=1):
        sems = self.sems

        nparts = [None]

        def attach(r):
            rs = r if isinstance(r, (list, tuple)) else [r]
            if nparts[0] is None:
                nparts[0] = len(rs)
            assert nparts[0] == len(rs)
            for x in rs:
                x.then_inc(sems[inc_sem], n)

        def fn(g, c):
            with g.If_eq(c[reg], 0):
                attach(make(g, 0))
            with g.Else():
                with g.If_eq(c[reg], 1):
                    attach(make(g, 1))
                with g.Else():
                    with g.If_eq(c[reg], 2):
                        attach(make(g, 2))
                    with g.Else():
                        attach(make(g, 3))
            return None
        self.p.bump(inc_sem, n * nparts_hint, static_n=4 * n * nparts_hint)
        self.p.prog[eng].append((tuple(x for x in waits if x is not None), fn, None, 0))
        return self.p.cnt(inc_sem)

    def swrite(self, make_fn, lanes, lane01, waits=()):
        sems = self.sems
        swl = lanes[lane01]
        wt = [x for x in waits if x is not None]
        if self.p.cnt(swl) > 0:
            wt.append((swl, self.p.cnt(swl)))
        self.p.bump(swl, 16)

        def fn(s_, c):
            make_fn(s_, c).then_inc(sems[swl], 16)
            return None
        self.p.prog["sp"].append((tuple(wt), fn, None, 0))
        return (swl, self.p.cnt(swl))

    def bwrite(self, make, lanes, lane01, nparts, waits=()):
        swl = lanes[lane01]
        wt = [x for x in waits if x is not None]
        if self.p.cnt(swl) > 0:
            wt.append((swl, self.p.cnt(swl)))
        val = self.branch4(make, swl, waits=wt, nparts_hint=nparts)
        return (swl, val)

    def phase0(self):
        p, w, sems = self.p, self.w, self.sems
        w("gp", lambda g, c: g.dma_start(out=self.cidt[0:1, 0:1], in_=self.cls_in[:, :]),
          inc="gl0", n=16)
        w("gp", lambda g, c: g.dma_start(out=self.cidt[0:1, 1:2], in_=self.nonce_in[:, :]),
          waits=[("gl0", 16)], inc="gl0", n=16)
        self.cid_ld = p.cnt("gl0")

        w("dve", lambda v, c: v.memset(self.onecol_b[:, :], 1.0))
        w("dve", lambda v, c: v.memset(self.onecol_f[:, :], 1.0))
        w("dve", lambda v, c: v.memset(self.onerow_f[:, :], 1.0))
        w("dve", lambda v, c: v.memset(self.gprow[:, :], 0.0))
        w("dve", lambda v, c: v.memset(self.mrow[:, :], 0.0), inc="dv")
        self.dv_const = p.cnt("dv")

    def phase1(self):
        p, w = self.p, self.w

        gseq = {0: [], 1: []}
        for wi, wt_src in ((0, self.w1T_t), (1, self.w2T_t)):
            for hc in range(NH):
                gseq[hc % 2].append(
                    (wi, wt_src[hc].rearrange("(c p) h -> p c h", p=128), 2048))
        for fc in range(NE):
            src = self.w3T_t[fc].rearrange("(c p) f -> p c f", p=128)
            for s0, ln in ((0, 2048), (2048, 2048), (4096, 1536)):
                gseq[fc % 2].append((2, src[:, s0 // 128:(s0 + ln) // 128, :], ln))
        ngp = len(gseq[0])
        assert len(gseq[1]) == ngp and ngp <= 72
        for i in range(ngp):
            assert gseq[0][i][0] == gseq[1][i][0] and gseq[0][i][2] == gseq[1][i][2]
        self.gseq, self.ngp = gseq, ngp
        self.gred_dv = [None] * ngp
        self.gw_cols = {wi: [i for i in range(ngp) if gseq[0][i][0] == wi]
                        for wi in range(3)}
        self._gamma_i = 0

        self.emit_half(0)
        self.emit_half(1)
        self.emit_gamma(ngp)
        self.gamma_final()
        self.barrier()

    def emit_gamma(self, k):
        p, w, sems = self.p, self.w, self.sems
        i0 = self._gamma_i
        for i in range(i0, min(i0 + k, self.ngp)):
            def gdma(s, c, i=i):
                w0, src0, ln0 = self.gseq[0][i]
                _, src1, _ = self.gseq[1][i]
                gout = self.gbuf[:, 0:ln0].rearrange("p (c h) -> p c h", h=128)
                with s.If_eq(c["a_par"], 0):
                    s.dma_start(out=gout, in_=src0).then_inc(sems["sm0"], 16)
                with s.Else():
                    s.dma_start(out=gout, in_=src1).then_inc(sems["sm0"], 16)
                return None
            waits = ([("dv", self.gred_dv[i - 1])] if i > 0
                     else [("dv", self.dv_const)])
            p.bump("sm0", 16, static_n=32)
            p.prog["act"].append((tuple(waits), gdma, None, 0))
            sl_val = p.cnt("sm0")

            def gred(v, c, i=i):
                _, _, ln = self.gseq[0][i]
                return v.tensor_reduce(self.gstat[:, i:i + 1], self.gbuf[:, 0:ln],
                                       axis=AX, op=ALU.add, apply_absolute_value=True)
            w("dve", gred, waits=[("sm0", sl_val)], inc="dv")
            self.gred_dv[i] = p.cnt("dv")
        self._gamma_i = min(i0 + k, self.ngp)

    def emit_half(self, half):
        p, w, cell = self.p, self.w, self.cell
        sems = self.sems
        xb, sqs, statrows = self.xb, self.sqs, self.statrows
        xoff = half * TQ

        if half == 1:
            self.stat_free[4] = self.acc_rot.rel[0]
            self.stat_free[5] = self.acc_rot.rel[1]
            for k in range(4):
                self.stat_free[k] = self.psS1row_free

        xld = []
        for cc in range(NE):
            def ldx(g, c, cc=cc, xoff=xoff):
                return g.dma_start(
                    out=xb[cc], in_=self.x_in[cc * 128:(cc + 1) * 128, xoff:xoff + TQ])
            free = [("pe", cell["h0_xn_users_pe"])] if half == 1 else []
            xld.append(self.gp_st.load(ldx, free=free))
        free = [("dv", cell["h0_rope_dv"])] if half == 1 else []
        cs0 = self.gp_st2.load(lambda g, c, xoff=xoff: g.dma_start(
            out=self.cosb, in_=self.cos_in[:, xoff:xoff + TQ]), free=free)
        cs1 = self.gp_st2.load(lambda g, c, xoff=xoff: g.dma_start(
            out=self.sinb, in_=self.sin_in[:, xoff:xoff + TQ]), free=free)

        for cc in range(NE):
            lane, rel = self.sqs_rot.acquire()

            def sq(s, c, cc=cc, lane=lane):
                return s.activation(out=sqs[lane], in_=xb[cc], func=AF.Square)
            w("act", sq, waits=[xld[cc], rel], inc="ac")
            sq_ac = p.cnt("ac")
            for sp in range(4):
                def mmx(t, c, cc=cc, sp=sp):
                    return t.matmul(statrows[sp], self.onecol_b[:, :],
                                    xb[cc][:, sp * 512:(sp + 1) * 512],
                                    start=(cc == 0), stop=(cc == NE - 1))
                wt_ = [xld[cc]] if sp == 0 else []
                if cc == 0:
                    wt_.append(self.stat_free[sp])
                    self.stat_free[sp] = None
                    if sp == 0:
                        wt_.append(("dv", self.dv_const))
                w("pe", mmx, waits=wt_, inc="pe")
            for sp in range(4):
                def mmq(t, c, cc=cc, sp=sp, lane=lane):
                    return t.matmul(statrows[4 + sp], self.onecol_b[:, :],
                                    sqs[lane][:, sp * 512:(sp + 1) * 512],
                                    start=(cc == 0), stop=(cc == NE - 1))
                wt_ = []
                if sp == 0:
                    wt_.append(("ac", sq_ac))
                if cc == 0:
                    wt_.append(self.stat_free[4 + sp])
                    self.stat_free[4 + sp] = None
                w("pe", mmq, waits=wt_, inc="pe")
            self.sqs_rot.release(lane, "pe", p.cnt("pe"))
        stats_pe = p.cnt("pe")

        rowX, rowQ, rowM = self.rowX, self.rowQ, self.rowM

        def rows_calc(v, c):
            ch = self.chain
            for sp in range(4):
                v.tensor_copy(rowX[0:1, sp * 512:(sp + 1) * 512], statrows[sp])
            for sp in range(4):
                i = v.tensor_copy(rowQ[0:1, sp * 512:(sp + 1) * 512], statrows[4 + sp])
            ch(v, c, i)
            ch(v, c, v.tensor_scalar(rowX, rowX, 1.0 / E, None, op0=ALU.mult))
            ch(v, c, v.tensor_scalar(rowQ, rowQ, 1.0 / E, None, op0=ALU.mult))
            ch(v, c, v.tensor_mul(rowM, rowX, rowX))
            ch(v, c, v.tensor_sub(rowQ, rowQ, rowM))
            return v.tensor_scalar(rowQ, rowQ, LN_EPS, None, op0=ALU.add)
        w("dve", rows_calc, waits=[("pe", stats_pe)], inc="dv")
        for sp in range(8):
            self.stat_free[sp] = ("dv", p.cnt("dv"))
        w("act", lambda s, c: s.activation(out=rowQ, in_=rowQ, func=AF.Sqrt),
          waits=[("dv", p.cnt("dv"))], inc="ac")

        def rows_fin(v, c):
            ch = self.chain
            ch(v, c, v.reciprocal(rowQ, rowQ))
            ch(v, c, v.tensor_mul(rowX, rowX, rowQ))
            return v.tensor_scalar(rowX, rowX, -1.0, None, op0=ALU.mult)
        w("dve", rows_fin, waits=[("ac", p.cnt("ac"))], inc="dv")
        rows_dv = p.cnt("dv")

        ab_ac = None
        for src_row, dst in ((rowQ, self.ab_A), (rowX, self.ab_B)):
            for sp in range(4):
                def bmm(t, c, src_row=src_row, sp=sp):
                    return t.matmul(self.psC[:, :], self.onerow_f[:, :],
                                    src_row[0:1, sp * 512:(sp + 1) * 512],
                                    start=True, stop=True)
                wt_ = [("dv", rows_dv)]
                if ab_ac is not None:
                    wt_.append(("ac", ab_ac))
                else:
                    wt_.append(self.stat_free[6])
                    self.stat_free[6] = None
                w("pe", bmm, waits=wt_, inc="pe")

                def bcp(s, c, dst=dst, sp=sp):
                    return s.copy(out=dst[:, sp * 512:(sp + 1) * 512], in_=self.psC[:, :])
                w("act", bcp, waits=[("pe", p.cnt("pe"))], inc="ac")
                ab_ac = p.cnt("ac")
        self.stat_free[6] = ("ac", ab_ac)
        self.rows_free = ("pe", p.cnt("pe"))

        for cc in range(NE):
            lane, rel = self.sqs_rot.acquire()

            def ap1(v, c, cc=cc, lane=lane):
                return v.tensor_mul(sqs[lane], xb[cc], self.ab_A)

            def ap2(v, c, cc=cc, lane=lane):
                return v.tensor_add(xb[cc], sqs[lane], self.ab_B)
            w("dve", ap1, waits=[("ac", ab_ac), rel], inc="dv")
            w("dve", ap2, waits=[("dv", p.cnt("dv"))], inc="dv")
            self.sqs_rot.release(lane, "dv", p.cnt("dv"))
        xn_dv = p.cnt("dv")

        self.emit_gamma(10)

        for which in (["k", "q"] if half == 0 else ["k"]):
            wt_src = self.wkT_t if which == "k" else self.wqT_t
            for fc in range(NE):
                wl_lane, wl_rel = self.wt_rot.acquire()

                def ldw(g, c, wt_src=wt_src, fc=fc, wl_lane=wl_lane):
                    return g.dma_start(
                        out=self.wt_b[wl_lane].rearrange("p (c f) -> p c f", f=128),
                        in_=wt_src[fc].rearrange("(c p) f -> p c f", p=128))
                wl = self.gp_st.load(ldw, free=[wl_rel])

                ev_lane, ev_rel = self.ev_rot.acquire()
                for sp in range(4):
                    a_lane, a_rel = self.acc_rot.acquire()
                    acc = self.ps_acc[a_lane]
                    for e_ in range(NE):
                        def mm(t, c, acc=acc, wl_lane=wl_lane, e_=e_, sp=sp):
                            return t.matmul(
                                acc[:, :], self.wt_b[wl_lane][:, e_ * 128:(e_ + 1) * 128],
                                xb[e_][:, sp * 512:(sp + 1) * 512],
                                start=(e_ == 0), stop=(e_ == NE - 1))
                        wt_ = []
                        if e_ == 0:
                            wt_ = [wl, ("dv", xn_dv), a_rel]
                        w("pe", mm, waits=wt_, inc="pe")
                    mm_pe = p.cnt("pe")

                    def ev(s, c, acc=acc, sp=sp, ev_lane=ev_lane):
                        return s.copy(out=self.evacb[ev_lane][:, sp * 512:(sp + 1) * 512],
                                      in_=acc[:, :])
                    w("act", ev,
                      waits=[("pe", mm_pe), ev_rel if sp == 0 else None], inc="ac")
                    self.acc_rot.release(a_lane, "ac", p.cnt("ac"))
                self.wt_rot.release(wl_lane, "pe", p.cnt("pe"))
                ev_ac = p.cnt("ac")

                k_lane, k_rel = self.ksb_rot.acquire()
                t_lane, t_rel = self.t12_rot.acquire()
                src, dst = self.evacb[ev_lane], self.ksb[k_lane]
                t12x, cosb, sinb = self.t12[t_lane], self.cosb, self.sinb
                srl = "sr0" if t_lane == 0 else "sr1"
                swap_waits = [("ac", ev_ac), t_rel]
                if p.cnt(srl) > 0:
                    swap_waits.append((srl, p.cnt(srl)))

                def swap(s, c, src=src, t12x=t12x, srl=srl):
                    s.dma_start(out=t12x[0:64, :],
                                in_=src[64:128, :]).then_inc(sems[srl], 16)
                    s.dma_start(out=t12x[64:128, :],
                                in_=src[0:64, :]).then_inc(sems[srl], 16)
                    return None
                p.bump(srl, 32)
                p.prog["act"].append((tuple(x for x in swap_waits if x is not None),
                                      swap, None, 0))
                swap_val = (srl, p.cnt(srl))

                def rope(v, c, src=src, dst=dst, t12x=t12x):
                    v.tensor_mul(dst, src, cosb)
                    self.chain(v, c, v.tensor_mul(t12x, t12x, sinb))
                    return v.tensor_add(dst, dst, t12x)
                w("dve", rope, waits=[swap_val, cs0, cs1, k_rel], inc="dv")
                rope_dv = p.cnt("dv")
                self.ev_rot.release(ev_lane, "dv", rope_dv)
                self.t12_rot.release(t_lane, "dv", rope_dv)

                sq_lane, sq_rel = self.sqs_rot.acquire()

                def nsq(s, c, k_lane=k_lane, sq_lane=sq_lane):
                    return s.activation(out=sqs[sq_lane], in_=self.ksb[k_lane],
                                        func=AF.Square)
                w("act", nsq, waits=[("dv", rope_dv), sq_rel], inc="ac")
                nsq_ac = p.cnt("ac")
                nrow_tgts = [self.psS0[0:1, 0:512], self.psS0[0:1, 512:1024],
                             self.psS1[0:1, 0:512], self.psS1[0:1, 512:1024]]
                for sp in range(4):
                    def nmm(t, c, sp=sp, sq_lane=sq_lane):
                        return t.matmul(nrow_tgts[sp], self.onecol_b[:, :],
                                        sqs[sq_lane][:, sp * 512:(sp + 1) * 512],
                                        start=True, stop=True)
                    wt_ = []
                    if sp == 0:
                        wt_ = [("ac", nsq_ac), self.psS1row_free]
                        wt_ += [self.stat_free[k] for k in range(4)]
                        for k in range(4):
                            self.stat_free[k] = None
                    w("pe", nmm, waits=wt_, inc="pe")
                self.sqs_rot.release(sq_lane, "pe", p.cnt("pe"))
                nmm_pe = p.cnt("pe")

                def nred(v, c, which=which, fc=fc, half=half):
                    ch = self.chain
                    if which == "q":
                        for sp in range(4):
                            i = v.tensor_reduce(self.mrow[0:1, 4 * fc + sp:4 * fc + sp + 1],
                                                nrow_tgts[sp], axis=AX, op=ALU.max)
                        return i
                    ch(v, c, v.tensor_reduce(self.mrow[0:1, 80:81],
                                             self.psS0[0:1, 0:1024],
                                             axis=AX, op=ALU.max))
                    ch(v, c, v.tensor_reduce(self.mrow[0:1, 81:82],
                                             self.psS1[0:1, 0:1024],
                                             axis=AX, op=ALU.max))
                    ch(v, c, v.tensor_max(self.mrow[0:1, 80:81],
                                          self.mrow[0:1, 80:81],
                                          self.mrow[0:1, 81:82]))
                    tgt = self.mrow[0:1, 64 + fc:64 + fc + 1]
                    if half == 1:
                        return v.tensor_max(tgt, tgt, self.mrow[0:1, 80:81])
                    return v.tensor_copy(tgt, self.mrow[0:1, 80:81])
                w("dve", nred, waits=[("pe", nmm_pe)], inc="dv")
                self.psS1row_free = ("dv", p.cnt("dv"))

                if which == "q":
                    def qwr(s_, c, fc=fc, k_lane=k_lane):
                        return s_.dma_start(
                            out=self.q_dram[fc * 128:(fc + 1) * 128, :],
                            in_=self.ksb[k_lane])
                    val = self.swrite(qwr, ("qw0", "qw1"), fc % 2,
                                      waits=[("dv", rope_dv)])
                    self.ksb_rot.release(k_lane, *val)
                else:
                    KG = self.KG

                    def kwr(s_, j, fc=fc, half=half, k_lane=k_lane):
                        jj = j if half == 0 else OTHER[j]
                        out = []
                        for blk in range(16):
                            g_ = 4 * blk + jj
                            out.append(s_.dma_start(
                                out=KG[fc * 128:(fc + 1) * 128,
                                       g_ * 128:(g_ + 1) * 128],
                                in_=self.ksb[k_lane][:, blk * 128:(blk + 1) * 128]))
                        return out
                    val = self.bwrite(kwr, ("kw0", "kw1"), fc % 2, 16,
                                      waits=[("dv", rope_dv)])
                    self.ksb_rot.release(k_lane, *val)

        self.emit_gamma(10)

        for fs in range(4):
            wv_lane = fs % 2
            wv_rel = cell.get(("wv_rel", wv_lane))

            def ldv(g, c, fs=fs, wv_lane=wv_lane):
                return g.dma_start(
                    out=self.wv_sp[wv_lane].rearrange("p (c f) -> p c f", f=512),
                    in_=self.wvT[:, fs * 512:(fs + 1) * 512]
                    .rearrange("(c p) f -> p c f", p=128))
            vl = self.gp_st2.load(ldv, free=[wv_rel])
            for tt in range(16):
                a_lane, a_rel = self.acc_rot.acquire()
                acc = self.ps_acc[a_lane]
                for e_ in range(NE):
                    def vmm(t, c, acc=acc, wv_lane=wv_lane, e_=e_, tt=tt):
                        return t.matmul(
                            acc[:, :], xb[e_][:, tt * 128:(tt + 1) * 128],
                            self.wv_sp[wv_lane][:, e_ * 512:(e_ + 1) * 512],
                            start=(e_ == 0), stop=(e_ == NE - 1))
                    wt_ = [vl, ("dv", xn_dv), a_rel] if e_ == 0 else []
                    w("pe", vmm, waits=wt_, inc="pe")
                vmm_pe = p.cnt("pe")

                vt_lane, vt_rel = self.vt_rot.acquire()

                def vev(s, c, acc=acc, vt_lane=vt_lane):
                    return s.copy(out=self.vtile[vt_lane], in_=acc[:, :])
                w("act", vev, waits=[("pe", vmm_pe), vt_rel], inc="ac")
                vev_ac = p.cnt("ac")
                self.acc_rot.release(a_lane, "ac", vev_ac)

                VG = self.VG

                def vwr(s_, j, tt=tt, half=half, fs=fs, vt_lane=vt_lane):
                    g_ = local_blocks(j)[tt + 16 * half]
                    return s_.dma_start(
                        out=VG[g_ * 128:(g_ + 1) * 128, fs * 512:(fs + 1) * 512],
                        in_=self.vtile[vt_lane])
                val = self.bwrite(vwr, ("vw0", "vw1"), tt % 2, 1,
                                  waits=[("ac", vev_ac)])
                self.vt_rot.release(vt_lane, *val)
            cell[("wv_rel", wv_lane)] = ("pe", p.cnt("pe"))

        if half == 0:
            cell["h0_xn_users_pe"] = p.cnt("pe")
            cell["h0_rope_dv"] = p.cnt("dv")
        self.emit_gamma(16)

    def gamma_final(self):
        p, w = self.p, self.w
        first_waits = [("dv", self.gred_dv[-1]), self.stat_free[7],
                       self.psS1row_free]
        self.stat_free[7] = None
        for wi in range(3):
            cols = self.gw_cols[wi]
            c0, c1 = min(cols), max(cols) + 1
            assert cols == list(range(c0, c1))

            def gmm(t, c, c0=c0, c1=c1):
                return t.matmul(self.psD[0:1, 0:c1 - c0], self.onecol_f[:, :],
                                self.gstat[:, c0:c1], start=True, stop=True)
            w("pe", gmm, waits=first_waits if wi == 0 else [("dv", p.cnt("dv"))],
              inc="pe")

            def gsum(v, c, wi=wi, c0=c0, c1=c1):
                return v.tensor_reduce(self.mrow[0:1, 84 + wi:85 + wi],
                                       self.psD[0:1, 0:c1 - c0], axis=AX, op=ALU.add)
            w("dve", gsum, waits=[("pe", p.cnt("pe"))], inc="dv")
        self.gam_dv = p.cnt("dv")
        self.psDrow_free = ("dv", self.gam_dv)

    def barrier(self):
        p, w, sems = self.p, self.w, self.sems

        def gp_stage(v, c):
            v.tensor_copy(self.gprow[0:1, 0:3], self.mrow[0:1, 84:87])
            return v.tensor_copy(self.gprow[0:1, 16:32], self.mrow[0:1, 64:80])
        w("dve", gp_stage, waits=[("dv", self.gam_dv)], inc="dv")
        stage_dv = p.cnt("dv")

        GP = self.GP

        def gpwr(s_, j):
            base = 0 if j in (0, 1) else 64
            return s_.dma_start(out=GP[0:1, base:base + 32], in_=self.gprow[0:1, 0:32])
        self.branch4(gpwr, "sw", waits=[("dv", stage_dv)])

        flag_waits = [(s, p.cnt(s))
                      for s in ("kw0", "kw1", "vw0", "vw1", "qw0", "qw1", "sw")
                      if p.cnt(s) > 0]

        def pre_flag(s_, c):
            for s, v in flag_waits:
                s_.wait_ge(sems[s], v)
            return None
        p.prog["sp"].append(((), pre_flag, None, 0))

        FLAG = self.FLAG

        def flagwr(s_, j):
            slot = 0 if j in (0, 1) else 1
            return s_.dma_start(out=FLAG[0:1, slot:slot + 1], in_=self.cidt[0:1, 1:2])
        self.branch4(flagwr, "sw")

        sw_done = p.cnt("sw")

        def spin(s_, c):
            s_.wait_ge(sems["sw"], sw_done)
            with s_.register("r_nd") as rnd, s_.register("r_f0") as rf0, \
                    s_.register("r_f1") as rf1, s_.register("r_non") as rnon:
                s_.reg_load(rnon, self.cidt[0:1, 1:2])
                s_.reg_mov(rnd, 1)
                with s_.While(rnd):
                    s_.sem_clear(sems["poll"])
                    s_.dma_start(out=self.flagt[:, :],
                                 in_=FLAG[:, :]).then_inc(sems["poll"], 16)
                    s_.wait_ge(sems["poll"], 16)
                    s_.reg_load(rf0, self.flagt[0:1, 0:1])
                    s_.reg_load(rf1, self.flagt[0:1, 1:2])
                    s_.reg_alu(rf0, rf0, rnon, ALU.is_equal)
                    s_.reg_alu(rf1, rf1, rnon, ALU.is_equal)
                    s_.reg_alu(rnd, rf0, rf1, ALU.add)
                    s_.reg_alu(rnd, rnd, 2, ALU.is_lt)
            s_.sem_clear(sems["poll"])
            return None
        p.prog["sp"].append(((), spin, None, 0))
        w("sp", lambda s_, c: s_.nop(), inc="bar")
        self.BAR = p.cnt("bar")

    def attn_phase(self):
        p, w, cell, sems = self.p, self.w, self.cell, self.sems
        BAR = self.BAR
        kh, vh, qhb, pbuf, at_t = self.kh, self.vh, self.qhb, self.pbuf, self.at_t
        psO = [self.psA, self.psB]
        psN = [self.psC, self.psD]
        masksb, rbig, rrow = self.masksb, self.rbig, self.rrow
        mrow, gprow, misc_f = self.mrow, self.gprow, self.misc_f

        ml = self.sc_m.load(
            lambda s, c: s.dma_start(out=masksb[:, :] if hasattr(masksb, 'ap') else masksb, in_=self.masks_in[:, :]),
            free=[("bar", BAR)])
        gl0 = self.sc_m.load(
            lambda s, c: s.dma_start(out=gprow[0:1, 0:32], in_=self.GP[0:1, 0:32]),
            free=[("bar", BAR)])
        gl = self.sc_m.load(
            lambda s, c: s.dma_start(out=gprow[0:1, 64:96], in_=self.GP[0:1, 64:96]),
            free=[("bar", BAR)])

        rows = self.rows
        kn2r = rows[0:1, 512:528]
        lnq = rows[0:1, 528:592]
        brow = rows[0:1, 592:656]

        def knmax(v, c):
            return v.tensor_max(kn2r, gprow[0:1, 16:32], gprow[0:1, 80:96])
        w("dve", knmax, waits=[gl0, gl, self.rows_free], inc="dv")
        kn_dv = p.cnt("dv")

        def lns(s, c):
            ch = self.chain
            ch(s, c, s.activation(out=kn2r, in_=kn2r, func=AF.Ln), key="cac")
            return s.activation(out=lnq, in_=mrow[0:1, 0:64], func=AF.Ln)
        w("act", lns, waits=[("dv", kn_dv)], inc="ac")
        ln_ac = p.cnt("ac")

        def bsum(v, c):
            ch = self.chain
            i = None
            for cc4 in range(4):
                i = v.tensor_add(brow[0:1, cc4::4], lnq[0:1, cc4::4],
                                 kn2r[0:1, 0:16])
            return i
        w("dve", bsum, waits=[("ac", ln_ac)], inc="dv")
        bs_dv = p.cnt("dv")

        def bexp(s, c):
            return s.activation(out=brow, in_=brow, func=AF.Exp, scale=0.5)
        w("act", bexp, waits=[("dv", bs_dv)], inc="ac")
        be_ac = p.cnt("ac")

        def bneg(v, c):
            return v.tensor_scalar(brow, brow, -SCALE, None, op0=ALU.mult)
        w("dve", bneg, waits=[("ac", be_ac)], inc="dv")
        br_dv = p.cnt("dv")

        def bmm(t, c):
            return t.matmul(self.psA[:, 0:64], self.onerow_f[:, :],
                            brow, start=True, stop=True)
        w("pe", bmm, waits=[("dv", br_dv), self.acc_rot.rel[0]], inc="pe")

        def bcp(s, c):
            return s.copy(out=misc_f[:, 0:64], in_=self.psA[:, 0:64])
        w("act", bcp, waits=[("pe", p.cnt("pe"))], inc="ac")
        bias_ac = p.cnt("ac")

        s_free = [self.psS1row_free, self.psS1row_free]
        pb_free = [None, None]
        o_free = [("ac", bias_ac), self.acc_rot.rel[1]]
        n_free = [self.stat_free[6], ("dv", self.gam_dv)]
        at_free = [None, None]
        cons = {}
        pending_epi = []
        first_epi = [True]

        for h in range(H):
            lane = h % 2

            def lq(s, c, h=h, lane=lane):
                return s.dma_start(out=qhb[lane],
                                   in_=self.q_dram[h * 128:(h + 1) * 128, :])

            def lk(s, c, h=h, lane=lane):
                return s.dma_start(out=kh[lane],
                                   in_=self.KG[h * 128:(h + 1) * 128, :])

            def lv(s, c, h=h, lane=lane):
                return s.dma_start(
                    out=vh[lane].rearrange("p (a d) -> p a d", d=128),
                    in_=self.VG[:, h * 128:(h + 1) * 128]
                    .rearrange("(a p) d -> p a d", p=128))
            fw = [("bar", BAR)] if h < 2 else [("pe", cons[h - 2])]
            qld = self.sc_q.load(lq, free=fw)
            kld = self.sc_k.load(lk, free=fw)
            vld = self.sc_v.load(lv, free=fw)

            for cch in range(4):
                nkt = 16 * (cch + 1)
                ngrp = nkt // 2
                for g in range(ngrp):
                    if g == 1:
                        while pending_epi:
                            pending_epi.pop(0)()
                    kts = (2 * g, 2 * g + 1)
                    sb_ = self.psS[g % 2]
                    for i_, kt in enumerate(kts):
                        def smm(t, c, sb_=sb_, i_=i_, kt=kt, lane=lane, cch=cch):
                            return t.matmul(
                                sb_[:, i_ * 512:(i_ + 1) * 512],
                                kh[lane][:, kt * 128:(kt + 1) * 128],
                                qhb[lane][:, cch * 512:(cch + 1) * 512],
                                start=True, stop=True)
                        wt_ = [kld, qld, s_free[g % 2]] if i_ == 0 else []
                        w("pe", smm, waits=wt_, inc="pe")
                    smm_pe = p.cnt("pe")

                    def expf(s, c, sb_=sb_, g=g, h=h, cch=cch):
                        return s.activation(out=pbuf[g % 2], in_=sb_[:, :],
                                            func=AF.Exp,
                                            bias=misc_f[:, 4 * h + cch:4 * h + cch + 1],
                                            scale=SCALE)
                    wt_ = [("pe", smm_pe), pb_free[g % 2]]
                    if h == 0 and cch == 0 and g == 0:
                        wt_.append(("ac", bias_ac))
                    w("act", expf, waits=wt_, inc="ac")
                    exp_ac = p.cnt("ac")
                    s_free[g % 2] = ("ac", exp_ac)

                    mk_dv = None
                    for i_, kt in enumerate(kts):
                        beta = kt - 16 * cch
                        if 0 <= beta < 16:
                            def mk(v, c, g=g, i_=i_, beta=beta):
                                sl = pbuf[g % 2][:, i_ * 512:(i_ + 1) * 512]
                                return v.tensor_mul(
                                    sl, sl, masksb[:, beta * 512:(beta + 1) * 512])
                            w("dve", mk, waits=[("ac", exp_ac), ml], inc="dv")
                            mk_dv = p.cnt("dv")
                    p_ready = ("dv", mk_dv) if mk_dv is not None else ("ac", exp_ac)

                    for i_, kt in enumerate(kts):
                        last = (g == ngrp - 1 and i_ == 1)

                        def pv(t, c, kt=kt, i_=i_, g=g, lane=lane, cch=cch,
                               last=last):
                            return t.matmul(
                                psO[cch % 2][:, :],
                                vh[lane][:, kt * 128:(kt + 1) * 128],
                                pbuf[g % 2][:, i_ * 512:(i_ + 1) * 512],
                                start=(g == 0 and i_ == 0), stop=last)
                        wt_ = [p_ready] if i_ == 0 else []
                        if g == 0 and i_ == 0:
                            wt_ += [vld, o_free[cch % 2]]
                        w("pe", pv, waits=wt_, inc="pe")
                    for i_, kt in enumerate(kts):
                        last = (g == ngrp - 1 and i_ == 1)

                        def omm(t, c, i_=i_, g=g, cch=cch, last=last):
                            return t.matmul(
                                psN[cch % 2][0:1, 0:512], self.onecol_b[:, :],
                                pbuf[g % 2][:, i_ * 512:(i_ + 1) * 512],
                                start=(g == 0 and i_ == 0), stop=last)
                        wt_ = [n_free[cch % 2]] if (g == 0 and i_ == 0) else []
                        w("pe", omm, waits=wt_, inc="pe")
                    pb_free[g % 2] = ("pe", p.cnt("pe"))
                last_pe = p.cnt("pe")
                cons[h] = last_pe

                def make_epi(h=h, cch=cch, last_pe=last_pe):
                    def epi():
                        def recip(v, c):
                            ch = self.chain
                            ch(v, c, v.tensor_copy(rrow, psN[cch % 2][0:1, 0:512]))
                            return v.reciprocal(rrow, rrow)
                        wt_ = [("pe", last_pe)]
                        if first_epi[0]:
                            wt_.append(self.rows_free)
                        w("dve", recip, waits=wt_, inc="dv")
                        rec_dv = p.cnt("dv")

                        def rbc(t, c):
                            return t.matmul(psN[cch % 2][:, :], self.onerow_f[:, :],
                                            rrow, start=True, stop=True)
                        w("pe", rbc, waits=[("dv", rec_dv)], inc="pe")

                        def rcp(s, c):
                            return s.copy(out=rbig, in_=psN[cch % 2][:, :])
                        wt_ = [("pe", p.cnt("pe"))]
                        if first_epi[0]:
                            wt_.append(("dv", self.gred_dv[-1]))
                            first_epi[0] = False
                        w("act", rcp, waits=wt_, inc="ac")
                        rb_ac = p.cnt("ac")
                        n_free[cch % 2] = ("ac", rb_ac)

                        al = cch % 2
                        arel = at_free[al]

                        def amul(v, c, al=al, cch=cch):
                            return v.tensor_mul(at_t[al], psO[cch % 2][:, :], rbig)
                        w("dve", amul, waits=[("ac", rb_ac), arel], inc="dv")
                        am_dv = p.cnt("dv")
                        o_free[cch % 2] = ("dv", am_dv)

                        def awr(s_, c, h=h, cch=cch, al=al):
                            return s_.dma_start(
                                out=self.attn_dram[h * 128:(h + 1) * 128,
                                                   cch * 512:(cch + 1) * 512],
                                in_=at_t[al])
                        at_free[al] = self.swrite(awr, ("aw0", "aw1"), al,
                                                  waits=[("dv", am_dv)])
                    return epi
                pending_epi.append(make_epi())
        while pending_epi:
            pending_epi.pop(0)()
        self.attn_done = [("aw0", p.cnt("aw0")), ("aw1", p.cnt("aw1"))]
        self.attn_sfree = list(s_free)
        self.attn_nfree = list(n_free)
        self.attn_ofree = list(o_free)
        self.attn_cons_last = ("pe", p.cnt("pe"))

    def emit_ln(self, src_tiles, tok, in_ready, sq_scr, ab_A, ab_B,
                rX, rQ, row_ps, row_ps_free, scr_free=None, nfeat=E):
        p, w = self.p, self.w
        nsp = tok // 512
        assert nsp <= 2
        sq_rel = [None, None]
        for cc in range(NE):
            lane = cc % 2
            rdy = in_ready[cc]

            def sq(s, c, cc=cc, lane=lane):
                return s.activation(out=sq_scr[lane][:, 0:tok],
                                    in_=src_tiles[cc][:, 0:tok], func=AF.Square)
            wt0 = [rdy, sq_rel[lane]]
            if cc < 2:
                wt0.append(scr_free)
            w("act", sq, waits=wt0, inc="ac")
            sq_ac = p.cnt("ac")
            for sp in range(nsp):
                def mmx(t, c, cc=cc, sp=sp):
                    return t.matmul(row_ps[sp], self.onecol_b[:, :],
                                    src_tiles[cc][:, sp * 512:(sp + 1) * 512],
                                    start=(cc == 0), stop=(cc == NE - 1))
                wt_ = []
                if sp == 0:
                    wt_.append(rdy)
                if cc == 0:
                    wt_.append(row_ps_free[sp])
                w("pe", mmx, waits=wt_, inc="pe")
            for sp in range(nsp):
                def mmq(t, c, cc=cc, sp=sp, lane=lane):
                    return t.matmul(row_ps[2 + sp], self.onecol_b[:, :],
                                    sq_scr[lane][:, sp * 512:(sp + 1) * 512],
                                    start=(cc == 0), stop=(cc == NE - 1))
                wt_ = []
                if sp == 0:
                    wt_.append(("ac", sq_ac))
                if cc == 0:
                    wt_.append(row_ps_free[2 + sp])
                w("pe", mmq, waits=wt_, inc="pe")
            sq_rel[lane] = ("pe", p.cnt("pe"))
        stats_pe = p.cnt("pe")

        def rows_calc(v, c):
            ch = self.chain
            for sp in range(nsp):
                v.tensor_copy(rX[0:1, sp * 512:(sp + 1) * 512], row_ps[sp])
            for sp in range(nsp):
                i = v.tensor_copy(rQ[0:1, sp * 512:(sp + 1) * 512], row_ps[2 + sp])
            ch(v, c, i)
            ch(v, c, v.tensor_scalar(rX, rX, 1.0 / nfeat, None, op0=ALU.mult))
            ch(v, c, v.tensor_scalar(rQ, rQ, 1.0 / nfeat, None, op0=ALU.mult))
            ch(v, c, v.tensor_mul(self.rowM[0:1, 0:tok], rX, rX))
            ch(v, c, v.tensor_sub(rQ, rQ, self.rowM[0:1, 0:tok]))
            return v.tensor_scalar(rQ, rQ, LN_EPS, None, op0=ALU.add)
        w("dve", rows_calc, waits=[("pe", stats_pe), self.rowm_free], inc="dv")
        rc_dv = p.cnt("dv")

        def rows_ln(s, c):
            ch = self.chain
            ch(s, c, s.activation(out=rQ, in_=rQ, func=AF.Ln), key="cac")
            return s.activation(out=rQ, in_=rQ, func=AF.Exp, scale=-0.5)
        w("act", rows_ln, waits=[("dv", rc_dv)], inc="ac")
        rstd_ac = p.cnt("ac")

        def rows_fin(v, c):
            ch = self.chain
            ch(v, c, v.tensor_mul(rX, rX, rQ))
            return v.tensor_scalar(rX, rX, -1.0, None, op0=ALU.mult)
        w("dve", rows_fin, waits=[("ac", rstd_ac)], inc="dv")
        rows_dv = p.cnt("dv")

        ab_ac = None
        bc_ps = row_ps[0].tensor if False else None
        for src_row, dst in ((rQ, ab_A), (rX, ab_B)):
            for sp in range(nsp):
                def bmm(t, c, src_row=src_row, sp=sp):
                    return t.matmul(self.bcast_bank[:, 0:512], self.onerow_f[:, :],
                                    src_row[0:1, sp * 512:(sp + 1) * 512],
                                    start=True, stop=True)
                wt_ = [("dv", rows_dv)]
                if ab_ac is not None:
                    wt_.append(("ac", ab_ac))
                else:
                    wt_.append(self.bcast_bank_free)
                w("pe", bmm, waits=wt_, inc="pe")

                def bcp(s, c, dst=dst, sp=sp):
                    return s.copy(out=dst[:, sp * 512:(sp + 1) * 512],
                                  in_=self.bcast_bank[:, 0:512])
                wt2 = [("pe", p.cnt("pe"))]
                if ab_ac is None:
                    wt2.append(scr_free)
                w("act", bcp, waits=wt2, inc="ac")
                ab_ac = p.cnt("ac")
        self.bcast_bank_free = ("ac", ab_ac)
        self.rowm_free = ("dv", rc_dv)
        return ab_ac, ("pe", p.cnt("pe"))

    def wo_phase(self):
        p, w, sems = self.p, self.w, self.sems
        Bb, C = self.Bb, self.C
        worhs4 = [Bb[:, i * 8192:(i + 1) * 8192] for i in range(4)]
        wt3 = [Bb[:, 32768 + i * 2048:32768 + (i + 1) * 2048] for i in range(2)]
        evo = [Bb[:, 36864 + i * 512:36864 + (i + 1) * 512] for i in range(2)]
        asq3 = [Bb[:, 37888 + i * 512:37888 + (i + 1) * 512] for i in range(2)]
        ev2 = [Bb[:, i * 2048:(i + 1) * 2048] for i in range(2)]
        ab3_A = Bb[:, 40960 + 2048 * 0:43008]
        ab3_B = Bb[:, 43008:45056]
        xre = C[:, 0:2048]
        xmo = C[:, 2048:4096]
        rowX = self.rows[0:1, 0:2048]
        rowQ = self.rows[0:1, 2048:4096]

        ad = self.attn_done
        rhs_ld = []
        for t4 in range(4):
            def lr(s, c, t4=t4):
                return s.dma_start(
                    out=worhs4[t4].rearrange("p (c t) -> p c t", t=512),
                    in_=self.attn_dram[:, t4 * 512:(t4 + 1) * 512]
                    .rearrange("(c p) t -> p c t", p=128))
            fw = (list(ad) if t4 == 0 else []) + [self.attn_cons_last]
            rhs_ld.append(self.sc_m.load(lr, free=fw))

        def zrows(v, c):
            ch = self.chain
            ch(v, c, v.memset(rowX, 0.0))
            return v.memset(rowQ, 0.0)
        w("dve", zrows, waits=[self.attn_cons_last], inc="dv")
        rz_dv = p.cnt("dv")

        wo_acc = Rot(2)
        wo_acc.rel = [self.attn_ofree[0], self.attn_ofree[1]]
        evo_rot = Rot(2)
        asq_rot = Rot(2)
        ev_wr = {}
        prow_free = [self.attn_nfree[0], self.attn_nfree[1]]
        acc_dv = rz_dv
        for fc in range(NE):
            wl_lane = fc % 2
            wl_rel = self.cell.get(("wt3_rel", wl_lane))

            def ldw(g, c, fc=fc, wl_lane=wl_lane):
                return g.dma_start(
                    out=wt3[wl_lane].rearrange("p (c f) -> p c f", f=128),
                    in_=self.woT_t[fc].rearrange("(c p) f -> p c f", p=128))
            wl = self.gp_st.load(ldw, free=[wl_rel])

            for t4 in range(4):
                a_lane, a_rel = wo_acc.acquire()
                acc = self.ps_acc[a_lane]
                for hd in range(NE):
                    def mm(t, c, acc=acc, wl_lane=wl_lane, hd=hd, t4=t4):
                        return t.matmul(
                            acc[:, :], wt3[wl_lane][:, hd * 128:(hd + 1) * 128],
                            worhs4[t4][:, hd * 512:(hd + 1) * 512],
                            start=(hd == 0), stop=(hd == NE - 1))
                    wt_ = []
                    if hd == 0:
                        wt_ = [wl, a_rel]
                        if fc == 0:
                            wt_.append(rhs_ld[t4])
                    w("pe", mm, waits=wt_, inc="pe")
                mm_pe = p.cnt("pe")

                e_lane, e_rel = evo_rot.acquire()
                e_rel_dv = self.cell.get(("evo_dv", e_lane))

                def ev(s, c, acc=acc, e_lane=e_lane):
                    return s.copy(out=evo[e_lane], in_=acc[:, :])
                w("act", ev, waits=[("pe", mm_pe), e_rel, e_rel_dv], inc="ac")
                ev_ac = p.cnt("ac")
                wo_acc.release(a_lane, "ac", ev_ac)

                q_lane, q_rel = asq_rot.acquire()

                def sq(s, c, e_lane=e_lane, q_lane=q_lane):
                    return s.activation(out=asq3[q_lane], in_=evo[e_lane],
                                        func=AF.Square)
                w("act", sq, waits=[q_rel], inc="ac")
                sq_ac = p.cnt("ac")

                def mmx(t, c, e_lane=e_lane):
                    return t.matmul(self.psC[0:1, 0:512], self.onecol_b[:, :],
                                    evo[e_lane], start=True, stop=True)
                w("pe", mmx, waits=[("ac", ev_ac), prow_free[0]], inc="pe")
                prow_free[0] = None

                def mmq(t, c, q_lane=q_lane):
                    return t.matmul(self.psD[0:1, 0:512], self.onecol_b[:, :],
                                    asq3[q_lane], start=True, stop=True)
                w("pe", mmq, waits=[("ac", sq_ac), prow_free[1]], inc="pe")
                prow_free[1] = None
                rows_pe = p.cnt("pe")

                def racc(v, c, t4=t4):
                    ch = self.chain
                    sl = slice(t4 * 512, (t4 + 1) * 512)
                    ch(v, c, v.tensor_add(rowX[0:1, sl], rowX[0:1, sl],
                                          self.psC[0:1, 0:512]))
                    return v.tensor_add(rowQ[0:1, sl], rowQ[0:1, sl],
                                        self.psD[0:1, 0:512])
                w("dve", racc, waits=[("pe", rows_pe), ("dv", acc_dv)], inc="dv")
                acc_dv = p.cnt("dv")
                asq_rot.release(q_lane, "dv", acc_dv)
                prow_free = [("dv", acc_dv), ("dv", acc_dv)]
                self.cell[("evo_dv", e_lane)] = ("dv", acc_dv)

                def awr(s_, c, fc=fc, t4=t4, e_lane=e_lane):
                    return s_.dma_start(
                        out=self.attn2_dram[fc * 128:(fc + 1) * 128,
                                            t4 * 512:(t4 + 1) * 512],
                        in_=evo[e_lane])
                val = self.swrite(awr, ("aw0", "aw1"), e_lane,
                                  waits=[("ac", ev_ac)])
                evo_rot.release(e_lane, *val)
            self.cell[("wt3_rel", wl_lane)] = ("pe", p.cnt("pe"))
        a2_done = [("aw0", p.cnt("aw0")), ("aw1", p.cnt("aw1"))]
        stats_dv = acc_dv

        def rows_calc(v, c):
            ch = self.chain
            ch(v, c, v.tensor_scalar(rowX, rowX, 1.0 / E, None, op0=ALU.mult))
            ch(v, c, v.tensor_scalar(rowQ, rowQ, 1.0 / E, None, op0=ALU.mult))
            ch(v, c, v.tensor_mul(self.rowM, rowX, rowX))
            ch(v, c, v.tensor_sub(rowQ, rowQ, self.rowM))
            return v.tensor_scalar(rowQ, rowQ, LN_EPS, None, op0=ALU.add)
        w("dve", rows_calc, waits=[("dv", stats_dv)], inc="dv")

        def rows_ln(s, c):
            ch = self.chain
            ch(s, c, s.activation(out=rowQ, in_=rowQ, func=AF.Ln), key="cac")
            return s.activation(out=rowQ, in_=rowQ, func=AF.Exp, scale=-0.5)
        w("act", rows_ln, waits=[("dv", p.cnt("dv"))], inc="ac")

        def rows_fin(v, c):
            ch = self.chain
            ch(v, c, v.tensor_mul(rowX, rowX, rowQ))
            return v.tensor_scalar(rowX, rowX, -1.0, None, op0=ALU.mult)
        w("dve", rows_fin, waits=[("ac", p.cnt("ac"))], inc="dv")
        rows_dv = p.cnt("dv")

        ab_ac = None
        for src_row, dst in ((rowQ, ab3_A), (rowX, ab3_B)):
            for sp in range(4):
                def bmm(t, c, src_row=src_row, sp=sp):
                    return t.matmul(self.psS0[:, 0:512], self.onerow_f[:, :],
                                    src_row[0:1, sp * 512:(sp + 1) * 512],
                                    start=True, stop=True)
                wt_ = [("dv", rows_dv)]
                wt_.append(("ac", ab_ac) if ab_ac is not None else self.attn_sfree[0])
                w("pe", bmm, waits=wt_, inc="pe")

                def bcp(s, c, dst=dst, sp=sp):
                    return s.copy(out=dst[:, sp * 512:(sp + 1) * 512],
                                  in_=self.psS0[:, 0:512])
                w("act", bcp, waits=[("pe", p.cnt("pe"))], inc="ac")
                ab_ac = p.cnt("ac")
        ab3_ac = ab_ac
        self.bcast_s0_free = ("ac", ab_ac)

        xre_free = None
        ev2_rot = Rot(2)
        xmo_free = None
        for fc in range(NE):
            e_lane, e_rel = ev2_rot.acquire()

            def lda(s, c, fc=fc, e_lane=e_lane):
                return s.dma_start(out=ev2[e_lane],
                                   in_=self.attn2_dram[fc * 128:(fc + 1) * 128, :])
            fw = [e_rel]
            if fc < 2:
                fw += a2_done
            al = self.sc_v.load(lda, free=fw)

            def ldx(s, c, fc=fc):
                return s.dma_start(out=xre[:, :],
                                   in_=self.x_in[fc * 128:(fc + 1) * 128, 0:TQ])
            xl = self.sc_k.load(ldx, free=[xre_free])

            def xmcalc(v, c, fc=fc, e_lane=e_lane):
                ch = self.chain
                ch(v, c, v.tensor_mul(xmo, ev2[e_lane], ab3_A))
                ch(v, c, v.tensor_add(xmo, xmo, ab3_B))
                return v.tensor_add(xmo, xmo, xre)
            w("dve", xmcalc, waits=[al, xl, ("ac", ab3_ac), xmo_free], inc="dv")
            xm_dv = p.cnt("dv")
            ev2_rot.release(e_lane, "dv", xm_dv)
            xre_free = ("dv", xm_dv)

            def xwr(s_, c, fc=fc):
                return s_.dma_start(
                    out=self.xmid_dram[fc * 128:(fc + 1) * 128, :], in_=xmo)
            xmo_free = self.swrite(xwr, ("xw0", "xw1"), fc % 2,
                                   waits=[("dv", xm_dv)])
        self.xmid_done = [("xw0", p.cnt("xw0")), ("xw1", p.cnt("xw1"))]
        self.wo_rel = list(wo_acc.rel)
        self.wo_psC_free = prow_free[0]
        self.wo_psD_free = prow_free[1]
        self.wo_worhs_users = ("pe", p.cnt("pe"))
        self.rows_free = ("pe", p.cnt("pe"))

    def debug_dump3(self):
        p, sems = self.p, self.sems
        dbg = self.dbg
        xd = self.xmid_done

        def d3(s_, c):
            for s, v in xd:
                s_.wait_ge(sems[s], v)
            s_.dma_start(out=dbg["xm"][:, :],
                         in_=self.xmid_dram[:, :]).then_inc(sems["ow"], 16)
            s_.wait_ge(sems["ow"], 16)
            return None
        p.prog["sp"].append(((), d3, None, 0))

    def debug_dump2(self):
        p, sems = self.p, self.sems
        dbg = self.dbg
        ad = self.attn_done

        def d2(s_, c):
            for s, v in ad:
                s_.wait_ge(sems[s], v)
            s_.dma_start(out=dbg["at"][:, :],
                         in_=self.attn_dram[:, :]).then_inc(sems["ow"], 16)
            s_.wait_ge(sems["ow"], 16)
            return None
        p.prog["sp"].append(((), d2, None, 0))

    def ffn_phase(self):
        p, w, sems = self.p, self.w, self.sems
        Bb, C, Eb, Fb = self.Bb, self.C, self.Eb, self.Fb
        xn2, y3b, hidv = self.xn2, self.y3b, self.hidv
        qw1v, qw2v, qw3v = self.qw1v, self.qw2v, self.qw3v
        raw1, raw2, sig4 = self.raw1, self.raw2, self.sig4
        misc_f = self.misc_f
        row4X, row4Q = self.row4X, self.row4Q
        self.rowM = C[0:1, 2048:4096]
        self.rowm_free = None
        self.bcast_bank = self.psS0[:, 0:512]
        self.bcast_bank_free = self.bcast_s0_free

        def thrc(v, c):
            ch = self.chain
            m = self.mrow
            ch(v, c, v.tensor_add(m[0:1, 88:91], self.gprow[0:1, 0:3],
                                  self.gprow[0:1, 64:67]))
            ch(v, c, v.tensor_scalar(m[0:1, 88:91], m[0:1, 88:91], 1.0 / QCNT,
                                     None, op0=ALU.mult))
            ch(v, c, v.tensor_scalar(m[0:1, 88:91], m[0:1, 88:91], Q_EPS, 0.5,
                                     op0=ALU.add, op1=ALU.mult))
            return v.tensor_scalar(m[0:1, 92:95], m[0:1, 88:91], -1.0, None,
                                   op0=ALU.mult)
        w("dve", thrc, waits=[("dv", self.gam_dv)], inc="dv")
        thr_dv = p.cnt("dv")

        def thrmm(t, c):
            return t.matmul(self.psS1[:, 0:8], self.onerow_f[:, :],
                            self.mrow[0:1, 88:96], start=True, stop=True)
        w("pe", thrmm, waits=[("dv", thr_dv), self.attn_sfree[1]], inc="pe")
        thr_pe = p.cnt("pe")

        def thrcp(s, c):
            return s.copy(out=misc_f[:, 16:24], in_=self.psS1[:, 0:8])
        w("act", thrcp, waits=[("pe", thr_pe)], inc="ac")
        thr_ac = p.cnt("ac")
        thr_ap = [misc_f[:, 16 + i:17 + i] for i in range(3)]
        nthr_ap = [misc_f[:, 20 + i:21 + i] for i in range(3)]
        psS1_free = ("ac", thr_ac)

        y_acc = Rot(2)
        y_acc.rel = list(self.wo_rel)
        y2_rel = [self.wo_psC_free, self.wo_psD_free]
        qwst = {}
        hf_state = {}

        for hf in range(2):
            hcols = slice(hf * 1024, (hf + 1) * 1024)

            x2ld = []
            for cc in range(NE):
                def ldm(g, c, cc=cc, hcols=hcols):
                    return g.dma_start(
                        out=xn2[cc],
                        in_=self.xmid_dram[cc * 128:(cc + 1) * 128, hcols])
                if hf == 0:
                    fw = list(self.xmid_done) + [self.attn_cons_last] \
                        if cc == 0 else []
                else:
                    fw = [("pe", hf_state[0]["xn2_users"])] if cc == 0 else []
                x2ld.append(self.gp_st.load(ldm, free=fw))
            row_ps = [self.psC[0:1, 0:512], self.psD[0:1, 0:512],
                      self.psS1[0:1, 0:512], self.psS1[0:1, 512:1024]]
            if hf == 0:
                row_ps_free = [y2_rel[0], y2_rel[1], psS1_free, psS1_free]
                scr_free = self.wo_worhs_users
            else:
                hd0 = hf_state[0]
                row_ps_free = [hd0["y2_free"][0], hd0["y2_free"][1],
                               hd0["psS1_free"], hd0["psS1_free"]]
                scr_free = hd0["ocalc_done"]
            self.rowm_free = ("dv", p.cnt("dv"))
            ab_ac, _bc_pe = self.emit_ln(
                xn2, 1024, x2ld, [hidv[0], hidv[1]], hidv[2], hidv[3],
                row4X, row4Q, row_ps, row_ps_free, scr_free=scr_free)
            psS1_free = ("dv", self.rowm_free[1])
            ap_dv = []
            for cc in range(NE):
                lane = cc % 2

                def ap1(v, c, cc=cc, lane=lane):
                    return v.tensor_mul(hidv[4 + lane], xn2[cc], hidv[2])

                def ap2(v, c, cc=cc, lane=lane):
                    return v.tensor_add(xn2[cc], hidv[4 + lane], hidv[3])
                wt_ = [("ac", ab_ac)]
                if cc < 2:
                    wt_.append(scr_free)
                if cc >= 2:
                    wt_.append(("dv", ap_dv[cc - 2]))
                w("dve", ap1, waits=wt_, inc="dv")
                w("dve", ap2, waits=[("dv", p.cnt("dv"))], inc="dv")
                ap_dv.append(p.cnt("dv"))
            xn2_dv = p.cnt("dv")
            if self.debug == 4 and hf == 0:
                for cc in range(NE):
                    def dxw(s_, c, cc=cc):
                        return s_.dma_start(
                            out=self.dbg["xn2"][cc * 128:(cc + 1) * 128, :],
                            in_=xn2[cc]).then_inc(sems["ow"], 16) and None
                    p.bump("ow", 16)
                    p.prog["sp"].append(((("dv", xn2_dv),), dxw, None, 0))

            first_hm = True
            for hc in range(NH):
                if hf == 0:
                    def ld1(s, c, hc=hc):
                        return s.dma_start(
                            out=raw1[:, :].rearrange("p (c f) -> p c f", f=128),
                            in_=self.w1T_t[hc].rearrange("(c p) f -> p c f", p=128))

                    def ld2(s, c, hc=hc):
                        return s.dma_start(
                            out=raw2[:, :].rearrange("p (c f) -> p c f", f=128),
                            in_=self.w2T_t[hc].rearrange("(c p) f -> p c f", p=128))
                    fw1 = [qwst.get(("qz1",))]
                    fw2 = [qwst.get(("qz2",))]
                    if hc == 0:
                        fw1 += list(self.xmid_done) + [("dv", self.rowm_free[1])]
                        fw2 += list(self.xmid_done) + [("dv", self.rowm_free[1])]
                    l1 = self.sc_k.load(ld1, free=fw1)
                    l2 = self.sc_v.load(ld2, free=fw2)

                    def qz1(v, c):
                        ch = self.chain
                        ch(v, c, v.tensor_scalar(Fb[:, 0:2048], raw1, nthr_ap[0],
                                                 None, op0=ALU.is_le))
                        return v.scalar_tensor_tensor(
                            qw1v, raw1, thr_ap[0], Fb[:, 0:2048],
                            op0=ALU.is_ge, op1=ALU.subtract)

                    def qz2(v, c):
                        ch = self.chain
                        ch(v, c, v.tensor_scalar(Fb[:, 2048:4096], raw2, nthr_ap[1],
                                                 None, op0=ALU.is_le))
                        return v.scalar_tensor_tensor(
                            qw2v, raw2, thr_ap[1], Fb[:, 2048:4096],
                            op0=ALU.is_ge, op1=ALU.subtract)
                    wt1 = [l1, ("ac", thr_ac), qwst.get(("mm1",)),
                           qwst.get(("wr1",))]
                    wt2 = [l2, qwst.get(("mm2",)), qwst.get(("wr2",))]
                    w("dve", qz1, waits=wt1, inc="dv")
                    q1_dv = p.cnt("dv")
                    w("dve", qz2, waits=wt2, inc="dv")
                    q2_dv = p.cnt("dv")

                    def w1wr(s_, c, hc=hc):
                        return s_.dma_start(out=self.qw1_dram[hc], in_=qw1v)

                    def w2wr(s_, c, hc=hc):
                        return s_.dma_start(out=self.qw2_dram[hc], in_=qw2v)
                    qwst[("wr1",)] = self.swrite(w1wr, ("ww0", "ww1"), 0,
                                                 waits=[("dv", q1_dv)])
                    qwst[("wr2",)] = self.swrite(w2wr, ("ww0", "ww1"), 1,
                                                 waits=[("dv", q2_dv)])
                    qw_rdy1 = ("dv", q1_dv)
                    qw_rdy2 = ("dv", q2_dv)
                else:
                    def lq1(s, c, hc=hc):
                        return s.dma_start(out=qw1v, in_=self.qw1_dram[hc])

                    def lq2(s, c, hc=hc):
                        return s.dma_start(out=qw2v, in_=self.qw2_dram[hc])
                    fw1 = [qwst.get(("mm1",))]
                    fw2 = [qwst.get(("mm2",))]
                    if hc == 0:
                        fw1 += [("ww0", p.cnt("ww0")), ("ww1", p.cnt("ww1"))]
                        fw2 += [("ww0", p.cnt("ww0")), ("ww1", p.cnt("ww1"))]
                    qw_rdy1 = self.sc_k.load(lq1, free=fw1)
                    qw_rdy2 = self.sc_v.load(lq2, free=fw2)

                for ts_ in range(2):
                    a_lane, a_rel = y_acc.acquire()
                    acc1 = self.ps_acc[a_lane]
                    acc2 = [self.psC, self.psD][a_lane]
                    for e_ in range(NE):
                        def m1(t, c, acc1=acc1, e_=e_, ts_=ts_):
                            return t.matmul(
                                acc1[:, :], qw1v[:, e_ * 128:(e_ + 1) * 128],
                                xn2[e_][:, ts_ * 512:(ts_ + 1) * 512],
                                start=(e_ == 0), stop=(e_ == NE - 1))
                        wt_ = [qw_rdy1, ("dv", xn2_dv), a_rel] if e_ == 0 else []
                        w("pe", m1, waits=wt_, inc="pe")
                    for e_ in range(NE):
                        def m2(t, c, acc2=acc2, e_=e_, ts_=ts_):
                            return t.matmul(
                                acc2[:, :], qw2v[:, e_ * 128:(e_ + 1) * 128],
                                xn2[e_][:, ts_ * 512:(ts_ + 1) * 512],
                                start=(e_ == 0), stop=(e_ == NE - 1))
                        wt_ = [qw_rdy2, y2_rel[a_lane]] if e_ == 0 else []
                        y2_rel[a_lane] = None
                        w("pe", m2, waits=wt_, inc="pe")
                    mm_pe = p.cnt("pe")

                    def sg(s, c, acc1=acc1, ts_=ts_):
                        return s.activation(out=sig4[:, ts_ * 512:(ts_ + 1) * 512],
                                            in_=acc1[:, :], func=AF.Sigmoid)
                    wt_ = [("pe", mm_pe), qwst.get(("hidmul", a_lane))]
                    w("act", sg, waits=wt_, inc="ac")
                    sg_ac = p.cnt("ac")

                    def hm(v, c, acc2=acc2, hc=hc, ts_=ts_):
                        sl = slice(ts_ * 512, (ts_ + 1) * 512)
                        return v.tensor_mul(hidv[hc][:, sl], sig4[:, sl],
                                            acc2[:, :])
                    wt_ = [("ac", sg_ac)]
                    if first_hm:
                        wt_ += [self.wo_worhs_users, ("ac", ab_ac)]
                        if hf == 1:
                            wt_.append(hf_state[0]["y3_users"])
                        first_hm = False
                    w("dve", hm, waits=wt_, inc="dv")
                    rel = ("dv", p.cnt("dv"))
                    y_acc.release(a_lane, *rel)
                    qwst[("hidmul", a_lane)] = rel
                if ts_ == 1:
                    qwst[("mm1",)] = ("pe", mm_pe)
                    qwst[("mm2",)] = ("pe", mm_pe)
            hid_dv = p.cnt("dv")
            hf_state[hf] = {"xn2_users": p.cnt("pe"),
                            "y2_free": [qwst.get(("hidmul", 0)),
                                        qwst.get(("hidmul", 1))],
                            "psS1_free": psS1_free}

            pieces = ((0, 2048), (2048, 2048), (4096, 1536))
            first_ev3 = True
            for fc in range(NE):
                if hf == 0:
                    prev_mm3 = qwst.get(("mm3",))
                    for pi, (s0, ln) in enumerate(pieces):
                        def ld3(s, c, fc=fc, s0=s0, ln=ln):
                            return s.dma_start(
                                out=raw1[:, 0:ln].rearrange(
                                    "p (c f) -> p c f", f=128),
                                in_=self.w3T_t[fc].rearrange(
                                    "(c p) f -> p c f", p=128)
                                [:, s0 // 128:(s0 + ln) // 128, :])
                        fw = [qwst.get(("qz3",))]
                        if fc == 0 and pi == 0:
                            fw.append(qwst.get(("mm1",)))
                        l3 = self.sc_q.load(ld3, free=fw)

                        def qz3(v, c, s0=s0, ln=ln):
                            ch = self.chain
                            ch(v, c, v.tensor_scalar(Eb[:, 0:ln], raw1[:, 0:ln],
                                                     nthr_ap[2], None,
                                                     op0=ALU.is_le))
                            return v.scalar_tensor_tensor(
                                qw3v[:, s0:s0 + ln], raw1[:, 0:ln], thr_ap[2],
                                Eb[:, 0:ln], op0=ALU.is_ge, op1=ALU.subtract)
                        wt_ = [l3]
                        if pi == 0:
                            wt_ += [prev_mm3, qwst.get(("wr3",)),
                                    qwst.get(("lq12",))]
                        w("dve", qz3, waits=wt_, inc="dv")
                        qwst[("qz3",)] = ("dv", p.cnt("dv"))
                    q3_dv = p.cnt("dv")

                    def w3wr(s_, c, fc=fc):
                        return s_.dma_start(out=self.qw3_dram[fc], in_=qw3v)
                    qwst[("wr3",)] = self.swrite(w3wr, ("ww0", "ww1"), fc % 2,
                                                 waits=[("dv", q3_dv)])
                    qw3_rdy = ("dv", q3_dv)
                else:
                    def lq3(s, c, fc=fc):
                        return s.dma_start(out=qw3v, in_=self.qw3_dram[fc])
                    fw = [qwst.get(("mm3",))]
                    if fc == 0:
                        fw += [("ww0", p.cnt("ww0")), ("ww1", p.cnt("ww1"))]
                    qw3_rdy = self.sc_q.load(lq3, free=fw)

                for ts_ in range(2):
                    a_lane, a_rel = y_acc.acquire()
                    acc = self.ps_acc[a_lane]
                    for hc in range(NH):
                        def m3(t, c, acc=acc, hc=hc, ts_=ts_):
                            return t.matmul(
                                acc[:, :], qw3v[:, hc * 128:(hc + 1) * 128],
                                hidv[hc][:, ts_ * 512:(ts_ + 1) * 512],
                                start=(hc == 0), stop=(hc == NH - 1))
                        wt_ = [qw3_rdy, ("dv", hid_dv), a_rel] if hc == 0 else []
                        w("pe", m3, waits=wt_, inc="pe")
                    mm_pe = p.cnt("pe")
                    qwst[("mm3",)] = ("pe", mm_pe)

                    def ev3(s, c, acc=acc, fc=fc, ts_=ts_):
                        return s.copy(out=y3b[fc][:, ts_ * 512:(ts_ + 1) * 512],
                                      in_=acc[:, :])
                    wt_ = [("pe", mm_pe)]
                    if first_ev3:
                        if hf == 0:
                            wt_.append(self.attn_cons_last)
                        else:
                            wt_.append(hf_state[0]["ocalc_done"])
                        first_ev3 = False
                    w("act", ev3, waits=wt_, inc="ac")
                    y_acc.release(a_lane, "ac", p.cnt("ac"))
            y3_ac = p.cnt("ac")
            hf_state[hf]["y3_users"] = ("pe", p.cnt("pe"))
            if self.debug == 4 and hf == 0:
                for cc in range(NE):
                    def dyw(s_, c, cc=cc):
                        return s_.dma_start(
                            out=self.dbg["y3"][cc * 128:(cc + 1) * 128, :],
                            in_=y3b[cc]).then_inc(sems["ow"], 16) and None
                    p.bump("ow", 16)
                    p.prog["sp"].append(((("ac", y3_ac),), dyw, None, 0))

                def dsw(s_, c):
                    s_.dma_start(out=self.dbg["sig"][:, :],
                                 in_=sig4[:, :]).then_inc(sems["ow"], 16)
                    s_.dma_start(out=self.dbg["hid"][:, :],
                                 in_=hidv[NH - 1][:, :]).then_inc(sems["ow"], 16)
                    return None
                p.bump("ow", 32)
                p.prog["sp"].append(((("ac", y3_ac),), dsw, None, 0))

            row_ps = [self.psC[0:1, 0:512], self.psD[0:1, 0:512],
                      self.psS1[0:1, 0:512], self.psS1[0:1, 512:1024]]
            row_ps_free = [hf_state[hf]["y2_free"][0],
                           hf_state[hf]["y2_free"][1],
                           psS1_free, psS1_free]
            in_rdy = [("ac", y3_ac)] * NE
            self.rowm_free = ("dv", p.cnt("dv"))
            abf_ac, _ = self.emit_ln(
                y3b, 1024, in_rdy, [hidv[4], hidv[5]], hidv[0], hidv[1],
                row4X, row4Q, row_ps, row_ps_free,
                scr_free=("ac", y3_ac))
            psS1_free = ("dv", self.rowm_free[1])

            out_free = None
            ld_free = None
            last_q_dv = ("dv", p.cnt("dv"))
            for cc in range(NE):
                def ldm2(s, c, cc=cc, hcols=hcols):
                    return s.dma_start(
                        out=raw1[:, 0:1024],
                        in_=self.xmid_dram[cc * 128:(cc + 1) * 128, hcols])
                fw = [ld_free]
                if cc == 0 and hf == 0:
                    fw.append(qwst.get(("qz3",)))
                if cc == 0:
                    fw.append(("dv", self.rowm_free[1]))
                xl = self.sc_m.load(ldm2, free=fw)

                def ocalc(v, c, cc=cc):
                    ch = self.chain
                    ch(v, c, v.tensor_mul(raw2[:, 0:1024], y3b[cc], hidv[0]))
                    ch(v, c, v.tensor_add(raw2[:, 0:1024], raw2[:, 0:1024],
                                          hidv[1]))
                    return v.tensor_add(raw2[:, 0:1024], raw2[:, 0:1024],
                                        raw1[:, 0:1024])
                w("dve", ocalc, waits=[xl, ("ac", abf_ac), out_free], inc="dv")
                oc_dv = p.cnt("dv")

                def owr(s_, c, cc=cc, hcols=hcols):
                    return s_.dma_start(
                        out=self.out_d[cc * 128:(cc + 1) * 128, hcols],
                        in_=raw2[:, 0:1024])
                val = self.swrite(owr, ("ow0", "ow1"), cc % 2,
                                  waits=[("dv", oc_dv)])
                out_free = val
                ld_free = ("dv", oc_dv)
            hf_state[hf]["ocalc_done"] = ("dv", p.cnt("dv"))
            if hf == 0:
                psCD_free = ("dv", self.rowm_free[1])
                qwst[("mm1",)] = ("pe", hf_state[0]["xn2_users"])
                qwst[("mm2",)] = ("pe", hf_state[0]["xn2_users"])
                qwst[("mm3",)] = hf_state[0]["y3_users"]
                qwst[("qz3",)] = None
                qwst[("wr1",)] = qwst[("wr2",)] = qwst[("wr3",)] = None
                qwst[("hidmul", 0)] = psCD_free
                qwst[("hidmul", 1)] = psCD_free
                y2_rel = [psCD_free, psCD_free]
                hf_state[0]["y2_free"] = [psCD_free, psCD_free]

        ow_tot = [("ow0", p.cnt("ow0")), ("ow1", p.cnt("ow1"))]

        def fin(s_, c):
            for s, v in ow_tot:
                s_.wait_ge(sems[s], v)
            return None
        p.prog["sp"].append(((), fin, None, 0))

    def debug_dump1(self):
        p, sems = self.p, self.sems
        dbg = self.dbg

        def d1(s_, c):
            s_.dma_start(out=dbg["kg"][:, :], in_=self.KG[:, :]).then_inc(sems["ow"], 16)
            s_.dma_start(out=dbg["vg"][:, :], in_=self.VG[:, :]).then_inc(sems["ow"], 16)
            s_.dma_start(out=dbg["gp"][0:1, 0:32], in_=self.GP[0:1, 0:32]).then_inc(sems["ow"], 16)
            s_.dma_start(out=dbg["gp"][0:1, 64:96], in_=self.GP[0:1, 64:96]).then_inc(sems["ow"], 16)
            s_.dma_start(out=dbg["q"][:, :], in_=self.q_dram[:, :]).then_inc(sems["ow"], 16)
            s_.wait_ge(sems["ow"], 80)
            return None
        p.prog["sp"].append(((("bar", self.BAR),), d1, None, 0))

    def finish(self):
        nc, p, sems, cell = self.nc, self.p, self.sems, self.cell
        cid_ap = self.cidt[0:1, 0:1]
        with nc.Block() as block:
            @block.tensor
            def _(t):
                run_prog(t, p.prog["pe"], sems, cell)

            @block.scalar
            def _(s_):
                with s_.register("a_par") as rp:
                    cell["a_par"] = rp
                    s_.wait_ge(sems["gl0"], self.cid_ld)
                    s_.reg_load(rp, cid_ap)
                    s_.reg_alu(rp, rp, 1, ALU.arith_shift_right)
                    run_prog(s_, p.prog["act"], sems, cell)

            @block.vector
            def _(v):
                run_prog(v, p.prog["dve"], sems, cell)

            @block.gpsimd
            def _(g):
                run_prog(g, p.prog["gp"], sems, cell)

            @block.sync
            def _(s_):
                with s_.register("s_cls") as rc:
                    cell["s_cls"] = rc
                    s_.wait_ge(sems["gl0"], self.cid_ld)
                    s_.reg_load(rc, cid_ap)
                    run_prog(s_, p.prog["sp"], sems, cell)
        self.st.close()
        self.verify_sems()

    def verify_sems(self):
        import re
        from collections import defaultdict
        totals = defaultdict(int)
        pat = re.compile(r"update:S\[(\w+)\]\+[+=]?(\d+)")
        for inst in self.nc.all_instructions():
            for m in pat.finditer(str(inst)):
                totals[m.group(1)] += int(m.group(2))
        bad = []
        for name, expect in self.p.sem_static.items():
            actual = totals.get(name, 0)
            if actual != expect and name not in ("poll",):
                bad.append((name, expect, actual))
        if bad:
            raise AssertionError(f"sem count mismatches (plan vs built): {bad}")


def run_prog(eng, prog, sems, cell):
    for waits, fn, inc, n in prog:
        for s, v in waits:
            eng.wait_ge(sems[s], v)
        instr = fn(eng, cell)
        if inc is not None and instr is not None:
            instr.then_inc(sems[inc], n)



def build_nc(debug=0):
    mk = MK(debug)
    mk.phase0()
    mk.phase1()
    if debug == 1:
        mk.debug_dump1()
        mk.finish()
        return mk.nc, mk.dbg
    mk.attn_phase()
    if debug == 2:
        mk.debug_dump2()
        mk.finish()
        return mk.nc, mk.dbg
    mk.wo_phase()
    if debug == 3:
        mk.debug_dump3()
        mk.finish()
        return mk.nc, mk.dbg
    mk.ffn_phase()
    mk.finish()
    return mk.nc, mk.dbg



def _tile_w(wT):
    ein, f = wT.shape
    return np.ascontiguousarray(wT.reshape(ein, f // 128, 128).transpose(1, 0, 2))


def _rope_tables():
    inv = (1.0 / (np.float32(ROPE_BASE) **
                  (np.arange(0, D, 2, dtype=np.float32) / np.float32(D)))).astype(np.float32)
    t = np.arange(T, dtype=np.float32)
    f = t[:, None] * inv[None, :]
    cos = np.concatenate([np.cos(f), np.cos(f)], axis=-1).astype(np.float32)
    sin = np.concatenate([np.sin(f), np.sin(f)], axis=-1).astype(np.float32)
    return np.ascontiguousarray(cos.T), np.ascontiguousarray(sin.T)


def _masks(j):
    import ml_dtypes
    out = np.zeros((128, 16, 4, 128), dtype=np.float32)
    tri = (np.arange(128)[:, None] <= np.arange(128)[None, :]).astype(np.float32)
    for beta in range(16):
        for m in range(4):
            d = beta - 4 * m - j
            if d < 0:
                out[:, beta, m, :] = 1.0
            elif d == 0:
                out[:, beta, m, :] = tri
    return np.ascontiguousarray(out.reshape(128, 16 * 512)).astype(ml_dtypes.bfloat16)


_CACHE = {}


def prep(args):
    _CACHE["rope"] = _rope_tables()
    _CACHE["wt"] = {
        "q": _tile_w(np.ascontiguousarray(np.asarray(args["L0_attn_wq"]).T)),
        "k": _tile_w(np.ascontiguousarray(np.asarray(args["L0_attn_wk"]).T)),
        "o": _tile_w(np.ascontiguousarray(np.asarray(args["L0_attn_wo"]).T)),
        "v": np.ascontiguousarray(np.asarray(args["L0_attn_wv"]).T),
        "1": _tile_w(np.ascontiguousarray(np.asarray(args["L0_ffn_w1"]).T)),
        "2": _tile_w(np.ascontiguousarray(np.asarray(args["L0_ffn_w2"]).T)),
        "3": _tile_w(np.ascontiguousarray(np.asarray(args["L0_ffn_w3"]).T)),
    }


def make_inputs(core, args, nonce):
    b = core // 4
    j = CLASS_OF_L[core % 4]
    toks = token_ids(j)
    x = np.asarray(args["L0_attn_x"])[b]
    cos, sin = _CACHE["rope"]
    wt = _CACHE["wt"]
    return {
        "x": np.ascontiguousarray(x[toks].T),
        "wqT_t": wt["q"], "wkT_t": wt["k"], "woT_t": wt["o"], "wvT": wt["v"],
        "w1T_t": wt["1"], "w2T_t": wt["2"], "w3T_t": wt["3"],
        "cosT": np.ascontiguousarray(cos[:, toks]),
        "sinT": np.ascontiguousarray(
            np.concatenate([-sin[0:D // 2, toks], sin[D // 2:, toks]], axis=0)),
        "masks": _masks(j),
        "cls": np.array([[j]], dtype=np.int32),
        "nonce": np.array([[nonce]], dtype=np.int32),
    }


def kernel(**inputs):
    prep(inputs)
    nonce = int(np.random.randint(1, 2**31 - 1))
    nc, _ = build_nc(debug=0)
    in_maps = [make_inputs(c, inputs, nonce) for c in range(8)]
    res = run_bass_kernel_spmd(nc, in_maps, core_ids=list(range(8)))
    out = np.zeros((B, T, E), dtype=np.float32)
    for c in range(8):
        b = c // 4
        j = CLASS_OF_L[c % 4]
        toks = token_ids(j)[:TQ]
        out[b, toks, :] = np.asarray(res.results[c]["out"]).T
    return out


# version 02
# speedup vs baseline: 1.2746x; 1.2746x over previous
import math
import numpy as np
from contextlib import ExitStack

import concourse.bass as bass
import concourse.mybir as mybir
from concourse.bass_utils import run_bass_kernel_spmd

F32 = mybir.dt.float32
BF16 = mybir.dt.bfloat16
I32 = mybir.dt.int32
AF = mybir.ActivationFunctionType
ALU = mybir.AluOpType
AX = mybir.AxisListType.X

B, T, E, H, D = 2, 8192, 2048, 16, 128
HID = 5632
NE = E // 128
NH = HID // 128
TQ = 2048
TKV = 4096
ROPE_BASE = 1e6
LN_EPS = 1e-5
Q_EPS = 1e-8
SCALE = 1.0 / math.sqrt(D)
QCNT = float(HID * E)

ENGINES = ("pe", "act", "dve", "gp", "sp")
CLASS_OF_L = {0: 0, 2: 1, 1: 2, 3: 3}
OTHER = {0: 1, 1: 0, 2: 3, 3: 2}


def qblocks(j):
    return [4 * i + j for i in range(16)]


def local_blocks(j):
    return qblocks(j) + qblocks(OTHER[j])


def token_ids(j):
    return np.concatenate([np.arange(g * 128, (g + 1) * 128) for g in local_blocks(j)])


class Plan:
    def __init__(self):
        self.prog = {e: [] for e in ENGINES}
        self.sem = {}
        self.sem_static = {}

    def cnt(self, s):
        return self.sem.get(s, 0)

    def bump(self, s, n, static_n=None):
        self.sem[s] = self.cnt(s) + n
        self.sem_static[s] = self.sem_static.get(s, 0) + (n if static_n is None else static_n)
        return self.sem[s]

    def op(self, eng, fn, waits=(), inc=None, n=1):
        if inc is not None:
            self.bump(inc, n)
        self.prog[eng].append((tuple(x for x in waits if x is not None), fn, inc, n))


class Stream:

    def __init__(self, plan, queue, lanes):
        self.p, self.q, self.lanes, self.i = plan, queue, lanes, 0

    def load(self, fn, free=()):
        lane = self.lanes[self.i % 2]
        self.i += 1
        waits = []
        if self.p.cnt(lane) > 0:
            waits.append((lane, self.p.cnt(lane)))
        waits += [f for f in free if f is not None]
        self.p.op(self.q, fn, waits=waits, inc=lane, n=16)
        return (lane, self.p.cnt(lane))


class Rot:
    def __init__(self, n):
        self.n = n
        self.rel = [None] * n
        self.i = 0

    def acquire(self):
        lane = self.i % self.n
        self.i += 1
        return lane, self.rel[lane]

    def release(self, lane, sem, val):
        self.rel[lane] = (sem, val)



class MK:
    def __init__(self, debug=0):
        self.debug = debug
        nc = self.nc = bass.Bass("TRN2", target_bir_lowering=False, debug=False)
        p = self.p = Plan()
        self.st = ExitStack()
        self.cell = {}

        def din(name, shape, dt=F32):
            return nc.dram_tensor(name, shape, dt, kind="ExternalInput").ap()

        self.x_in = din("x", [E, TKV])
        self.wqT_t = din("wqT_t", [NE, E, 128])
        self.wkT_t = din("wkT_t", [NE, E, 128])
        self.woT_t = din("woT_t", [NE, E, 128])
        self.wvT = din("wvT", [E, E])
        self.w1T_t = din("w1T_t", [NH, E, 128])
        self.w2T_t = din("w2T_t", [NH, E, 128])
        self.w3T_t = din("w3T_t", [NE, HID, 128])
        self.cos_in = din("cosT", [D, TKV])
        self.sin_in = din("sinT", [D, TKV])
        self.masks_in = din("masks", [128, 16 * 512], BF16)
        self.cls_in = din("cls", [1, 1], I32)
        self.nonce_in = din("nonce", [1, 1], I32)

        self.out_d = nc.dram_tensor("out", [E, TQ], F32, kind="ExternalOutput").ap()
        self.q_dram = nc.dram_tensor("q_dram", [E, TQ], BF16).ap()
        self.attn_dram = nc.dram_tensor("attn_dram", [E, TQ], BF16).ap()
        self.xmid_dram = nc.dram_tensor("xmid_dram", [E, TQ], F32).ap()
        self.attn2_dram = nc.dram_tensor("attn2_dram", [E, TQ], BF16).ap()
        self.qw1_dram = nc.dram_tensor("qw1_dram", [NH, 128, E], BF16).ap()
        self.qw2_dram = nc.dram_tensor("qw2_dram", [NH, 128, E], BF16).ap()
        self.qw3_dram = nc.dram_tensor("qw3_dram", [NE, 128, HID], BF16).ap()
        self.KG = nc.dram_tensor("KG", [E, T], BF16, addr_space="Shared").ap()
        self.VG = nc.dram_tensor("VG", [T, E], BF16, addr_space="Shared").ap()
        self.GP = nc.dram_tensor("GP", [1, 128], F32, addr_space="Shared").ap()
        self.FLAG = nc.dram_tensor("FLAG", [1, 2], I32, addr_space="Shared").ap()

        self.dbg = {}
        if debug == 1:
            self.dbg["kg"] = nc.dram_tensor("dbg_kg", [E, T], BF16, kind="ExternalOutput").ap()
            self.dbg["vg"] = nc.dram_tensor("dbg_vg", [T, E], BF16, kind="ExternalOutput").ap()
            self.dbg["q"] = nc.dram_tensor("dbg_q", [E, TQ], BF16, kind="ExternalOutput").ap()
            self.dbg["gp"] = nc.dram_tensor("dbg_gp", [1, 128], F32, kind="ExternalOutput").ap()
        if debug == 2:
            self.dbg["at"] = nc.dram_tensor("dbg_at", [E, TQ], BF16, kind="ExternalOutput").ap()
        if debug == 3:
            self.dbg["xm"] = nc.dram_tensor("dbg_xm", [E, TQ], F32, kind="ExternalOutput").ap()
        if debug == 4:
            self.dbg["xn2"] = nc.dram_tensor("dbg_xn2", [E, 1024], BF16, kind="ExternalOutput").ap()
            self.dbg["y3"] = nc.dram_tensor("dbg_y3", [E, 1024], BF16, kind="ExternalOutput").ap()
            self.dbg["sig"] = nc.dram_tensor("dbg_sig", [128, 1024], BF16, kind="ExternalOutput").ap()
            self.dbg["hid"] = nc.dram_tensor("dbg_hid", [128, 1024], BF16, kind="ExternalOutput").ap()

        def sbt(name, shape, dt):
            return self.st.enter_context(nc.sbuf_tensor(name, shape, dt))

        self.A = sbt("A", [128, 32768], BF16)
        self.Bb = sbt("Bb", [128, 45056], BF16)
        self.C = sbt("C", [128, 4096], F32)
        self.Eb = sbt("Eb", [128, 4096], BF16)
        self.Fb = sbt("Fb", [128, 5632], BF16)
        self.rows = sbt("rows", [1, 4096], F32)
        self.misc_f = sbt("misc_f", [128, 64], F32)
        self.mrow = sbt("mrow", [1, 128], F32)
        self.gstat = sbt("gstat", [128, 72], F32)
        self.gprow = sbt("gprow", [1, 128], F32)
        self.onecol_b = sbt("onecol_b", [128, 1], BF16)
        self.onecol_f = sbt("onecol_f", [128, 1], F32)
        self.onerow_f = sbt("onerow_f", [1, 128], F32)
        self.cidt = sbt("cidt", [1, 2], I32)
        self.flagt = sbt("flagt", [1, 2], I32)
        self.sig4 = sbt("sig4", [128, 1024], BF16)

        A, Bb, C, Eb = self.A, self.Bb, self.C, self.Eb
        self.xb = [A[:, c * 2048:(c + 1) * 2048] for c in range(NE)]
        self.wv_sp = [Bb[:, i * 8192:(i + 1) * 8192] for i in range(2)]
        self.wt_b = [Bb[:, 16384 + i * 2048:16384 + (i + 1) * 2048] for i in range(2)]
        self.ksb = [Bb[:, 20480 + i * 2048:20480 + (i + 1) * 2048] for i in range(2)]
        self.sqs = [Bb[:, 24576 + i * 2048:24576 + (i + 1) * 2048] for i in range(2)]
        self.ab_A = Bb[:, 28672:30720]
        self.ab_B = Bb[:, 30720:32768]
        self.t12 = [Bb[:, 32768:34816], Bb[:, 39936:41984]]
        self.evacb = [Bb[:, 34816 + i * 2048:34816 + (i + 1) * 2048] for i in range(2)]
        self.vtile = [Bb[:, 38912 + i * 512:38912 + (i + 1) * 512] for i in range(2)]
        self.cosb = Eb[:, 0:2048]
        self.sinb = Eb[:, 2048:4096]
        self.gbuf = C[:, 0:2048]
        self.rrow = self.rows[0:1, 0:512]
        self.rowX = self.rows[0:1, 0:2048]
        self.rowQ = self.rows[0:1, 2048:4096]
        self.rowM = C[0:1, 2048:4096]

        self.kh = [A[:, i * 8192:(i + 1) * 8192] for i in range(2)]
        self.vh = [A[:, 16384 + i * 8192:16384 + (i + 1) * 8192] for i in range(2)]
        self.masksb = Bb[:, 0:8192]
        self.qhb = [Bb[:, 8192 + i * 2048:8192 + (i + 1) * 2048] for i in range(2)]
        self.pbuf = [Bb[:, 12288 + i * 1024:12288 + (i + 1) * 1024] for i in range(2)]
        self.at_t = [Bb[:, 14336 + i * 512:14336 + (i + 1) * 512] for i in range(2)]
        self.rbig = C[:, 0:512]

        self.worhs = [Bb[:, i * 8192:(i + 1) * 8192] for i in range(2)]
        self.wt3_b = [Bb[:, 16384 + i * 2048:16384 + (i + 1) * 2048] for i in range(2)]
        self.asq3 = [Bb[:, 20480 + i * 2048:20480 + (i + 1) * 2048] for i in range(2)]
        self.xre = C[:, 0:2048]
        self.xmo = C[:, 2048:4096]

        self.xn2 = [A[:, c * 1024:(c + 1) * 1024] for c in range(NE)]
        self.y3b = [A[:, 16384 + c * 1024:16384 + (c + 1) * 1024] for c in range(NE)]
        self.hidv = [Bb[:, h * 1024:(h + 1) * 1024] for h in range(NH)]
        self.qw1v = Eb[:, 0:2048]
        self.qw2v = Eb[:, 2048:4096]
        self.qw3v = self.Fb[:, 0:5632]
        self.raw1 = C[:, 0:2048]
        self.raw2 = C[:, 2048:4096]
        self.ab4_A = self.rows[0:1, 0:1024]
        self.ab4_B = self.rows[0:1, 1024:2048]
        self.row4X = self.rows[0:1, 2048:3072]
        self.row4Q = self.rows[0:1, 3072:4096]

        psume = self.st.enter_context
        self.psS0 = psume(nc.psum_tensor("psS0", [128, 1024], F32))
        self.psS1 = psume(nc.psum_tensor("psS1", [128, 1024], F32))
        self.psA = psume(nc.psum_tensor("psA", [128, 512], F32))
        self.psB = psume(nc.psum_tensor("psB", [128, 512], F32))
        self.psC = psume(nc.psum_tensor("psC", [128, 512], F32))
        self.psD = psume(nc.psum_tensor("psD", [128, 512], F32))
        self.ps_acc = [self.psA, self.psB]
        self.psS = [self.psS0, self.psS1]
        self.statrows = [self.psS0[0:1, 0:512], self.psS0[0:1, 512:1024],
                         self.psS1[0:1, 0:512], self.psS1[0:1, 512:1024],
                         self.psA[0:1, 0:512], self.psB[0:1, 0:512],
                         self.psC[0:1, 0:512], self.psD[0:1, 0:512]]

        self.sems = {}
        for s in ["gl0", "gl1", "gl2", "gl3",
                  "sk0", "sk1", "sv0", "sv1", "sq0", "sq1", "sm0", "sm1",
                  "sr0", "sr1", "cdv", "cac",
                  "kw0", "kw1", "vw0", "vw1", "qw0", "qw1", "aw0", "aw1",
                  "xw0", "xw1", "ww0", "ww1", "ow0", "ow1",
                  "sw", "sw0", "sw1", "poll", "bar",
                  "pe", "ac", "dv", "pc", "ow"]:
            self.sems[s] = self.st.enter_context(nc.semaphore(s))

        self.gp_st = Stream(p, "gp", ["gl0", "gl1"])
        self.gp_st2 = Stream(p, "gp", ["gl2", "gl3"])
        self.sc_k = Stream(p, "act", ["sk0", "sk1"])
        self.sc_v = Stream(p, "act", ["sv0", "sv1"])
        self.sc_q = Stream(p, "act", ["sq0", "sq1"])
        self.sc_m = Stream(p, "act", ["sm0", "sm1"])
        self.t12_rot = Rot(2)

        self.acc_rot = Rot(2)
        self.wt_rot = Rot(2)
        self.ev_rot = Rot(2)
        self.ksb_rot = Rot(2)
        self.sqs_rot = Rot(2)
        self.vt_rot = Rot(2)
        self.stat_free = [None] * 8
        self.psS1row_free = None

    def w(self, eng, fn, waits=(), inc=None, n=1):
        self.p.op(eng, fn, waits, inc, n)

    def chain(self, eng, cell, instr, key="cdv"):
        cell[key] = cell.get(key, 0) + 1
        instr.then_inc(self.sems[key], 1)
        eng.wait_ge(self.sems[key], cell[key])

    def branch4(self, make, inc_sem, n=16, waits=(), eng="sp", reg="s_cls",
                nparts_hint=1):
        sems = self.sems

        nparts = [None]

        def attach(r):
            rs = r if isinstance(r, (list, tuple)) else [r]
            if nparts[0] is None:
                nparts[0] = len(rs)
            assert nparts[0] == len(rs)
            for x in rs:
                x.then_inc(sems[inc_sem], n)

        def fn(g, c):
            with g.If_eq(c[reg], 0):
                attach(make(g, 0))
            with g.Else():
                with g.If_eq(c[reg], 1):
                    attach(make(g, 1))
                with g.Else():
                    with g.If_eq(c[reg], 2):
                        attach(make(g, 2))
                    with g.Else():
                        attach(make(g, 3))
            return None
        self.p.bump(inc_sem, n * nparts_hint, static_n=4 * n * nparts_hint)
        self.p.prog[eng].append((tuple(x for x in waits if x is not None), fn, None, 0))
        return self.p.cnt(inc_sem)

    def swrite(self, make_fn, lanes, lane01, waits=()):
        sems = self.sems
        swl = lanes[lane01]
        wt = [x for x in waits if x is not None]
        if self.p.cnt(swl) > 0:
            wt.append((swl, self.p.cnt(swl)))
        self.p.bump(swl, 16)

        def fn(s_, c):
            make_fn(s_, c).then_inc(sems[swl], 16)
            return None
        self.p.prog["sp"].append((tuple(wt), fn, None, 0))
        return (swl, self.p.cnt(swl))

    def bwrite(self, make, lanes, lane01, nparts, waits=()):
        swl = lanes[lane01]
        wt = [x for x in waits if x is not None]
        if self.p.cnt(swl) > 0:
            wt.append((swl, self.p.cnt(swl)))
        val = self.branch4(make, swl, waits=wt, nparts_hint=nparts)
        return (swl, val)

    def phase0(self):
        p, w, sems = self.p, self.w, self.sems
        w("gp", lambda g, c: g.dma_start(out=self.cidt[0:1, 0:1], in_=self.cls_in[:, :]),
          inc="gl0", n=16)
        w("gp", lambda g, c: g.dma_start(out=self.cidt[0:1, 1:2], in_=self.nonce_in[:, :]),
          waits=[("gl0", 16)], inc="gl0", n=16)
        self.cid_ld = p.cnt("gl0")

        w("dve", lambda v, c: v.memset(self.onecol_b[:, :], 1.0))
        w("dve", lambda v, c: v.memset(self.onecol_f[:, :], 1.0))
        w("dve", lambda v, c: v.memset(self.onerow_f[:, :], 1.0))
        w("dve", lambda v, c: v.memset(self.gprow[:, :], 0.0))
        w("dve", lambda v, c: v.memset(self.mrow[:, :], 0.0), inc="dv")
        self.dv_const = p.cnt("dv")

    def phase1(self):
        p, w = self.p, self.w

        gseq = {0: [], 1: []}
        for wi, wt_src in ((0, self.w1T_t), (1, self.w2T_t)):
            for hc in range(NH):
                gseq[hc % 2].append(
                    (wi, wt_src[hc].rearrange("(c p) h -> p c h", p=128), 2048))
        for fc in range(NE):
            src = self.w3T_t[fc].rearrange("(c p) f -> p c f", p=128)
            for s0, ln in ((0, 2048), (2048, 2048), (4096, 1536)):
                gseq[fc % 2].append((2, src[:, s0 // 128:(s0 + ln) // 128, :], ln))
        ngp = len(gseq[0])
        assert len(gseq[1]) == ngp and ngp <= 72
        for i in range(ngp):
            assert gseq[0][i][0] == gseq[1][i][0] and gseq[0][i][2] == gseq[1][i][2]
        self.gseq, self.ngp = gseq, ngp
        self.gred_dv = [None] * ngp
        self.gw_cols = {wi: [i for i in range(ngp) if gseq[0][i][0] == wi]
                        for wi in range(3)}
        self._gamma_i = 0

        self.emit_half(0)
        self.emit_half(1)
        self.emit_gamma(ngp)
        self.gamma_final()
        self.barrier()

    def emit_gamma(self, k):
        p, w, sems = self.p, self.w, self.sems
        i0 = self._gamma_i
        for i in range(i0, min(i0 + k, self.ngp)):
            def gdma(s, c, i=i):
                w0, src0, ln0 = self.gseq[0][i]
                _, src1, _ = self.gseq[1][i]
                gout = self.gbuf[:, 0:ln0].rearrange("p (c h) -> p c h", h=128)
                with s.If_eq(c["a_par"], 0):
                    s.dma_start(out=gout, in_=src0).then_inc(sems["sm0"], 16)
                with s.Else():
                    s.dma_start(out=gout, in_=src1).then_inc(sems["sm0"], 16)
                return None
            waits = ([("dv", self.gred_dv[i - 1])] if i > 0
                     else [("dv", self.dv_const)])
            p.bump("sm0", 16, static_n=32)
            p.prog["act"].append((tuple(waits), gdma, None, 0))
            sl_val = p.cnt("sm0")

            def gred(v, c, i=i):
                _, _, ln = self.gseq[0][i]
                return v.tensor_reduce(self.gstat[:, i:i + 1], self.gbuf[:, 0:ln],
                                       axis=AX, op=ALU.add, apply_absolute_value=True)
            w("dve", gred, waits=[("sm0", sl_val)], inc="dv")
            self.gred_dv[i] = p.cnt("dv")
        self._gamma_i = min(i0 + k, self.ngp)

    def emit_half(self, half):
        p, w, cell = self.p, self.w, self.cell
        sems = self.sems
        xb, sqs, statrows = self.xb, self.sqs, self.statrows
        xoff = half * TQ

        if half == 1:
            self.stat_free[4] = self.acc_rot.rel[0]
            self.stat_free[5] = self.acc_rot.rel[1]
            for k in range(4):
                self.stat_free[k] = self.psS1row_free

        xld = []
        for cc in range(NE):
            def ldx(g, c, cc=cc, xoff=xoff):
                return g.dma_start(
                    out=xb[cc], in_=self.x_in[cc * 128:(cc + 1) * 128, xoff:xoff + TQ])
            free = [("pe", cell["h0_xn_users_pe"])] if half == 1 else []
            xld.append(self.gp_st.load(ldx, free=free))
        free = [("dv", cell["h0_rope_dv"])] if half == 1 else []
        cs0 = self.gp_st2.load(lambda g, c, xoff=xoff: g.dma_start(
            out=self.cosb, in_=self.cos_in[:, xoff:xoff + TQ]), free=free)
        cs1 = self.gp_st2.load(lambda g, c, xoff=xoff: g.dma_start(
            out=self.sinb, in_=self.sin_in[:, xoff:xoff + TQ]), free=free)

        for cc in range(NE):
            lane, rel = self.sqs_rot.acquire()

            def sq(s, c, cc=cc, lane=lane):
                return s.activation(out=sqs[lane], in_=xb[cc], func=AF.Square)
            w("act", sq, waits=[xld[cc], rel], inc="ac")
            sq_ac = p.cnt("ac")
            for sp in range(4):
                def mmx(t, c, cc=cc, sp=sp):
                    return t.matmul(statrows[sp], self.onecol_b[:, :],
                                    xb[cc][:, sp * 512:(sp + 1) * 512],
                                    start=(cc == 0), stop=(cc == NE - 1))
                wt_ = [xld[cc]] if sp == 0 else []
                if cc == 0:
                    wt_.append(self.stat_free[sp])
                    self.stat_free[sp] = None
                    if sp == 0:
                        wt_.append(("dv", self.dv_const))
                w("pe", mmx, waits=wt_, inc="pe")
            for sp in range(4):
                def mmq(t, c, cc=cc, sp=sp, lane=lane):
                    return t.matmul(statrows[4 + sp], self.onecol_b[:, :],
                                    sqs[lane][:, sp * 512:(sp + 1) * 512],
                                    start=(cc == 0), stop=(cc == NE - 1))
                wt_ = []
                if sp == 0:
                    wt_.append(("ac", sq_ac))
                if cc == 0:
                    wt_.append(self.stat_free[4 + sp])
                    self.stat_free[4 + sp] = None
                w("pe", mmq, waits=wt_, inc="pe")
            self.sqs_rot.release(lane, "pe", p.cnt("pe"))
        stats_pe = p.cnt("pe")

        rowX, rowQ, rowM = self.rowX, self.rowQ, self.rowM

        def rows_calc(v, c):
            ch = self.chain
            for sp in range(4):
                v.tensor_copy(rowX[0:1, sp * 512:(sp + 1) * 512], statrows[sp])
            for sp in range(4):
                i = v.tensor_copy(rowQ[0:1, sp * 512:(sp + 1) * 512], statrows[4 + sp])
            ch(v, c, i)
            ch(v, c, v.tensor_scalar(rowX, rowX, 1.0 / E, None, op0=ALU.mult))
            ch(v, c, v.tensor_scalar(rowQ, rowQ, 1.0 / E, None, op0=ALU.mult))
            ch(v, c, v.tensor_mul(rowM, rowX, rowX))
            ch(v, c, v.tensor_sub(rowQ, rowQ, rowM))
            return v.tensor_scalar(rowQ, rowQ, LN_EPS, None, op0=ALU.add)
        w("dve", rows_calc, waits=[("pe", stats_pe)], inc="dv")
        for sp in range(8):
            self.stat_free[sp] = ("dv", p.cnt("dv"))
        w("act", lambda s, c: s.activation(out=rowQ, in_=rowQ, func=AF.Sqrt),
          waits=[("dv", p.cnt("dv"))], inc="ac")

        def rows_fin(v, c):
            ch = self.chain
            ch(v, c, v.reciprocal(rowQ, rowQ))
            ch(v, c, v.tensor_mul(rowX, rowX, rowQ))
            return v.tensor_scalar(rowX, rowX, -1.0, None, op0=ALU.mult)
        w("dve", rows_fin, waits=[("ac", p.cnt("ac"))], inc="dv")
        rows_dv = p.cnt("dv")

        ab_ac = None
        for src_row, dst in ((rowQ, self.ab_A), (rowX, self.ab_B)):
            for sp in range(4):
                def bmm(t, c, src_row=src_row, sp=sp):
                    return t.matmul(self.psC[:, :], self.onerow_f[:, :],
                                    src_row[0:1, sp * 512:(sp + 1) * 512],
                                    start=True, stop=True)
                wt_ = [("dv", rows_dv)]
                if ab_ac is not None:
                    wt_.append(("ac", ab_ac))
                else:
                    wt_.append(self.stat_free[6])
                    self.stat_free[6] = None
                w("pe", bmm, waits=wt_, inc="pe")

                def bcp(s, c, dst=dst, sp=sp):
                    return s.copy(out=dst[:, sp * 512:(sp + 1) * 512], in_=self.psC[:, :])
                w("act", bcp, waits=[("pe", p.cnt("pe"))], inc="ac")
                ab_ac = p.cnt("ac")
        self.stat_free[6] = ("ac", ab_ac)
        self.rows_free = ("pe", p.cnt("pe"))

        for cc in range(NE):
            lane, rel = self.sqs_rot.acquire()

            def ap1(v, c, cc=cc, lane=lane):
                return v.tensor_mul(sqs[lane], xb[cc], self.ab_A)

            def ap2(v, c, cc=cc, lane=lane):
                return v.tensor_add(xb[cc], sqs[lane], self.ab_B)
            w("dve", ap1, waits=[("ac", ab_ac), rel], inc="dv")
            w("dve", ap2, waits=[("dv", p.cnt("dv"))], inc="dv")
            self.sqs_rot.release(lane, "dv", p.cnt("dv"))
        xn_dv = p.cnt("dv")

        self.emit_gamma(10)

        for which in (["k", "q"] if half == 0 else ["k"]):
            wt_src = self.wkT_t if which == "k" else self.wqT_t
            for fc in range(NE):
                wl_lane, wl_rel = self.wt_rot.acquire()

                def ldw(g, c, wt_src=wt_src, fc=fc, wl_lane=wl_lane):
                    return g.dma_start(
                        out=self.wt_b[wl_lane].rearrange("p (c f) -> p c f", f=128),
                        in_=wt_src[fc].rearrange("(c p) f -> p c f", p=128))
                wl = self.gp_st.load(ldw, free=[wl_rel])

                ev_lane, ev_rel = self.ev_rot.acquire()
                for sp in range(4):
                    a_lane, a_rel = self.acc_rot.acquire()
                    acc = self.ps_acc[a_lane]
                    for e_ in range(NE):
                        def mm(t, c, acc=acc, wl_lane=wl_lane, e_=e_, sp=sp):
                            return t.matmul(
                                acc[:, :], self.wt_b[wl_lane][:, e_ * 128:(e_ + 1) * 128],
                                xb[e_][:, sp * 512:(sp + 1) * 512],
                                start=(e_ == 0), stop=(e_ == NE - 1))
                        wt_ = []
                        if e_ == 0:
                            wt_ = [wl, ("dv", xn_dv), a_rel]
                        w("pe", mm, waits=wt_, inc="pe")
                    mm_pe = p.cnt("pe")

                    def ev(s, c, acc=acc, sp=sp, ev_lane=ev_lane):
                        return s.copy(out=self.evacb[ev_lane][:, sp * 512:(sp + 1) * 512],
                                      in_=acc[:, :])
                    w("act", ev,
                      waits=[("pe", mm_pe), ev_rel if sp == 0 else None], inc="ac")
                    self.acc_rot.release(a_lane, "ac", p.cnt("ac"))
                self.wt_rot.release(wl_lane, "pe", p.cnt("pe"))
                ev_ac = p.cnt("ac")

                k_lane, k_rel = self.ksb_rot.acquire()
                t_lane, t_rel = self.t12_rot.acquire()
                src, dst = self.evacb[ev_lane], self.ksb[k_lane]
                t12x, cosb, sinb = self.t12[t_lane], self.cosb, self.sinb
                srl = "sr0" if t_lane == 0 else "sr1"
                swap_waits = [("ac", ev_ac), t_rel]
                if p.cnt(srl) > 0:
                    swap_waits.append((srl, p.cnt(srl)))

                def swap(s, c, src=src, t12x=t12x, srl=srl):
                    s.dma_start(out=t12x[0:64, :],
                                in_=src[64:128, :]).then_inc(sems[srl], 16)
                    s.dma_start(out=t12x[64:128, :],
                                in_=src[0:64, :]).then_inc(sems[srl], 16)
                    return None
                p.bump(srl, 32)
                p.prog["act"].append((tuple(x for x in swap_waits if x is not None),
                                      swap, None, 0))
                swap_val = (srl, p.cnt(srl))

                def rope(v, c, src=src, dst=dst, t12x=t12x):
                    v.tensor_mul(dst, src, cosb)
                    self.chain(v, c, v.tensor_mul(t12x, t12x, sinb))
                    return v.tensor_add(dst, dst, t12x)
                w("dve", rope, waits=[swap_val, cs0, cs1, k_rel], inc="dv")
                rope_dv = p.cnt("dv")
                self.ev_rot.release(ev_lane, "dv", rope_dv)
                self.t12_rot.release(t_lane, "dv", rope_dv)

                sq_lane, sq_rel = self.sqs_rot.acquire()

                def nsq(s, c, k_lane=k_lane, sq_lane=sq_lane):
                    return s.activation(out=sqs[sq_lane], in_=self.ksb[k_lane],
                                        func=AF.Square)
                w("act", nsq, waits=[("dv", rope_dv), sq_rel], inc="ac")
                nsq_ac = p.cnt("ac")
                nrow_tgts = [self.psS0[0:1, 0:512], self.psS0[0:1, 512:1024],
                             self.psS1[0:1, 0:512], self.psS1[0:1, 512:1024]]
                for sp in range(4):
                    def nmm(t, c, sp=sp, sq_lane=sq_lane):
                        return t.matmul(nrow_tgts[sp], self.onecol_b[:, :],
                                        sqs[sq_lane][:, sp * 512:(sp + 1) * 512],
                                        start=True, stop=True)
                    wt_ = []
                    if sp == 0:
                        wt_ = [("ac", nsq_ac), self.psS1row_free]
                        wt_ += [self.stat_free[k] for k in range(4)]
                        for k in range(4):
                            self.stat_free[k] = None
                    w("pe", nmm, waits=wt_, inc="pe")
                self.sqs_rot.release(sq_lane, "pe", p.cnt("pe"))
                nmm_pe = p.cnt("pe")

                def nred(v, c, which=which, fc=fc, half=half):
                    ch = self.chain
                    if which == "q":
                        for sp in range(4):
                            i = v.tensor_reduce(self.mrow[0:1, 4 * fc + sp:4 * fc + sp + 1],
                                                nrow_tgts[sp], axis=AX, op=ALU.max)
                        return i
                    ch(v, c, v.tensor_reduce(self.mrow[0:1, 80:81],
                                             self.psS0[0:1, 0:1024],
                                             axis=AX, op=ALU.max))
                    ch(v, c, v.tensor_reduce(self.mrow[0:1, 81:82],
                                             self.psS1[0:1, 0:1024],
                                             axis=AX, op=ALU.max))
                    ch(v, c, v.tensor_max(self.mrow[0:1, 80:81],
                                          self.mrow[0:1, 80:81],
                                          self.mrow[0:1, 81:82]))
                    tgt = self.mrow[0:1, 64 + fc:64 + fc + 1]
                    if half == 1:
                        return v.tensor_max(tgt, tgt, self.mrow[0:1, 80:81])
                    return v.tensor_copy(tgt, self.mrow[0:1, 80:81])
                w("dve", nred, waits=[("pe", nmm_pe)], inc="dv")
                self.psS1row_free = ("dv", p.cnt("dv"))

                if which == "q":
                    def qwr(s_, c, fc=fc, k_lane=k_lane):
                        return s_.dma_start(
                            out=self.q_dram[fc * 128:(fc + 1) * 128, :],
                            in_=self.ksb[k_lane])
                    val = self.swrite(qwr, ("qw0", "qw1"), fc % 2,
                                      waits=[("dv", rope_dv)])
                    self.ksb_rot.release(k_lane, *val)
                else:
                    KG = self.KG

                    def kwr(s_, j, fc=fc, half=half, k_lane=k_lane):
                        jj = j if half == 0 else OTHER[j]
                        out = []
                        for blk in range(16):
                            g_ = 4 * blk + jj
                            out.append(s_.dma_start(
                                out=KG[fc * 128:(fc + 1) * 128,
                                       g_ * 128:(g_ + 1) * 128],
                                in_=self.ksb[k_lane][:, blk * 128:(blk + 1) * 128]))
                        return out
                    val = self.bwrite(kwr, ("kw0", "kw1"), fc % 2, 16,
                                      waits=[("dv", rope_dv)])
                    self.ksb_rot.release(k_lane, *val)

        self.emit_gamma(10)

        for fs in range(4):
            wv_lane = fs % 2
            wv_rel = cell.get(("wv_rel", wv_lane))

            def ldv(g, c, fs=fs, wv_lane=wv_lane):
                return g.dma_start(
                    out=self.wv_sp[wv_lane].rearrange("p (c f) -> p c f", f=512),
                    in_=self.wvT[:, fs * 512:(fs + 1) * 512]
                    .rearrange("(c p) f -> p c f", p=128))
            vl = self.gp_st2.load(ldv, free=[wv_rel])
            for tt in range(16):
                a_lane, a_rel = self.acc_rot.acquire()
                acc = self.ps_acc[a_lane]
                for e_ in range(NE):
                    def vmm(t, c, acc=acc, wv_lane=wv_lane, e_=e_, tt=tt):
                        return t.matmul(
                            acc[:, :], xb[e_][:, tt * 128:(tt + 1) * 128],
                            self.wv_sp[wv_lane][:, e_ * 512:(e_ + 1) * 512],
                            start=(e_ == 0), stop=(e_ == NE - 1))
                    wt_ = [vl, ("dv", xn_dv), a_rel] if e_ == 0 else []
                    w("pe", vmm, waits=wt_, inc="pe")
                vmm_pe = p.cnt("pe")

                vt_lane, vt_rel = self.vt_rot.acquire()

                def vev(s, c, acc=acc, vt_lane=vt_lane):
                    return s.copy(out=self.vtile[vt_lane], in_=acc[:, :])
                w("act", vev, waits=[("pe", vmm_pe), vt_rel], inc="ac")
                vev_ac = p.cnt("ac")
                self.acc_rot.release(a_lane, "ac", vev_ac)

                VG = self.VG

                def vwr(s_, j, tt=tt, half=half, fs=fs, vt_lane=vt_lane):
                    g_ = local_blocks(j)[tt + 16 * half]
                    return s_.dma_start(
                        out=VG[g_ * 128:(g_ + 1) * 128, fs * 512:(fs + 1) * 512],
                        in_=self.vtile[vt_lane])
                val = self.bwrite(vwr, ("vw0", "vw1"), tt % 2, 1,
                                  waits=[("ac", vev_ac)])
                self.vt_rot.release(vt_lane, *val)
            cell[("wv_rel", wv_lane)] = ("pe", p.cnt("pe"))

        if half == 0:
            cell["h0_xn_users_pe"] = p.cnt("pe")
            cell["h0_rope_dv"] = p.cnt("dv")
        self.emit_gamma(16)

    def gamma_final(self):
        p, w = self.p, self.w
        first_waits = [("dv", self.gred_dv[-1]), self.stat_free[7],
                       self.psS1row_free]
        self.stat_free[7] = None
        for wi in range(3):
            cols = self.gw_cols[wi]
            c0, c1 = min(cols), max(cols) + 1
            assert cols == list(range(c0, c1))

            def gmm(t, c, c0=c0, c1=c1):
                return t.matmul(self.psD[0:1, 0:c1 - c0], self.onecol_f[:, :],
                                self.gstat[:, c0:c1], start=True, stop=True)
            w("pe", gmm, waits=first_waits if wi == 0 else [("dv", p.cnt("dv"))],
              inc="pe")

            def gsum(v, c, wi=wi, c0=c0, c1=c1):
                return v.tensor_reduce(self.mrow[0:1, 84 + wi:85 + wi],
                                       self.psD[0:1, 0:c1 - c0], axis=AX, op=ALU.add)
            w("dve", gsum, waits=[("pe", p.cnt("pe"))], inc="dv")
        self.gam_dv = p.cnt("dv")
        self.psDrow_free = ("dv", self.gam_dv)

    def barrier(self):
        p, w, sems = self.p, self.w, self.sems

        def gp_stage(v, c):
            v.tensor_copy(self.gprow[0:1, 0:3], self.mrow[0:1, 84:87])
            return v.tensor_copy(self.gprow[0:1, 16:32], self.mrow[0:1, 64:80])
        w("dve", gp_stage, waits=[("dv", self.gam_dv)], inc="dv")
        stage_dv = p.cnt("dv")

        GP = self.GP

        def gpwr(s_, j):
            base = 0 if j in (0, 1) else 64
            return s_.dma_start(out=GP[0:1, base:base + 32], in_=self.gprow[0:1, 0:32])
        self.branch4(gpwr, "sw", waits=[("dv", stage_dv)])

        flag_waits = [(s, p.cnt(s))
                      for s in ("kw0", "kw1", "vw0", "vw1", "qw0", "qw1", "sw")
                      if p.cnt(s) > 0]

        def pre_flag(s_, c):
            for s, v in flag_waits:
                s_.wait_ge(sems[s], v)
            return None
        p.prog["sp"].append(((), pre_flag, None, 0))

        FLAG = self.FLAG

        def flagwr(s_, j):
            slot = 0 if j in (0, 1) else 1
            return s_.dma_start(out=FLAG[0:1, slot:slot + 1], in_=self.cidt[0:1, 1:2])
        self.branch4(flagwr, "sw")

        sw_done = p.cnt("sw")

        def spin(s_, c):
            s_.wait_ge(sems["sw"], sw_done)
            with s_.register("r_nd") as rnd, s_.register("r_f0") as rf0, \
                    s_.register("r_f1") as rf1, s_.register("r_non") as rnon:
                s_.reg_load(rnon, self.cidt[0:1, 1:2])
                s_.reg_mov(rnd, 1)
                with s_.While(rnd):
                    s_.sem_clear(sems["poll"])
                    s_.dma_start(out=self.flagt[:, :],
                                 in_=FLAG[:, :]).then_inc(sems["poll"], 16)
                    s_.wait_ge(sems["poll"], 16)
                    s_.reg_load(rf0, self.flagt[0:1, 0:1])
                    s_.reg_load(rf1, self.flagt[0:1, 1:2])
                    s_.reg_alu(rf0, rf0, rnon, ALU.is_equal)
                    s_.reg_alu(rf1, rf1, rnon, ALU.is_equal)
                    s_.reg_alu(rnd, rf0, rf1, ALU.add)
                    s_.reg_alu(rnd, rnd, 2, ALU.is_lt)
            s_.sem_clear(sems["poll"])
            return None
        p.prog["sp"].append(((), spin, None, 0))
        w("sp", lambda s_, c: s_.nop(), inc="bar")
        self.BAR = p.cnt("bar")

    def attn_phase(self):
        p, w, cell, sems = self.p, self.w, self.cell, self.sems
        BAR = self.BAR
        kh, vh, qhb, pbuf, at_t = self.kh, self.vh, self.qhb, self.pbuf, self.at_t
        psO = [self.psA, self.psB]
        psN = [self.psC, self.psD]
        masksb, rbig, rrow = self.masksb, self.rbig, self.rrow
        mrow, gprow, misc_f = self.mrow, self.gprow, self.misc_f

        ml = self.sc_m.load(
            lambda s, c: s.dma_start(out=masksb[:, :] if hasattr(masksb, 'ap') else masksb, in_=self.masks_in[:, :]),
            free=[("bar", BAR)])
        gl0 = self.sc_m.load(
            lambda s, c: s.dma_start(out=gprow[0:1, 0:32], in_=self.GP[0:1, 0:32]),
            free=[("bar", BAR)])
        gl = self.sc_m.load(
            lambda s, c: s.dma_start(out=gprow[0:1, 64:96], in_=self.GP[0:1, 64:96]),
            free=[("bar", BAR)])

        rows = self.rows
        kn2r = rows[0:1, 512:528]
        lnq = rows[0:1, 528:592]
        brow = rows[0:1, 592:656]

        def knmax(v, c):
            return v.tensor_max(kn2r, gprow[0:1, 16:32], gprow[0:1, 80:96])
        w("dve", knmax, waits=[gl0, gl, self.rows_free], inc="dv")
        kn_dv = p.cnt("dv")

        def lns(s, c):
            ch = self.chain
            ch(s, c, s.activation(out=kn2r, in_=kn2r, func=AF.Ln), key="cac")
            return s.activation(out=lnq, in_=mrow[0:1, 0:64], func=AF.Ln)
        w("act", lns, waits=[("dv", kn_dv)], inc="ac")
        ln_ac = p.cnt("ac")

        def bsum(v, c):
            ch = self.chain
            i = None
            for cc4 in range(4):
                i = v.tensor_add(brow[0:1, cc4::4], lnq[0:1, cc4::4],
                                 kn2r[0:1, 0:16])
            return i
        w("dve", bsum, waits=[("ac", ln_ac)], inc="dv")
        bs_dv = p.cnt("dv")

        def bexp(s, c):
            return s.activation(out=brow, in_=brow, func=AF.Exp, scale=0.5)
        w("act", bexp, waits=[("dv", bs_dv)], inc="ac")
        be_ac = p.cnt("ac")

        def bneg(v, c):
            return v.tensor_scalar(brow, brow, -SCALE, None, op0=ALU.mult)
        w("dve", bneg, waits=[("ac", be_ac)], inc="dv")
        br_dv = p.cnt("dv")

        def bmm(t, c):
            return t.matmul(self.psA[:, 0:64], self.onerow_f[:, :],
                            brow, start=True, stop=True)
        w("pe", bmm, waits=[("dv", br_dv), self.acc_rot.rel[0]], inc="pe")

        def bcp(s, c):
            return s.copy(out=misc_f[:, 0:64], in_=self.psA[:, 0:64])
        w("act", bcp, waits=[("pe", p.cnt("pe"))], inc="ac")
        bias_ac = p.cnt("ac")

        s_free = [self.psS1row_free, self.psS1row_free]
        pb_free = [None, None]
        o_free = [("ac", bias_ac), self.acc_rot.rel[1]]
        n_free = [self.stat_free[6], ("dv", self.gam_dv)]
        at_free = [None, None]
        cons = {}
        pending_epi = []
        first_epi = [True]

        for h in range(H):
            lane = h % 2

            def lq(s, c, h=h, lane=lane):
                return s.dma_start(out=qhb[lane],
                                   in_=self.q_dram[h * 128:(h + 1) * 128, :])

            def lk(s, c, h=h, lane=lane):
                return s.dma_start(out=kh[lane],
                                   in_=self.KG[h * 128:(h + 1) * 128, :])

            def lv(s, c, h=h, lane=lane):
                return s.dma_start(
                    out=vh[lane].rearrange("p (a d) -> p a d", d=128),
                    in_=self.VG[:, h * 128:(h + 1) * 128]
                    .rearrange("(a p) d -> p a d", p=128))
            fw = [("bar", BAR)] if h < 2 else [("pe", cons[h - 2])]
            qld = self.sc_q.load(lq, free=fw)
            kld = self.sc_k.load(lk, free=fw)
            vld = self.sc_v.load(lv, free=fw)

            for cch in range(4):
                nkt = 16 * (cch + 1)
                ngrp = nkt // 2
                p_ready_map = {}

                def emit_S(g, cch=cch, lane=lane, h=h, kld=kld, qld=qld):
                    kts = (2 * g, 2 * g + 1)
                    sb_ = self.psS[g % 2]
                    for i_, kt in enumerate(kts):
                        def smm(t, c, sb_=sb_, i_=i_, kt=kt, lane=lane, cch=cch):
                            return t.matmul(
                                sb_[:, i_ * 512:(i_ + 1) * 512],
                                kh[lane][:, kt * 128:(kt + 1) * 128],
                                qhb[lane][:, cch * 512:(cch + 1) * 512],
                                start=True, stop=True)
                        wt_ = [kld, qld, s_free[g % 2]] if i_ == 0 else []
                        w("pe", smm, waits=wt_, inc="pe")
                    smm_pe = p.cnt("pe")

                    def expf(s, c, sb_=sb_, g=g, h=h, cch=cch):
                        return s.activation(out=pbuf[g % 2], in_=sb_[:, :],
                                            func=AF.Exp,
                                            bias=misc_f[:, 4 * h + cch:4 * h + cch + 1],
                                            scale=SCALE)
                    wt_ = [("pe", smm_pe), pb_free[g % 2]]
                    if h == 0 and cch == 0 and g == 0:
                        wt_.append(("ac", bias_ac))
                    w("act", expf, waits=wt_, inc="ac")
                    exp_ac = p.cnt("ac")
                    s_free[g % 2] = ("ac", exp_ac)

                    mk_dv = None
                    for i_, kt in enumerate(kts):
                        beta = kt - 16 * cch
                        if 0 <= beta < 16:
                            def mk(v, c, g=g, i_=i_, beta=beta):
                                sl = pbuf[g % 2][:, i_ * 512:(i_ + 1) * 512]
                                return v.tensor_mul(
                                    sl, sl, masksb[:, beta * 512:(beta + 1) * 512])
                            w("dve", mk, waits=[("ac", exp_ac), ml], inc="dv")
                            mk_dv = p.cnt("dv")
                    p_ready_map[g] = (("dv", mk_dv) if mk_dv is not None
                                      else ("ac", exp_ac))

                def emit_PV(g, cch=cch, lane=lane, vld=vld, ngrp=ngrp):
                    p_ready = p_ready_map[g]
                    kts = (2 * g, 2 * g + 1)
                    for i_, kt in enumerate(kts):
                        last = (g == ngrp - 1 and i_ == 1)

                        def pv(t, c, kt=kt, i_=i_, g=g, lane=lane, cch=cch,
                               last=last):
                            return t.matmul(
                                psO[cch % 2][:, :],
                                vh[lane][:, kt * 128:(kt + 1) * 128],
                                pbuf[g % 2][:, i_ * 512:(i_ + 1) * 512],
                                start=(g == 0 and i_ == 0), stop=last)
                        wt_ = [p_ready] if i_ == 0 else []
                        if g == 0 and i_ == 0:
                            wt_ += [vld, o_free[cch % 2]]
                        w("pe", pv, waits=wt_, inc="pe")
                    for i_, kt in enumerate(kts):
                        last = (g == ngrp - 1 and i_ == 1)

                        def omm(t, c, i_=i_, g=g, cch=cch, last=last):
                            return t.matmul(
                                psN[cch % 2][0:1, 0:512], self.onecol_b[:, :],
                                pbuf[g % 2][:, i_ * 512:(i_ + 1) * 512],
                                start=(g == 0 and i_ == 0), stop=last)
                        wt_ = [n_free[cch % 2]] if (g == 0 and i_ == 0) else []
                        w("pe", omm, waits=wt_, inc="pe")
                    pb_free[g % 2] = ("pe", p.cnt("pe"))

                emit_S(0)
                if ngrp > 1:
                    emit_S(1)
                for g in range(ngrp):
                    if g == 1:
                        while pending_epi:
                            pending_epi.pop(0)()
                    emit_PV(g)
                    if g + 2 < ngrp:
                        emit_S(g + 2)
                last_pe = p.cnt("pe")
                cons[h] = last_pe

                def make_epi(h=h, cch=cch, last_pe=last_pe):
                    def epi():
                        def recip(v, c):
                            ch = self.chain
                            ch(v, c, v.tensor_copy(rrow, psN[cch % 2][0:1, 0:512]))
                            return v.reciprocal(rrow, rrow)
                        wt_ = [("pe", last_pe)]
                        if first_epi[0]:
                            wt_.append(self.rows_free)
                        w("dve", recip, waits=wt_, inc="dv")
                        rec_dv = p.cnt("dv")

                        def rbc(t, c):
                            return t.matmul(psN[cch % 2][:, :], self.onerow_f[:, :],
                                            rrow, start=True, stop=True)
                        w("pe", rbc, waits=[("dv", rec_dv)], inc="pe")

                        def rcp(s, c):
                            return s.copy(out=rbig, in_=psN[cch % 2][:, :])
                        wt_ = [("pe", p.cnt("pe"))]
                        if first_epi[0]:
                            wt_.append(("dv", self.gred_dv[-1]))
                            first_epi[0] = False
                        w("act", rcp, waits=wt_, inc="ac")
                        rb_ac = p.cnt("ac")
                        n_free[cch % 2] = ("ac", rb_ac)

                        al = cch % 2
                        arel = at_free[al]

                        def amul(v, c, al=al, cch=cch):
                            return v.tensor_mul(at_t[al], psO[cch % 2][:, :], rbig)
                        w("dve", amul, waits=[("ac", rb_ac), arel], inc="dv")
                        am_dv = p.cnt("dv")
                        o_free[cch % 2] = ("dv", am_dv)

                        def awr(s_, c, h=h, cch=cch, al=al):
                            return s_.dma_start(
                                out=self.attn_dram[h * 128:(h + 1) * 128,
                                                   cch * 512:(cch + 1) * 512],
                                in_=at_t[al])
                        at_free[al] = self.swrite(awr, ("aw0", "aw1"), al,
                                                  waits=[("dv", am_dv)])
                    return epi
                pending_epi.append(make_epi())
        while pending_epi:
            pending_epi.pop(0)()
        self.attn_done = [("aw0", p.cnt("aw0")), ("aw1", p.cnt("aw1"))]
        self.attn_sfree = list(s_free)
        self.attn_nfree = list(n_free)
        self.attn_ofree = list(o_free)
        self.attn_cons_last = ("pe", p.cnt("pe"))

    def emit_ln(self, src_tiles, tok, in_ready, sq_scr, ab_A, ab_B,
                rX, rQ, row_ps, row_ps_free, scr_free=None, nfeat=E):
        p, w = self.p, self.w
        nsp = tok // 512
        assert nsp <= 2
        sq_rel = [None, None]
        for cc in range(NE):
            lane = cc % 2
            rdy = in_ready[cc]

            def sq(s, c, cc=cc, lane=lane):
                return s.activation(out=sq_scr[lane][:, 0:tok],
                                    in_=src_tiles[cc][:, 0:tok], func=AF.Square)
            wt0 = [rdy, sq_rel[lane]]
            if cc < 2:
                wt0.append(scr_free)
            w("act", sq, waits=wt0, inc="ac")
            sq_ac = p.cnt("ac")
            for sp in range(nsp):
                def mmx(t, c, cc=cc, sp=sp):
                    return t.matmul(row_ps[sp], self.onecol_b[:, :],
                                    src_tiles[cc][:, sp * 512:(sp + 1) * 512],
                                    start=(cc == 0), stop=(cc == NE - 1))
                wt_ = []
                if sp == 0:
                    wt_.append(rdy)
                if cc == 0:
                    wt_.append(row_ps_free[sp])
                w("pe", mmx, waits=wt_, inc="pe")
            for sp in range(nsp):
                def mmq(t, c, cc=cc, sp=sp, lane=lane):
                    return t.matmul(row_ps[2 + sp], self.onecol_b[:, :],
                                    sq_scr[lane][:, sp * 512:(sp + 1) * 512],
                                    start=(cc == 0), stop=(cc == NE - 1))
                wt_ = []
                if sp == 0:
                    wt_.append(("ac", sq_ac))
                if cc == 0:
                    wt_.append(row_ps_free[2 + sp])
                w("pe", mmq, waits=wt_, inc="pe")
            sq_rel[lane] = ("pe", p.cnt("pe"))
        stats_pe = p.cnt("pe")

        def rows_calc(v, c):
            ch = self.chain
            for sp in range(nsp):
                v.tensor_copy(rX[0:1, sp * 512:(sp + 1) * 512], row_ps[sp])
            for sp in range(nsp):
                i = v.tensor_copy(rQ[0:1, sp * 512:(sp + 1) * 512], row_ps[2 + sp])
            ch(v, c, i)
            ch(v, c, v.tensor_scalar(rX, rX, 1.0 / nfeat, None, op0=ALU.mult))
            ch(v, c, v.tensor_scalar(rQ, rQ, 1.0 / nfeat, None, op0=ALU.mult))
            ch(v, c, v.tensor_mul(self.rowM[0:1, 0:tok], rX, rX))
            ch(v, c, v.tensor_sub(rQ, rQ, self.rowM[0:1, 0:tok]))
            return v.tensor_scalar(rQ, rQ, LN_EPS, None, op0=ALU.add)
        w("dve", rows_calc, waits=[("pe", stats_pe), self.rowm_free], inc="dv")
        rc_dv = p.cnt("dv")

        def rows_ln(s, c):
            ch = self.chain
            ch(s, c, s.activation(out=rQ, in_=rQ, func=AF.Ln), key="cac")
            return s.activation(out=rQ, in_=rQ, func=AF.Exp, scale=-0.5)
        w("act", rows_ln, waits=[("dv", rc_dv)], inc="ac")
        rstd_ac = p.cnt("ac")

        def rows_fin(v, c):
            ch = self.chain
            ch(v, c, v.tensor_mul(rX, rX, rQ))
            return v.tensor_scalar(rX, rX, -1.0, None, op0=ALU.mult)
        w("dve", rows_fin, waits=[("ac", rstd_ac)], inc="dv")
        rows_dv = p.cnt("dv")

        ab_ac = None
        bc_ps = row_ps[0].tensor if False else None
        for src_row, dst in ((rQ, ab_A), (rX, ab_B)):
            for sp in range(nsp):
                def bmm(t, c, src_row=src_row, sp=sp):
                    return t.matmul(self.bcast_bank[:, 0:512], self.onerow_f[:, :],
                                    src_row[0:1, sp * 512:(sp + 1) * 512],
                                    start=True, stop=True)
                wt_ = [("dv", rows_dv)]
                if ab_ac is not None:
                    wt_.append(("ac", ab_ac))
                else:
                    wt_.append(self.bcast_bank_free)
                w("pe", bmm, waits=wt_, inc="pe")

                def bcp(s, c, dst=dst, sp=sp):
                    return s.copy(out=dst[:, sp * 512:(sp + 1) * 512],
                                  in_=self.bcast_bank[:, 0:512])
                wt2 = [("pe", p.cnt("pe"))]
                if ab_ac is None:
                    wt2.append(scr_free)
                w("act", bcp, waits=wt2, inc="ac")
                ab_ac = p.cnt("ac")
        self.bcast_bank_free = ("ac", ab_ac)
        self.rowm_free = ("dv", rc_dv)
        return ab_ac, ("pe", p.cnt("pe"))

    def wo_phase(self):
        p, w, sems = self.p, self.w, self.sems
        Bb, C = self.Bb, self.C
        worhs4 = [Bb[:, i * 8192:(i + 1) * 8192] for i in range(4)]
        wt3 = [Bb[:, 32768 + i * 2048:32768 + (i + 1) * 2048] for i in range(2)]
        evo = [Bb[:, 36864 + i * 512:36864 + (i + 1) * 512] for i in range(2)]
        asq3 = [Bb[:, 37888 + i * 512:37888 + (i + 1) * 512] for i in range(2)]
        ev2 = [Bb[:, i * 2048:(i + 1) * 2048] for i in range(2)]
        ab3_A = Bb[:, 40960 + 2048 * 0:43008]
        ab3_B = Bb[:, 43008:45056]
        xre = C[:, 0:2048]
        xmo = C[:, 2048:4096]
        rowX = self.rows[0:1, 0:2048]
        rowQ = self.rows[0:1, 2048:4096]

        ad = self.attn_done
        rhs_ld = []
        for t4 in range(4):
            def lr(s, c, t4=t4):
                return s.dma_start(
                    out=worhs4[t4].rearrange("p (c t) -> p c t", t=512),
                    in_=self.attn_dram[:, t4 * 512:(t4 + 1) * 512]
                    .rearrange("(c p) t -> p c t", p=128))
            fw = (list(ad) if t4 == 0 else []) + [self.attn_cons_last]
            rhs_ld.append(self.sc_m.load(lr, free=fw))

        def zrows(v, c):
            ch = self.chain
            ch(v, c, v.memset(rowX, 0.0))
            return v.memset(rowQ, 0.0)
        w("dve", zrows, waits=[self.attn_cons_last], inc="dv")
        rz_dv = p.cnt("dv")

        wo_acc = Rot(2)
        wo_acc.rel = [self.attn_ofree[0], self.attn_ofree[1]]
        evo_rot = Rot(2)
        asq_rot = Rot(2)
        ev_wr = {}
        prow_free = [self.attn_nfree[0], self.attn_nfree[1]]
        acc_dv = rz_dv
        for fc in range(NE):
            wl_lane = fc % 2
            wl_rel = self.cell.get(("wt3_rel", wl_lane))

            def ldw(g, c, fc=fc, wl_lane=wl_lane):
                return g.dma_start(
                    out=wt3[wl_lane].rearrange("p (c f) -> p c f", f=128),
                    in_=self.woT_t[fc].rearrange("(c p) f -> p c f", p=128))
            wl = self.gp_st.load(ldw, free=[wl_rel])

            for t4 in range(4):
                a_lane, a_rel = wo_acc.acquire()
                acc = self.ps_acc[a_lane]
                for hd in range(NE):
                    def mm(t, c, acc=acc, wl_lane=wl_lane, hd=hd, t4=t4):
                        return t.matmul(
                            acc[:, :], wt3[wl_lane][:, hd * 128:(hd + 1) * 128],
                            worhs4[t4][:, hd * 512:(hd + 1) * 512],
                            start=(hd == 0), stop=(hd == NE - 1))
                    wt_ = []
                    if hd == 0:
                        wt_ = [wl, a_rel]
                        if fc == 0:
                            wt_.append(rhs_ld[t4])
                    w("pe", mm, waits=wt_, inc="pe")
                mm_pe = p.cnt("pe")

                e_lane, e_rel = evo_rot.acquire()
                e_rel_dv = self.cell.get(("evo_dv", e_lane))

                def ev(s, c, acc=acc, e_lane=e_lane):
                    return s.copy(out=evo[e_lane], in_=acc[:, :])
                w("act", ev, waits=[("pe", mm_pe), e_rel, e_rel_dv], inc="ac")
                ev_ac = p.cnt("ac")
                wo_acc.release(a_lane, "ac", ev_ac)

                q_lane, q_rel = asq_rot.acquire()

                def sq(s, c, e_lane=e_lane, q_lane=q_lane):
                    return s.activation(out=asq3[q_lane], in_=evo[e_lane],
                                        func=AF.Square)
                w("act", sq, waits=[q_rel], inc="ac")
                sq_ac = p.cnt("ac")

                def mmx(t, c, e_lane=e_lane):
                    return t.matmul(self.psC[0:1, 0:512], self.onecol_b[:, :],
                                    evo[e_lane], start=True, stop=True)
                w("pe", mmx, waits=[("ac", ev_ac), prow_free[0]], inc="pe")
                prow_free[0] = None

                def mmq(t, c, q_lane=q_lane):
                    return t.matmul(self.psD[0:1, 0:512], self.onecol_b[:, :],
                                    asq3[q_lane], start=True, stop=True)
                w("pe", mmq, waits=[("ac", sq_ac), prow_free[1]], inc="pe")
                prow_free[1] = None
                rows_pe = p.cnt("pe")

                def racc(v, c, t4=t4):
                    ch = self.chain
                    sl = slice(t4 * 512, (t4 + 1) * 512)
                    ch(v, c, v.tensor_add(rowX[0:1, sl], rowX[0:1, sl],
                                          self.psC[0:1, 0:512]))
                    return v.tensor_add(rowQ[0:1, sl], rowQ[0:1, sl],
                                        self.psD[0:1, 0:512])
                w("dve", racc, waits=[("pe", rows_pe), ("dv", acc_dv)], inc="dv")
                acc_dv = p.cnt("dv")
                asq_rot.release(q_lane, "dv", acc_dv)
                prow_free = [("dv", acc_dv), ("dv", acc_dv)]
                self.cell[("evo_dv", e_lane)] = ("dv", acc_dv)

                def awr(s_, c, fc=fc, t4=t4, e_lane=e_lane):
                    return s_.dma_start(
                        out=self.attn2_dram[fc * 128:(fc + 1) * 128,
                                            t4 * 512:(t4 + 1) * 512],
                        in_=evo[e_lane])
                val = self.swrite(awr, ("aw0", "aw1"), e_lane,
                                  waits=[("ac", ev_ac)])
                evo_rot.release(e_lane, *val)
            self.cell[("wt3_rel", wl_lane)] = ("pe", p.cnt("pe"))
        a2_done = [("aw0", p.cnt("aw0")), ("aw1", p.cnt("aw1"))]
        stats_dv = acc_dv

        def rows_calc(v, c):
            ch = self.chain
            ch(v, c, v.tensor_scalar(rowX, rowX, 1.0 / E, None, op0=ALU.mult))
            ch(v, c, v.tensor_scalar(rowQ, rowQ, 1.0 / E, None, op0=ALU.mult))
            ch(v, c, v.tensor_mul(self.rowM, rowX, rowX))
            ch(v, c, v.tensor_sub(rowQ, rowQ, self.rowM))
            return v.tensor_scalar(rowQ, rowQ, LN_EPS, None, op0=ALU.add)
        w("dve", rows_calc, waits=[("dv", stats_dv)], inc="dv")

        def rows_ln(s, c):
            ch = self.chain
            ch(s, c, s.activation(out=rowQ, in_=rowQ, func=AF.Ln), key="cac")
            return s.activation(out=rowQ, in_=rowQ, func=AF.Exp, scale=-0.5)
        w("act", rows_ln, waits=[("dv", p.cnt("dv"))], inc="ac")

        def rows_fin(v, c):
            ch = self.chain
            ch(v, c, v.tensor_mul(rowX, rowX, rowQ))
            return v.tensor_scalar(rowX, rowX, -1.0, None, op0=ALU.mult)
        w("dve", rows_fin, waits=[("ac", p.cnt("ac"))], inc="dv")
        rows_dv = p.cnt("dv")

        ab_ac = None
        for src_row, dst in ((rowQ, ab3_A), (rowX, ab3_B)):
            for sp in range(4):
                def bmm(t, c, src_row=src_row, sp=sp):
                    return t.matmul(self.psS0[:, 0:512], self.onerow_f[:, :],
                                    src_row[0:1, sp * 512:(sp + 1) * 512],
                                    start=True, stop=True)
                wt_ = [("dv", rows_dv)]
                wt_.append(("ac", ab_ac) if ab_ac is not None else self.attn_sfree[0])
                w("pe", bmm, waits=wt_, inc="pe")

                def bcp(s, c, dst=dst, sp=sp):
                    return s.copy(out=dst[:, sp * 512:(sp + 1) * 512],
                                  in_=self.psS0[:, 0:512])
                w("act", bcp, waits=[("pe", p.cnt("pe"))], inc="ac")
                ab_ac = p.cnt("ac")
        ab3_ac = ab_ac
        self.bcast_s0_free = ("ac", ab_ac)

        xre_free = None
        ev2_rot = Rot(2)
        xmo_free = None
        for fc in range(NE):
            e_lane, e_rel = ev2_rot.acquire()

            def lda(s, c, fc=fc, e_lane=e_lane):
                return s.dma_start(out=ev2[e_lane],
                                   in_=self.attn2_dram[fc * 128:(fc + 1) * 128, :])
            fw = [e_rel]
            if fc < 2:
                fw += a2_done
            al = self.sc_v.load(lda, free=fw)

            def ldx(s, c, fc=fc):
                return s.dma_start(out=xre[:, :],
                                   in_=self.x_in[fc * 128:(fc + 1) * 128, 0:TQ])
            xl = self.sc_k.load(ldx, free=[xre_free])

            def xmcalc(v, c, fc=fc, e_lane=e_lane):
                ch = self.chain
                ch(v, c, v.tensor_mul(xmo, ev2[e_lane], ab3_A))
                ch(v, c, v.tensor_add(xmo, xmo, ab3_B))
                return v.tensor_add(xmo, xmo, xre)
            w("dve", xmcalc, waits=[al, xl, ("ac", ab3_ac), xmo_free], inc="dv")
            xm_dv = p.cnt("dv")
            ev2_rot.release(e_lane, "dv", xm_dv)
            xre_free = ("dv", xm_dv)

            def xwr(s_, c, fc=fc):
                return s_.dma_start(
                    out=self.xmid_dram[fc * 128:(fc + 1) * 128, :], in_=xmo)
            xmo_free = self.swrite(xwr, ("xw0", "xw1"), fc % 2,
                                   waits=[("dv", xm_dv)])
        self.xmid_done = [("xw0", p.cnt("xw0")), ("xw1", p.cnt("xw1"))]
        self.wo_rel = list(wo_acc.rel)
        self.wo_psC_free = prow_free[0]
        self.wo_psD_free = prow_free[1]
        self.wo_worhs_users = ("pe", p.cnt("pe"))
        self.rows_free = ("pe", p.cnt("pe"))

    def debug_dump3(self):
        p, sems = self.p, self.sems
        dbg = self.dbg
        xd = self.xmid_done

        def d3(s_, c):
            for s, v in xd:
                s_.wait_ge(sems[s], v)
            s_.dma_start(out=dbg["xm"][:, :],
                         in_=self.xmid_dram[:, :]).then_inc(sems["ow"], 16)
            s_.wait_ge(sems["ow"], 16)
            return None
        p.prog["sp"].append(((), d3, None, 0))

    def debug_dump2(self):
        p, sems = self.p, self.sems
        dbg = self.dbg
        ad = self.attn_done

        def d2(s_, c):
            for s, v in ad:
                s_.wait_ge(sems[s], v)
            s_.dma_start(out=dbg["at"][:, :],
                         in_=self.attn_dram[:, :]).then_inc(sems["ow"], 16)
            s_.wait_ge(sems["ow"], 16)
            return None
        p.prog["sp"].append(((), d2, None, 0))

    def ffn_phase(self):
        p, w, sems = self.p, self.w, self.sems
        Bb, C, Eb, Fb = self.Bb, self.C, self.Eb, self.Fb
        xn2, y3b, hidv = self.xn2, self.y3b, self.hidv
        qw1v, qw2v, qw3v = self.qw1v, self.qw2v, self.qw3v
        raw1, raw2, sig4 = self.raw1, self.raw2, self.sig4
        misc_f = self.misc_f
        row4X, row4Q = self.row4X, self.row4Q
        self.rowM = C[0:1, 2048:4096]
        self.rowm_free = None
        self.bcast_bank = self.psS0[:, 0:512]
        self.bcast_bank_free = self.bcast_s0_free

        def thrc(v, c):
            ch = self.chain
            m = self.mrow
            ch(v, c, v.tensor_add(m[0:1, 88:91], self.gprow[0:1, 0:3],
                                  self.gprow[0:1, 64:67]))
            ch(v, c, v.tensor_scalar(m[0:1, 88:91], m[0:1, 88:91], 1.0 / QCNT,
                                     None, op0=ALU.mult))
            ch(v, c, v.tensor_scalar(m[0:1, 88:91], m[0:1, 88:91], Q_EPS, 0.5,
                                     op0=ALU.add, op1=ALU.mult))
            return v.tensor_scalar(m[0:1, 92:95], m[0:1, 88:91], -1.0, None,
                                   op0=ALU.mult)
        w("dve", thrc, waits=[("dv", self.gam_dv)], inc="dv")
        thr_dv = p.cnt("dv")

        def thrmm(t, c):
            return t.matmul(self.psS1[:, 0:8], self.onerow_f[:, :],
                            self.mrow[0:1, 88:96], start=True, stop=True)
        w("pe", thrmm, waits=[("dv", thr_dv), self.attn_sfree[1]], inc="pe")
        thr_pe = p.cnt("pe")

        def thrcp(s, c):
            return s.copy(out=misc_f[:, 16:24], in_=self.psS1[:, 0:8])
        w("act", thrcp, waits=[("pe", thr_pe)], inc="ac")
        thr_ac = p.cnt("ac")
        thr_ap = [misc_f[:, 16 + i:17 + i] for i in range(3)]
        nthr_ap = [misc_f[:, 20 + i:21 + i] for i in range(3)]
        psS1_free = ("ac", thr_ac)

        y_acc = Rot(2)
        y_acc.rel = list(self.wo_rel)
        y2_rel = [self.wo_psC_free, self.wo_psD_free]
        qwst = {}
        hf_state = {}

        for hf in range(2):
            hcols = slice(hf * 1024, (hf + 1) * 1024)

            x2ld = []
            for cc in range(NE):
                def ldm(g, c, cc=cc, hcols=hcols):
                    return g.dma_start(
                        out=xn2[cc],
                        in_=self.xmid_dram[cc * 128:(cc + 1) * 128, hcols])
                if hf == 0:
                    fw = list(self.xmid_done) + [self.attn_cons_last] \
                        if cc == 0 else []
                else:
                    fw = [("pe", hf_state[0]["xn2_users"])] if cc == 0 else []
                x2ld.append(self.gp_st.load(ldm, free=fw))
            row_ps = [self.psC[0:1, 0:512], self.psD[0:1, 0:512],
                      self.psS1[0:1, 0:512], self.psS1[0:1, 512:1024]]
            if hf == 0:
                row_ps_free = [y2_rel[0], y2_rel[1], psS1_free, psS1_free]
                scr_free = self.wo_worhs_users
            else:
                hd0 = hf_state[0]
                row_ps_free = [hd0["y2_free"][0], hd0["y2_free"][1],
                               hd0["psS1_free"], hd0["psS1_free"]]
                scr_free = hd0["ocalc_done"]
            self.rowm_free = ("dv", p.cnt("dv"))
            ab_ac, _bc_pe = self.emit_ln(
                xn2, 1024, x2ld, [hidv[0], hidv[1]], hidv[2], hidv[3],
                row4X, row4Q, row_ps, row_ps_free, scr_free=scr_free)
            psS1_free = ("dv", self.rowm_free[1])
            ap_dv = []
            for cc in range(NE):
                lane = cc % 2

                def ap1(v, c, cc=cc, lane=lane):
                    return v.tensor_mul(hidv[4 + lane], xn2[cc], hidv[2])

                def ap2(v, c, cc=cc, lane=lane):
                    return v.tensor_add(xn2[cc], hidv[4 + lane], hidv[3])
                wt_ = [("ac", ab_ac)]
                if cc < 2:
                    wt_.append(scr_free)
                if cc >= 2:
                    wt_.append(("dv", ap_dv[cc - 2]))
                w("dve", ap1, waits=wt_, inc="dv")
                w("dve", ap2, waits=[("dv", p.cnt("dv"))], inc="dv")
                ap_dv.append(p.cnt("dv"))
            xn2_dv = p.cnt("dv")
            if self.debug == 4 and hf == 0:
                for cc in range(NE):
                    def dxw(s_, c, cc=cc):
                        return s_.dma_start(
                            out=self.dbg["xn2"][cc * 128:(cc + 1) * 128, :],
                            in_=xn2[cc]).then_inc(sems["ow"], 16) and None
                    p.bump("ow", 16)
                    p.prog["sp"].append(((("dv", xn2_dv),), dxw, None, 0))

            first_hm = True
            for hc in range(NH):
                if hf == 0:
                    def ld1(s, c, hc=hc):
                        return s.dma_start(
                            out=raw1[:, :].rearrange("p (c f) -> p c f", f=128),
                            in_=self.w1T_t[hc].rearrange("(c p) f -> p c f", p=128))

                    def ld2(s, c, hc=hc):
                        return s.dma_start(
                            out=raw2[:, :].rearrange("p (c f) -> p c f", f=128),
                            in_=self.w2T_t[hc].rearrange("(c p) f -> p c f", p=128))
                    fw1 = [qwst.get(("qz1",))]
                    fw2 = [qwst.get(("qz2",))]
                    if hc == 0:
                        fw1 += list(self.xmid_done) + [("dv", self.rowm_free[1])]
                        fw2 += list(self.xmid_done) + [("dv", self.rowm_free[1])]
                    l1 = self.sc_k.load(ld1, free=fw1)
                    l2 = self.sc_v.load(ld2, free=fw2)

                    def qz1(v, c):
                        ch = self.chain
                        ch(v, c, v.tensor_scalar(Fb[:, 0:2048], raw1, nthr_ap[0],
                                                 None, op0=ALU.is_le))
                        return v.scalar_tensor_tensor(
                            qw1v, raw1, thr_ap[0], Fb[:, 0:2048],
                            op0=ALU.is_ge, op1=ALU.subtract)

                    def qz2(v, c):
                        ch = self.chain
                        ch(v, c, v.tensor_scalar(Fb[:, 2048:4096], raw2, nthr_ap[1],
                                                 None, op0=ALU.is_le))
                        return v.scalar_tensor_tensor(
                            qw2v, raw2, thr_ap[1], Fb[:, 2048:4096],
                            op0=ALU.is_ge, op1=ALU.subtract)
                    wt1 = [l1, ("ac", thr_ac), qwst.get(("mm1",)),
                           qwst.get(("wr1",))]
                    wt2 = [l2, qwst.get(("mm2",)), qwst.get(("wr2",))]
                    w("dve", qz1, waits=wt1, inc="dv")
                    q1_dv = p.cnt("dv")
                    w("dve", qz2, waits=wt2, inc="dv")
                    q2_dv = p.cnt("dv")

                    def w1wr(s_, c, hc=hc):
                        return s_.dma_start(out=self.qw1_dram[hc], in_=qw1v)

                    def w2wr(s_, c, hc=hc):
                        return s_.dma_start(out=self.qw2_dram[hc], in_=qw2v)
                    qwst[("wr1",)] = self.swrite(w1wr, ("ww0", "ww1"), 0,
                                                 waits=[("dv", q1_dv)])
                    qwst[("wr2",)] = self.swrite(w2wr, ("ww0", "ww1"), 1,
                                                 waits=[("dv", q2_dv)])
                    qw_rdy1 = ("dv", q1_dv)
                    qw_rdy2 = ("dv", q2_dv)
                else:
                    def lq1(s, c, hc=hc):
                        return s.dma_start(out=qw1v, in_=self.qw1_dram[hc])

                    def lq2(s, c, hc=hc):
                        return s.dma_start(out=qw2v, in_=self.qw2_dram[hc])
                    fw1 = [qwst.get(("mm1",))]
                    fw2 = [qwst.get(("mm2",))]
                    if hc == 0:
                        fw1 += [("ww0", p.cnt("ww0")), ("ww1", p.cnt("ww1"))]
                        fw2 += [("ww0", p.cnt("ww0")), ("ww1", p.cnt("ww1"))]
                    qw_rdy1 = self.sc_k.load(lq1, free=fw1)
                    qw_rdy2 = self.sc_v.load(lq2, free=fw2)

                for ts_ in range(2):
                    a_lane, a_rel = y_acc.acquire()
                    acc1 = self.ps_acc[a_lane]
                    acc2 = [self.psC, self.psD][a_lane]
                    for e_ in range(NE):
                        def m1(t, c, acc1=acc1, e_=e_, ts_=ts_):
                            return t.matmul(
                                acc1[:, :], qw1v[:, e_ * 128:(e_ + 1) * 128],
                                xn2[e_][:, ts_ * 512:(ts_ + 1) * 512],
                                start=(e_ == 0), stop=(e_ == NE - 1))
                        wt_ = [qw_rdy1, ("dv", xn2_dv), a_rel] if e_ == 0 else []
                        w("pe", m1, waits=wt_, inc="pe")
                    for e_ in range(NE):
                        def m2(t, c, acc2=acc2, e_=e_, ts_=ts_):
                            return t.matmul(
                                acc2[:, :], qw2v[:, e_ * 128:(e_ + 1) * 128],
                                xn2[e_][:, ts_ * 512:(ts_ + 1) * 512],
                                start=(e_ == 0), stop=(e_ == NE - 1))
                        wt_ = [qw_rdy2, y2_rel[a_lane]] if e_ == 0 else []
                        y2_rel[a_lane] = None
                        w("pe", m2, waits=wt_, inc="pe")
                    mm_pe = p.cnt("pe")

                    def sg(s, c, acc1=acc1, ts_=ts_):
                        return s.activation(out=sig4[:, ts_ * 512:(ts_ + 1) * 512],
                                            in_=acc1[:, :], func=AF.Sigmoid)
                    wt_ = [("pe", mm_pe), qwst.get(("hidmul", a_lane))]
                    w("act", sg, waits=wt_, inc="ac")
                    sg_ac = p.cnt("ac")

                    def hm(v, c, acc2=acc2, hc=hc, ts_=ts_):
                        sl = slice(ts_ * 512, (ts_ + 1) * 512)
                        return v.tensor_mul(hidv[hc][:, sl], sig4[:, sl],
                                            acc2[:, :])
                    wt_ = [("ac", sg_ac)]
                    if first_hm:
                        wt_ += [self.wo_worhs_users, ("ac", ab_ac)]
                        if hf == 1:
                            wt_.append(hf_state[0]["y3_users"])
                        first_hm = False
                    w("dve", hm, waits=wt_, inc="dv")
                    rel = ("dv", p.cnt("dv"))
                    y_acc.release(a_lane, *rel)
                    qwst[("hidmul", a_lane)] = rel
                if ts_ == 1:
                    qwst[("mm1",)] = ("pe", mm_pe)
                    qwst[("mm2",)] = ("pe", mm_pe)
            hid_dv = p.cnt("dv")
            hf_state[hf] = {"xn2_users": p.cnt("pe"),
                            "y2_free": [qwst.get(("hidmul", 0)),
                                        qwst.get(("hidmul", 1))],
                            "psS1_free": psS1_free}

            pieces = ((0, 2048), (2048, 2048), (4096, 1536))
            first_ev3 = True
            for fc in range(NE):
                if hf == 0:
                    prev_mm3 = qwst.get(("mm3",))
                    for pi, (s0, ln) in enumerate(pieces):
                        def ld3(s, c, fc=fc, s0=s0, ln=ln):
                            return s.dma_start(
                                out=raw1[:, 0:ln].rearrange(
                                    "p (c f) -> p c f", f=128),
                                in_=self.w3T_t[fc].rearrange(
                                    "(c p) f -> p c f", p=128)
                                [:, s0 // 128:(s0 + ln) // 128, :])
                        fw = [qwst.get(("qz3",))]
                        if fc == 0 and pi == 0:
                            fw.append(qwst.get(("mm1",)))
                        l3 = self.sc_q.load(ld3, free=fw)

                        def qz3(v, c, s0=s0, ln=ln):
                            ch = self.chain
                            ch(v, c, v.tensor_scalar(Eb[:, 0:ln], raw1[:, 0:ln],
                                                     nthr_ap[2], None,
                                                     op0=ALU.is_le))
                            return v.scalar_tensor_tensor(
                                qw3v[:, s0:s0 + ln], raw1[:, 0:ln], thr_ap[2],
                                Eb[:, 0:ln], op0=ALU.is_ge, op1=ALU.subtract)
                        wt_ = [l3]
                        if pi == 0:
                            wt_ += [prev_mm3, qwst.get(("wr3",)),
                                    qwst.get(("lq12",))]
                        w("dve", qz3, waits=wt_, inc="dv")
                        qwst[("qz3",)] = ("dv", p.cnt("dv"))
                    q3_dv = p.cnt("dv")

                    def w3wr(s_, c, fc=fc):
                        return s_.dma_start(out=self.qw3_dram[fc], in_=qw3v)
                    qwst[("wr3",)] = self.swrite(w3wr, ("ww0", "ww1"), fc % 2,
                                                 waits=[("dv", q3_dv)])
                    qw3_rdy = ("dv", q3_dv)
                else:
                    def lq3(s, c, fc=fc):
                        return s.dma_start(out=qw3v, in_=self.qw3_dram[fc])
                    fw = [qwst.get(("mm3",))]
                    if fc == 0:
                        fw += [("ww0", p.cnt("ww0")), ("ww1", p.cnt("ww1"))]
                    qw3_rdy = self.sc_q.load(lq3, free=fw)

                for ts_ in range(2):
                    a_lane, a_rel = y_acc.acquire()
                    acc = self.ps_acc[a_lane]
                    for hc in range(NH):
                        def m3(t, c, acc=acc, hc=hc, ts_=ts_):
                            return t.matmul(
                                acc[:, :], qw3v[:, hc * 128:(hc + 1) * 128],
                                hidv[hc][:, ts_ * 512:(ts_ + 1) * 512],
                                start=(hc == 0), stop=(hc == NH - 1))
                        wt_ = [qw3_rdy, ("dv", hid_dv), a_rel] if hc == 0 else []
                        w("pe", m3, waits=wt_, inc="pe")
                    mm_pe = p.cnt("pe")
                    qwst[("mm3",)] = ("pe", mm_pe)

                    def ev3(s, c, acc=acc, fc=fc, ts_=ts_):
                        return s.copy(out=y3b[fc][:, ts_ * 512:(ts_ + 1) * 512],
                                      in_=acc[:, :])
                    wt_ = [("pe", mm_pe)]
                    if first_ev3:
                        if hf == 0:
                            wt_.append(self.attn_cons_last)
                        else:
                            wt_.append(hf_state[0]["ocalc_done"])
                        first_ev3 = False
                    w("act", ev3, waits=wt_, inc="ac")
                    y_acc.release(a_lane, "ac", p.cnt("ac"))
            y3_ac = p.cnt("ac")
            hf_state[hf]["y3_users"] = ("pe", p.cnt("pe"))
            if self.debug == 4 and hf == 0:
                for cc in range(NE):
                    def dyw(s_, c, cc=cc):
                        return s_.dma_start(
                            out=self.dbg["y3"][cc * 128:(cc + 1) * 128, :],
                            in_=y3b[cc]).then_inc(sems["ow"], 16) and None
                    p.bump("ow", 16)
                    p.prog["sp"].append(((("ac", y3_ac),), dyw, None, 0))

                def dsw(s_, c):
                    s_.dma_start(out=self.dbg["sig"][:, :],
                                 in_=sig4[:, :]).then_inc(sems["ow"], 16)
                    s_.dma_start(out=self.dbg["hid"][:, :],
                                 in_=hidv[NH - 1][:, :]).then_inc(sems["ow"], 16)
                    return None
                p.bump("ow", 32)
                p.prog["sp"].append(((("ac", y3_ac),), dsw, None, 0))

            row_ps = [self.psC[0:1, 0:512], self.psD[0:1, 0:512],
                      self.psS1[0:1, 0:512], self.psS1[0:1, 512:1024]]
            row_ps_free = [hf_state[hf]["y2_free"][0],
                           hf_state[hf]["y2_free"][1],
                           psS1_free, psS1_free]
            in_rdy = [("ac", y3_ac)] * NE
            self.rowm_free = ("dv", p.cnt("dv"))
            abf_ac, _ = self.emit_ln(
                y3b, 1024, in_rdy, [hidv[4], hidv[5]], hidv[0], hidv[1],
                row4X, row4Q, row_ps, row_ps_free,
                scr_free=("ac", y3_ac))
            psS1_free = ("dv", self.rowm_free[1])

            out_free = None
            ld_free = None
            last_q_dv = ("dv", p.cnt("dv"))
            for cc in range(NE):
                def ldm2(s, c, cc=cc, hcols=hcols):
                    return s.dma_start(
                        out=raw1[:, 0:1024],
                        in_=self.xmid_dram[cc * 128:(cc + 1) * 128, hcols])
                fw = [ld_free]
                if cc == 0 and hf == 0:
                    fw.append(qwst.get(("qz3",)))
                if cc == 0:
                    fw.append(("dv", self.rowm_free[1]))
                xl = self.sc_m.load(ldm2, free=fw)

                def ocalc(v, c, cc=cc):
                    ch = self.chain
                    ch(v, c, v.tensor_mul(raw2[:, 0:1024], y3b[cc], hidv[0]))
                    ch(v, c, v.tensor_add(raw2[:, 0:1024], raw2[:, 0:1024],
                                          hidv[1]))
                    return v.tensor_add(raw2[:, 0:1024], raw2[:, 0:1024],
                                        raw1[:, 0:1024])
                w("dve", ocalc, waits=[xl, ("ac", abf_ac), out_free], inc="dv")
                oc_dv = p.cnt("dv")

                def owr(s_, c, cc=cc, hcols=hcols):
                    return s_.dma_start(
                        out=self.out_d[cc * 128:(cc + 1) * 128, hcols],
                        in_=raw2[:, 0:1024])
                val = self.swrite(owr, ("ow0", "ow1"), cc % 2,
                                  waits=[("dv", oc_dv)])
                out_free = val
                ld_free = ("dv", oc_dv)
            hf_state[hf]["ocalc_done"] = ("dv", p.cnt("dv"))
            if hf == 0:
                psCD_free = ("dv", self.rowm_free[1])
                qwst[("mm1",)] = ("pe", hf_state[0]["xn2_users"])
                qwst[("mm2",)] = ("pe", hf_state[0]["xn2_users"])
                qwst[("mm3",)] = hf_state[0]["y3_users"]
                qwst[("qz3",)] = None
                qwst[("wr1",)] = qwst[("wr2",)] = qwst[("wr3",)] = None
                qwst[("hidmul", 0)] = psCD_free
                qwst[("hidmul", 1)] = psCD_free
                y2_rel = [psCD_free, psCD_free]
                hf_state[0]["y2_free"] = [psCD_free, psCD_free]

        ow_tot = [("ow0", p.cnt("ow0")), ("ow1", p.cnt("ow1"))]

        def fin(s_, c):
            for s, v in ow_tot:
                s_.wait_ge(sems[s], v)
            return None
        p.prog["sp"].append(((), fin, None, 0))

    def debug_dump1(self):
        p, sems = self.p, self.sems
        dbg = self.dbg

        def d1(s_, c):
            s_.dma_start(out=dbg["kg"][:, :], in_=self.KG[:, :]).then_inc(sems["ow"], 16)
            s_.dma_start(out=dbg["vg"][:, :], in_=self.VG[:, :]).then_inc(sems["ow"], 16)
            s_.dma_start(out=dbg["gp"][0:1, 0:32], in_=self.GP[0:1, 0:32]).then_inc(sems["ow"], 16)
            s_.dma_start(out=dbg["gp"][0:1, 64:96], in_=self.GP[0:1, 64:96]).then_inc(sems["ow"], 16)
            s_.dma_start(out=dbg["q"][:, :], in_=self.q_dram[:, :]).then_inc(sems["ow"], 16)
            s_.wait_ge(sems["ow"], 80)
            return None
        p.prog["sp"].append(((("bar", self.BAR),), d1, None, 0))

    def finish(self):
        nc, p, sems, cell = self.nc, self.p, self.sems, self.cell
        cid_ap = self.cidt[0:1, 0:1]
        with nc.Block() as block:
            @block.tensor
            def _(t):
                run_prog(t, p.prog["pe"], sems, cell)

            @block.scalar
            def _(s_):
                with s_.register("a_par") as rp:
                    cell["a_par"] = rp
                    s_.wait_ge(sems["gl0"], self.cid_ld)
                    s_.reg_load(rp, cid_ap)
                    s_.reg_alu(rp, rp, 1, ALU.arith_shift_right)
                    run_prog(s_, p.prog["act"], sems, cell)

            @block.vector
            def _(v):
                run_prog(v, p.prog["dve"], sems, cell)

            @block.gpsimd
            def _(g):
                run_prog(g, p.prog["gp"], sems, cell)

            @block.sync
            def _(s_):
                with s_.register("s_cls") as rc:
                    cell["s_cls"] = rc
                    s_.wait_ge(sems["gl0"], self.cid_ld)
                    s_.reg_load(rc, cid_ap)
                    run_prog(s_, p.prog["sp"], sems, cell)
        self.st.close()
        self.verify_sems()

    def verify_sems(self):
        import re
        from collections import defaultdict
        totals = defaultdict(int)
        pat = re.compile(r"update:S\[(\w+)\]\+[+=]?(\d+)")
        for inst in self.nc.all_instructions():
            for m in pat.finditer(str(inst)):
                totals[m.group(1)] += int(m.group(2))
        bad = []
        for name, expect in self.p.sem_static.items():
            actual = totals.get(name, 0)
            if actual != expect and name not in ("poll",):
                bad.append((name, expect, actual))
        if bad:
            raise AssertionError(f"sem count mismatches (plan vs built): {bad}")


def run_prog(eng, prog, sems, cell):
    for waits, fn, inc, n in prog:
        for s, v in waits:
            eng.wait_ge(sems[s], v)
        instr = fn(eng, cell)
        if inc is not None and instr is not None:
            instr.then_inc(sems[inc], n)



def build_nc(debug=0):
    mk = MK(debug)
    mk.phase0()
    mk.phase1()
    if debug == 1:
        mk.debug_dump1()
        mk.finish()
        return mk.nc, mk.dbg
    mk.attn_phase()
    if debug == 2:
        mk.debug_dump2()
        mk.finish()
        return mk.nc, mk.dbg
    mk.wo_phase()
    if debug == 3:
        mk.debug_dump3()
        mk.finish()
        return mk.nc, mk.dbg
    mk.ffn_phase()
    mk.finish()
    return mk.nc, mk.dbg



def _tile_w(wT):
    ein, f = wT.shape
    return np.ascontiguousarray(wT.reshape(ein, f // 128, 128).transpose(1, 0, 2))


def _rope_tables():
    inv = (1.0 / (np.float32(ROPE_BASE) **
                  (np.arange(0, D, 2, dtype=np.float32) / np.float32(D)))).astype(np.float32)
    t = np.arange(T, dtype=np.float32)
    f = t[:, None] * inv[None, :]
    cos = np.concatenate([np.cos(f), np.cos(f)], axis=-1).astype(np.float32)
    sin = np.concatenate([np.sin(f), np.sin(f)], axis=-1).astype(np.float32)
    return np.ascontiguousarray(cos.T), np.ascontiguousarray(sin.T)


def _masks(j):
    import ml_dtypes
    out = np.zeros((128, 16, 4, 128), dtype=np.float32)
    tri = (np.arange(128)[:, None] <= np.arange(128)[None, :]).astype(np.float32)
    for beta in range(16):
        for m in range(4):
            d = beta - 4 * m - j
            if d < 0:
                out[:, beta, m, :] = 1.0
            elif d == 0:
                out[:, beta, m, :] = tri
    return np.ascontiguousarray(out.reshape(128, 16 * 512)).astype(ml_dtypes.bfloat16)


_CACHE = {}


def prep(args):
    _CACHE["rope"] = _rope_tables()
    _CACHE["wt"] = {
        "q": _tile_w(np.ascontiguousarray(np.asarray(args["L0_attn_wq"]).T)),
        "k": _tile_w(np.ascontiguousarray(np.asarray(args["L0_attn_wk"]).T)),
        "o": _tile_w(np.ascontiguousarray(np.asarray(args["L0_attn_wo"]).T)),
        "v": np.ascontiguousarray(np.asarray(args["L0_attn_wv"]).T),
        "1": _tile_w(np.ascontiguousarray(np.asarray(args["L0_ffn_w1"]).T)),
        "2": _tile_w(np.ascontiguousarray(np.asarray(args["L0_ffn_w2"]).T)),
        "3": _tile_w(np.ascontiguousarray(np.asarray(args["L0_ffn_w3"]).T)),
    }


def make_inputs(core, args, nonce):
    b = core // 4
    j = CLASS_OF_L[core % 4]
    toks = token_ids(j)
    x = np.asarray(args["L0_attn_x"])[b]
    cos, sin = _CACHE["rope"]
    wt = _CACHE["wt"]
    return {
        "x": np.ascontiguousarray(x[toks].T),
        "wqT_t": wt["q"], "wkT_t": wt["k"], "woT_t": wt["o"], "wvT": wt["v"],
        "w1T_t": wt["1"], "w2T_t": wt["2"], "w3T_t": wt["3"],
        "cosT": np.ascontiguousarray(cos[:, toks]),
        "sinT": np.ascontiguousarray(
            np.concatenate([-sin[0:D // 2, toks], sin[D // 2:, toks]], axis=0)),
        "masks": _masks(j),
        "cls": np.array([[j]], dtype=np.int32),
        "nonce": np.array([[nonce]], dtype=np.int32),
    }


def kernel(**inputs):
    prep(inputs)
    nonce = int(np.random.randint(1, 2**31 - 1))
    nc, _ = build_nc(debug=0)
    in_maps = [make_inputs(c, inputs, nonce) for c in range(8)]
    res = run_bass_kernel_spmd(nc, in_maps, core_ids=list(range(8)))
    out = np.zeros((B, T, E), dtype=np.float32)
    for c in range(8):
        b = c // 4
        j = CLASS_OF_L[c % 4]
        toks = token_ids(j)[:TQ]
        out[b, toks, :] = np.asarray(res.results[c]["out"]).T
    return out


# version 03
# speedup vs baseline: 1.3165x; 1.0329x over previous
import math
import numpy as np
from contextlib import ExitStack

import concourse.bass as bass
import concourse.mybir as mybir
from concourse.bass_utils import run_bass_kernel_spmd

F32 = mybir.dt.float32
BF16 = mybir.dt.bfloat16
I32 = mybir.dt.int32
AF = mybir.ActivationFunctionType
ALU = mybir.AluOpType
AX = mybir.AxisListType.X

B, T, E, H, D = 2, 8192, 2048, 16, 128
HID = 5632
NE = E // 128
NH = HID // 128
TQ = 2048
TKV = 4096
ROPE_BASE = 1e6
LN_EPS = 1e-5
Q_EPS = 1e-8
SCALE = 1.0 / math.sqrt(D)
QCNT = float(HID * E)

ENGINES = ("pe", "act", "dve", "gp", "sp")
CLASS_OF_L = {0: 0, 2: 1, 1: 2, 3: 3}
OTHER = {0: 1, 1: 0, 2: 3, 3: 2}


def qblocks(j):
    return [4 * i + j for i in range(16)]


def local_blocks(j):
    return qblocks(j) + qblocks(OTHER[j])


def token_ids(j):
    return np.concatenate([np.arange(g * 128, (g + 1) * 128) for g in local_blocks(j)])


class Plan:
    def __init__(self):
        self.prog = {e: [] for e in ENGINES}
        self.sem = {}
        self.sem_static = {}

    def cnt(self, s):
        return self.sem.get(s, 0)

    def bump(self, s, n, static_n=None):
        self.sem[s] = self.cnt(s) + n
        self.sem_static[s] = self.sem_static.get(s, 0) + (n if static_n is None else static_n)
        return self.sem[s]

    def op(self, eng, fn, waits=(), inc=None, n=1):
        if inc is not None:
            self.bump(inc, n)
        self.prog[eng].append((tuple(x for x in waits if x is not None), fn, inc, n))


class Stream:

    def __init__(self, plan, queue, lanes):
        self.p, self.q, self.lanes, self.i = plan, queue, lanes, 0

    def load(self, fn, free=()):
        lane = self.lanes[self.i % 2]
        self.i += 1
        waits = []
        if self.p.cnt(lane) > 0:
            waits.append((lane, self.p.cnt(lane)))
        waits += [f for f in free if f is not None]
        self.p.op(self.q, fn, waits=waits, inc=lane, n=16)
        return (lane, self.p.cnt(lane))


class Rot:
    def __init__(self, n):
        self.n = n
        self.rel = [None] * n
        self.i = 0

    def acquire(self):
        lane = self.i % self.n
        self.i += 1
        return lane, self.rel[lane]

    def release(self, lane, sem, val):
        self.rel[lane] = (sem, val)



class MK:
    def __init__(self, debug=0):
        self.debug = debug
        nc = self.nc = bass.Bass("TRN2", target_bir_lowering=False, debug=False)
        p = self.p = Plan()
        self.st = ExitStack()
        self.cell = {}

        def din(name, shape, dt=F32):
            return nc.dram_tensor(name, shape, dt, kind="ExternalInput").ap()

        self.x_in = din("x", [E, TKV])
        self.wqT_t = din("wqT_t", [NE, E, 128])
        self.wkT_t = din("wkT_t", [NE, E, 128])
        self.woT_t = din("woT_t", [NE, E, 128])
        self.wvT = din("wvT", [E, E])
        self.w1T_t = din("w1T_t", [NH, E, 128])
        self.w2T_t = din("w2T_t", [NH, E, 128])
        self.w3T_t = din("w3T_t", [NE, HID, 128])
        self.cos_in = din("cosT", [D, TKV])
        self.sin_in = din("sinT", [D, TKV])
        self.masks_in = din("masks", [128, 16 * 512], BF16)
        self.cls_in = din("cls", [1, 1], I32)
        self.nonce_in = din("nonce", [1, 1], I32)

        self.out_d = nc.dram_tensor("out", [E, TQ], F32, kind="ExternalOutput").ap()
        self.q_dram = nc.dram_tensor("q_dram", [E, TQ], BF16).ap()
        self.attn_dram = nc.dram_tensor("attn_dram", [E, TQ], BF16).ap()
        self.xmid_dram = nc.dram_tensor("xmid_dram", [E, TQ], F32).ap()
        self.attn2_dram = nc.dram_tensor("attn2_dram", [E, TQ], BF16).ap()
        self.KG = nc.dram_tensor("KG", [E, T], BF16, addr_space="Shared").ap()
        self.VG = nc.dram_tensor("VG", [T, E], BF16, addr_space="Shared").ap()
        self.GP = nc.dram_tensor("GP", [1, 128], F32, addr_space="Shared").ap()
        self.FLAG = nc.dram_tensor("FLAG", [1, 4], I32, addr_space="Shared").ap()
        self.qwA1_sh = nc.dram_tensor("qwA1_sh", [NH // 2, 128, E], BF16,
                                      addr_space="Shared").ap()
        self.qwB1_sh = nc.dram_tensor("qwB1_sh", [NH // 2, 128, E], BF16,
                                      addr_space="Shared").ap()
        self.qwA2_sh = nc.dram_tensor("qwA2_sh", [NH // 2, 128, E], BF16,
                                      addr_space="Shared").ap()
        self.qwB2_sh = nc.dram_tensor("qwB2_sh", [NH // 2, 128, E], BF16,
                                      addr_space="Shared").ap()
        self.qwA3_sh = nc.dram_tensor("qwA3_sh", [NE // 2, 128, HID], BF16,
                                      addr_space="Shared").ap()
        self.qwB3_sh = nc.dram_tensor("qwB3_sh", [NE // 2, 128, HID], BF16,
                                      addr_space="Shared").ap()

        self.dbg = {}
        if debug == 1:
            self.dbg["kg"] = nc.dram_tensor("dbg_kg", [E, T], BF16, kind="ExternalOutput").ap()
            self.dbg["vg"] = nc.dram_tensor("dbg_vg", [T, E], BF16, kind="ExternalOutput").ap()
            self.dbg["q"] = nc.dram_tensor("dbg_q", [E, TQ], BF16, kind="ExternalOutput").ap()
            self.dbg["gp"] = nc.dram_tensor("dbg_gp", [1, 128], F32, kind="ExternalOutput").ap()
        if debug == 2:
            self.dbg["at"] = nc.dram_tensor("dbg_at", [E, TQ], BF16, kind="ExternalOutput").ap()
        if debug == 3:
            self.dbg["xm"] = nc.dram_tensor("dbg_xm", [E, TQ], F32, kind="ExternalOutput").ap()
        if debug == 4:
            self.dbg["xn2"] = nc.dram_tensor("dbg_xn2", [E, 1024], BF16, kind="ExternalOutput").ap()
            self.dbg["y3"] = nc.dram_tensor("dbg_y3", [E, 1024], BF16, kind="ExternalOutput").ap()
            self.dbg["sig"] = nc.dram_tensor("dbg_sig", [128, 1024], BF16, kind="ExternalOutput").ap()
            self.dbg["hid"] = nc.dram_tensor("dbg_hid", [128, 1024], BF16, kind="ExternalOutput").ap()

        def sbt(name, shape, dt):
            return self.st.enter_context(nc.sbuf_tensor(name, shape, dt))

        self.A = sbt("A", [128, 32768], BF16)
        self.Bb = sbt("Bb", [128, 45056], BF16)
        self.C = sbt("C", [128, 4096], F32)
        self.Eb = sbt("Eb", [128, 4096], BF16)
        self.Fb = sbt("Fb", [128, 5632], BF16)
        self.rows = sbt("rows", [1, 4096], F32)
        self.misc_f = sbt("misc_f", [128, 72], F32)
        self.mrow = sbt("mrow", [1, 128], F32)
        self.gstat = sbt("gstat", [128, 272], F32)
        self.gprow = sbt("gprow", [1, 128], F32)
        self.onecol_b = sbt("onecol_b", [128, 1], BF16)
        self.onecol_f = sbt("onecol_f", [128, 1], F32)
        self.onerow_f = sbt("onerow_f", [1, 128], F32)
        self.cidt = sbt("cidt", [1, 2], I32)
        self.flagt = sbt("flagt", [1, 4], I32)
        self.sig4 = self.Fb[:, 0:1024]

        A, Bb, C, Eb = self.A, self.Bb, self.C, self.Eb
        self.xb = [A[:, c * 2048:(c + 1) * 2048] for c in range(NE)]
        self.wv_sp = [Bb[:, i * 8192:(i + 1) * 8192] for i in range(2)]
        self.wt_b = [Bb[:, 16384 + i * 2048:16384 + (i + 1) * 2048] for i in range(2)]
        self.ksb = [Bb[:, 20480 + i * 2048:20480 + (i + 1) * 2048] for i in range(2)]
        self.sqs = [Bb[:, 24576 + i * 2048:24576 + (i + 1) * 2048] for i in range(2)]
        self.ab_A = Bb[:, 28672:30720]
        self.ab_B = Bb[:, 30720:32768]
        self.t12 = [Bb[:, 32768:34816], Bb[:, 39936:41984]]
        self.evacb = [Bb[:, 34816 + i * 2048:34816 + (i + 1) * 2048] for i in range(2)]
        self.vtile = [Bb[:, 38912 + i * 512:38912 + (i + 1) * 512] for i in range(2)]
        self.cosb = Eb[:, 0:2048]
        self.sinb = Eb[:, 2048:4096]
        self.gbufs = [C[:, 0:1024], C[:, 1024:2048]]
        self.rawq = [C[:, 512:1536], C[:, 1536:2560]]
        self.qe = [Eb[:, 0:2048], Eb[:, 2048:4096]]
        self.rrow = self.rows[0:1, 0:512]
        self.rowX = self.rows[0:1, 0:2048]
        self.rowQ = self.rows[0:1, 2048:4096]
        self.rowM = C[0:1, 2048:4096]

        self.kh = [A[:, i * 8192:(i + 1) * 8192] for i in range(2)]
        self.vh = [A[:, 16384 + i * 8192:16384 + (i + 1) * 8192] for i in range(2)]
        self.masksb = Bb[:, 0:8192]
        self.qhb = [Bb[:, 8192 + i * 2048:8192 + (i + 1) * 2048] for i in range(2)]
        self.pbuf = [Bb[:, 12288 + i * 1024:12288 + (i + 1) * 1024] for i in range(2)]
        self.at_t = [Bb[:, 14336 + i * 512:14336 + (i + 1) * 512] for i in range(2)]
        self.rbig = C[:, 0:512]

        self.worhs = [Bb[:, i * 8192:(i + 1) * 8192] for i in range(2)]
        self.wt3_b = [Bb[:, 16384 + i * 2048:16384 + (i + 1) * 2048] for i in range(2)]
        self.asq3 = [Bb[:, 20480 + i * 2048:20480 + (i + 1) * 2048] for i in range(2)]
        self.xre = C[:, 0:2048]
        self.xmo = C[:, 2048:4096]

        self.xn2 = [A[:, c * 1024:(c + 1) * 1024] for c in range(NE)]
        self.y3b = [A[:, 16384 + c * 1024:16384 + (c + 1) * 1024] for c in range(NE)]
        self.hidv = [Bb[:, h * 1024:(h + 1) * 1024] for h in range(NH)]
        self.qw1v = Eb[:, 0:2048]
        self.qw2v = Eb[:, 2048:4096]
        self.qw3v = self.Fb[:, 0:5632]
        self.raw1 = C[:, 0:2048]
        self.raw2 = C[:, 2048:4096]
        self.ab4_A = self.rows[0:1, 0:1024]
        self.ab4_B = self.rows[0:1, 1024:2048]
        self.row4X = self.rows[0:1, 2048:3072]
        self.row4Q = self.rows[0:1, 3072:4096]

        psume = self.st.enter_context
        self.psS0 = psume(nc.psum_tensor("psS0", [128, 1024], F32))
        self.psS1 = psume(nc.psum_tensor("psS1", [128, 1024], F32))
        self.psA = psume(nc.psum_tensor("psA", [128, 512], F32))
        self.psB = psume(nc.psum_tensor("psB", [128, 512], F32))
        self.psC = psume(nc.psum_tensor("psC", [128, 512], F32))
        self.psD = psume(nc.psum_tensor("psD", [128, 512], F32))
        self.ps_acc = [self.psA, self.psB]
        self.psS = [self.psS0, self.psS1]
        self.statrows = [self.psS0[0:1, 0:512], self.psS0[0:1, 512:1024],
                         self.psS1[0:1, 0:512], self.psS1[0:1, 512:1024],
                         self.psA[0:1, 0:512], self.psB[0:1, 0:512],
                         self.psC[0:1, 0:512], self.psD[0:1, 0:512]]

        self.sems = {}
        for s in ["gl0", "gl1", "gl2", "gl3",
                  "sk0", "sk1", "sv0", "sv1", "sq0", "sq1", "sm0", "sm1",
                  "sr0", "sr1", "cdv", "cac",
                  "kw0", "kw1", "vw0", "vw1", "qw0", "qw1", "aw0", "aw1",
                  "xw0", "xw1", "ww0", "ww1", "ow0", "ow1",
                  "sw", "sw0", "sw1", "poll", "bar", "bar2", "qf0",
                  "pe", "ac", "dv", "pc", "ow"]:
            self.sems[s] = self.st.enter_context(nc.semaphore(s))

        self.gp_st = Stream(p, "gp", ["gl0", "gl1"])
        self.gp_st2 = Stream(p, "gp", ["gl2", "gl3"])
        self.sc_k = Stream(p, "act", ["sk0", "sk1"])
        self.sc_v = Stream(p, "act", ["sv0", "sv1"])
        self.sc_q = Stream(p, "act", ["sq0", "sq1"])
        self.sc_m = Stream(p, "act", ["sm0", "sm1"])
        self.t12_rot = Rot(2)

        self.acc_rot = Rot(2)
        self.wt_rot = Rot(2)
        self.ev_rot = Rot(2)
        self.ksb_rot = Rot(2)
        self.sqs_rot = Rot(2)
        self.vt_rot = Rot(2)
        self.stat_free = [None] * 8
        self.psS1row_free = None

    def w(self, eng, fn, waits=(), inc=None, n=1):
        self.p.op(eng, fn, waits, inc, n)

    def chain(self, eng, cell, instr, key="cdv"):
        cell[key] = cell.get(key, 0) + 1
        instr.then_inc(self.sems[key], 1)
        eng.wait_ge(self.sems[key], cell[key])

    def branch4(self, make, inc_sem, n=16, waits=(), eng="sp", reg="s_cls",
                nparts_hint=1):
        sems = self.sems

        nparts = [None]

        def attach(r):
            rs = r if isinstance(r, (list, tuple)) else [r]
            if nparts[0] is None:
                nparts[0] = len(rs)
            assert nparts[0] == len(rs)
            for x in rs:
                x.then_inc(sems[inc_sem], n)

        def fn(g, c):
            with g.If_eq(c[reg], 0):
                attach(make(g, 0))
            with g.Else():
                with g.If_eq(c[reg], 1):
                    attach(make(g, 1))
                with g.Else():
                    with g.If_eq(c[reg], 2):
                        attach(make(g, 2))
                    with g.Else():
                        attach(make(g, 3))
            return None
        self.p.bump(inc_sem, n * nparts_hint, static_n=4 * n * nparts_hint)
        self.p.prog[eng].append((tuple(x for x in waits if x is not None), fn, None, 0))
        return self.p.cnt(inc_sem)

    def swrite(self, make_fn, lanes, lane01, waits=()):
        sems = self.sems
        swl = lanes[lane01]
        wt = [x for x in waits if x is not None]
        if self.p.cnt(swl) > 0:
            wt.append((swl, self.p.cnt(swl)))
        self.p.bump(swl, 16)

        def fn(s_, c):
            make_fn(s_, c).then_inc(sems[swl], 16)
            return None
        self.p.prog["sp"].append((tuple(wt), fn, None, 0))
        return (swl, self.p.cnt(swl))

    def bwrite(self, make, lanes, lane01, nparts, waits=()):
        swl = lanes[lane01]
        wt = [x for x in waits if x is not None]
        if self.p.cnt(swl) > 0:
            wt.append((swl, self.p.cnt(swl)))
        val = self.branch4(make, swl, waits=wt, nparts_hint=nparts)
        return (swl, val)

    def phase0(self):
        p, w, sems = self.p, self.w, self.sems
        w("gp", lambda g, c: g.dma_start(out=self.cidt[0:1, 0:1], in_=self.cls_in[:, :]),
          inc="gl0", n=16)
        w("gp", lambda g, c: g.dma_start(out=self.cidt[0:1, 1:2], in_=self.nonce_in[:, :]),
          waits=[("gl0", 16)], inc="gl0", n=16)
        self.cid_ld = p.cnt("gl0")

        w("dve", lambda v, c: v.memset(self.onecol_b[:, :], 1.0))
        w("dve", lambda v, c: v.memset(self.onecol_f[:, :], 1.0))
        w("dve", lambda v, c: v.memset(self.onerow_f[:, :], 1.0))
        w("dve", lambda v, c: v.memset(self.gprow[:, :], 0.0))
        w("dve", lambda v, c: v.memset(self.mrow[:, :], 0.0), inc="dv")
        self.dv_const = p.cnt("dv")

    def phase1(self):
        p, w = self.p, self.w

        gseq = []
        for wi, wt_src in ((0, self.w1T_t), (1, self.w2T_t)):
            for hc in range(NH):
                src = wt_src[hc].rearrange("(c p) h -> p c h", p=128)
                for h2 in range(2):
                    gseq.append((wi, src[:, 8 * h2:8 * h2 + 8, :], 1024))
        for fc in range(NE):
            src = self.w3T_t[fc].rearrange("(c p) f -> p c f", p=128)
            for s0, ln in ((0, 1024), (1024, 1024), (2048, 1024),
                           (3072, 1024), (4096, 768), (4864, 768)):
                gseq.append((2, src[:, s0 // 128:(s0 + ln) // 128, :], ln))
        ngp = len(gseq)
        assert ngp == 272
        self.gseq, self.ngp = gseq, ngp
        self.gred_dv = [None] * ngp
        self.gw_cols = {wi: [i for i in range(ngp) if gseq[i][0] == wi]
                        for wi in range(3)}
        self._gamma_i = 0

        self.emit_half(0)
        self.emit_half(1)
        self.emit_gamma(ngp)
        self.gamma_final()
        self.barrier()

    def emit_gamma(self, k):
        p, w = self.p, self.w
        i0 = self._gamma_i
        for i in range(i0, min(i0 + k, self.ngp)):
            lane = i % 2

            def gdma(s, c, i=i, lane=lane):
                _, src, ln = self.gseq[i]
                gout = self.gbufs[lane][:, 0:ln].rearrange("p (c h) -> p c h",
                                                           h=128)
                return s.dma_start(out=gout, in_=src)
            free = ([("dv", self.gred_dv[i - 2])] if i >= 2
                    else [("dv", self.dv_const)])
            lv = self.sc_m.load(gdma, free=free)

            def gred(v, c, i=i, lane=lane):
                _, _, ln = self.gseq[i]
                return v.tensor_reduce(self.gstat[:, i:i + 1],
                                       self.gbufs[lane][:, 0:ln],
                                       axis=AX, op=ALU.add,
                                       apply_absolute_value=True)
            w("dve", gred, waits=[lv], inc="dv")
            self.gred_dv[i] = p.cnt("dv")
        self._gamma_i = min(i0 + k, self.ngp)

    def emit_half(self, half):
        p, w, cell = self.p, self.w, self.cell
        sems = self.sems
        xb, sqs, statrows = self.xb, self.sqs, self.statrows
        xoff = half * TQ

        if half == 1:
            self.stat_free[4] = self.acc_rot.rel[0]
            self.stat_free[5] = self.acc_rot.rel[1]
            for k in range(4):
                self.stat_free[k] = self.psS1row_free

        xld = []
        for cc in range(NE):
            def ldx(g, c, cc=cc, xoff=xoff):
                return g.dma_start(
                    out=xb[cc], in_=self.x_in[cc * 128:(cc + 1) * 128, xoff:xoff + TQ])
            free = [("pe", cell["h0_xn_users_pe"])] if half == 1 else []
            xld.append(self.gp_st.load(ldx, free=free))
        free = [("dv", cell["h0_rope_dv"])] if half == 1 else []
        cs0 = self.gp_st2.load(lambda g, c, xoff=xoff: g.dma_start(
            out=self.cosb, in_=self.cos_in[:, xoff:xoff + TQ]), free=free)
        cs1 = self.gp_st2.load(lambda g, c, xoff=xoff: g.dma_start(
            out=self.sinb, in_=self.sin_in[:, xoff:xoff + TQ]), free=free)

        for cc in range(NE):
            lane, rel = self.sqs_rot.acquire()

            def sq(s, c, cc=cc, lane=lane):
                return s.activation(out=sqs[lane], in_=xb[cc], func=AF.Square)
            w("act", sq, waits=[xld[cc], rel], inc="ac")
            sq_ac = p.cnt("ac")
            for sp in range(4):
                def mmx(t, c, cc=cc, sp=sp):
                    return t.matmul(statrows[sp], self.onecol_b[:, :],
                                    xb[cc][:, sp * 512:(sp + 1) * 512],
                                    start=(cc == 0), stop=(cc == NE - 1))
                wt_ = [xld[cc]] if sp == 0 else []
                if cc == 0:
                    wt_.append(self.stat_free[sp])
                    self.stat_free[sp] = None
                    if sp == 0:
                        wt_.append(("dv", self.dv_const))
                w("pe", mmx, waits=wt_, inc="pe")
            for sp in range(4):
                def mmq(t, c, cc=cc, sp=sp, lane=lane):
                    return t.matmul(statrows[4 + sp], self.onecol_b[:, :],
                                    sqs[lane][:, sp * 512:(sp + 1) * 512],
                                    start=(cc == 0), stop=(cc == NE - 1))
                wt_ = []
                if sp == 0:
                    wt_.append(("ac", sq_ac))
                if cc == 0:
                    wt_.append(self.stat_free[4 + sp])
                    self.stat_free[4 + sp] = None
                w("pe", mmq, waits=wt_, inc="pe")
            self.sqs_rot.release(lane, "pe", p.cnt("pe"))
        stats_pe = p.cnt("pe")

        rowX, rowQ, rowM = self.rowX, self.rowQ, self.rowM

        def rows_calc(v, c):
            ch = self.chain
            for sp in range(4):
                v.tensor_copy(rowX[0:1, sp * 512:(sp + 1) * 512], statrows[sp])
            for sp in range(4):
                i = v.tensor_copy(rowQ[0:1, sp * 512:(sp + 1) * 512], statrows[4 + sp])
            ch(v, c, i)
            ch(v, c, v.tensor_scalar(rowX, rowX, 1.0 / E, None, op0=ALU.mult))
            ch(v, c, v.tensor_scalar(rowQ, rowQ, 1.0 / E, None, op0=ALU.mult))
            ch(v, c, v.tensor_mul(rowM, rowX, rowX))
            ch(v, c, v.tensor_sub(rowQ, rowQ, rowM))
            return v.tensor_scalar(rowQ, rowQ, LN_EPS, None, op0=ALU.add)
        w("dve", rows_calc, waits=[("pe", stats_pe)], inc="dv")
        for sp in range(8):
            self.stat_free[sp] = ("dv", p.cnt("dv"))
        w("act", lambda s, c: s.activation(out=rowQ, in_=rowQ, func=AF.Sqrt),
          waits=[("dv", p.cnt("dv"))], inc="ac")

        def rows_fin(v, c):
            ch = self.chain
            ch(v, c, v.reciprocal(rowQ, rowQ))
            ch(v, c, v.tensor_mul(rowX, rowX, rowQ))
            return v.tensor_scalar(rowX, rowX, -1.0, None, op0=ALU.mult)
        w("dve", rows_fin, waits=[("ac", p.cnt("ac"))], inc="dv")
        rows_dv = p.cnt("dv")

        ab_ac = None
        for src_row, dst in ((rowQ, self.ab_A), (rowX, self.ab_B)):
            for sp in range(4):
                def bmm(t, c, src_row=src_row, sp=sp):
                    return t.matmul(self.psC[:, :], self.onerow_f[:, :],
                                    src_row[0:1, sp * 512:(sp + 1) * 512],
                                    start=True, stop=True)
                wt_ = [("dv", rows_dv)]
                if ab_ac is not None:
                    wt_.append(("ac", ab_ac))
                else:
                    wt_.append(self.stat_free[6])
                    self.stat_free[6] = None
                w("pe", bmm, waits=wt_, inc="pe")

                def bcp(s, c, dst=dst, sp=sp):
                    return s.copy(out=dst[:, sp * 512:(sp + 1) * 512], in_=self.psC[:, :])
                w("act", bcp, waits=[("pe", p.cnt("pe"))], inc="ac")
                ab_ac = p.cnt("ac")
        self.stat_free[6] = ("ac", ab_ac)
        self.rows_free = ("pe", p.cnt("pe"))

        for cc in range(NE):
            lane, rel = self.sqs_rot.acquire()

            def ap1(v, c, cc=cc, lane=lane):
                return v.tensor_mul(sqs[lane], xb[cc], self.ab_A)

            def ap2(v, c, cc=cc, lane=lane):
                return v.tensor_add(xb[cc], sqs[lane], self.ab_B)
            w("dve", ap1, waits=[("ac", ab_ac), rel], inc="dv")
            w("dve", ap2, waits=[("dv", p.cnt("dv"))], inc="dv")
            self.sqs_rot.release(lane, "dv", p.cnt("dv"))
        xn_dv = p.cnt("dv")

        self.emit_gamma(12)

        for which in (["k", "q"] if half == 0 else ["k"]):
            wt_src = self.wkT_t if which == "k" else self.wqT_t
            for fc in range(NE):
                wl_lane, wl_rel = self.wt_rot.acquire()

                def ldw(g, c, wt_src=wt_src, fc=fc, wl_lane=wl_lane):
                    return g.dma_start(
                        out=self.wt_b[wl_lane].rearrange("p (c f) -> p c f", f=128),
                        in_=wt_src[fc].rearrange("(c p) f -> p c f", p=128))
                wl = self.gp_st.load(ldw, free=[wl_rel])

                ev_lane, ev_rel = self.ev_rot.acquire()
                for sp in range(4):
                    a_lane, a_rel = self.acc_rot.acquire()
                    acc = self.ps_acc[a_lane]
                    for e_ in range(NE):
                        def mm(t, c, acc=acc, wl_lane=wl_lane, e_=e_, sp=sp):
                            return t.matmul(
                                acc[:, :], self.wt_b[wl_lane][:, e_ * 128:(e_ + 1) * 128],
                                xb[e_][:, sp * 512:(sp + 1) * 512],
                                start=(e_ == 0), stop=(e_ == NE - 1))
                        wt_ = []
                        if e_ == 0:
                            wt_ = [wl, ("dv", xn_dv), a_rel]
                        w("pe", mm, waits=wt_, inc="pe")
                    mm_pe = p.cnt("pe")

                    def ev(s, c, acc=acc, sp=sp, ev_lane=ev_lane):
                        return s.copy(out=self.evacb[ev_lane][:, sp * 512:(sp + 1) * 512],
                                      in_=acc[:, :])
                    w("act", ev,
                      waits=[("pe", mm_pe), ev_rel if sp == 0 else None], inc="ac")
                    self.acc_rot.release(a_lane, "ac", p.cnt("ac"))
                self.wt_rot.release(wl_lane, "pe", p.cnt("pe"))
                ev_ac = p.cnt("ac")

                k_lane, k_rel = self.ksb_rot.acquire()
                t_lane, t_rel = self.t12_rot.acquire()
                src, dst = self.evacb[ev_lane], self.ksb[k_lane]
                t12x, cosb, sinb = self.t12[t_lane], self.cosb, self.sinb
                srl = "sr0" if t_lane == 0 else "sr1"
                swap_waits = [("ac", ev_ac), t_rel]
                if p.cnt(srl) > 0:
                    swap_waits.append((srl, p.cnt(srl)))

                def swap(s, c, src=src, t12x=t12x, srl=srl):
                    s.dma_start(out=t12x[0:64, :],
                                in_=src[64:128, :]).then_inc(sems[srl], 16)
                    s.dma_start(out=t12x[64:128, :],
                                in_=src[0:64, :]).then_inc(sems[srl], 16)
                    return None
                p.bump(srl, 32)
                p.prog["act"].append((tuple(x for x in swap_waits if x is not None),
                                      swap, None, 0))
                swap_val = (srl, p.cnt(srl))

                def rope(v, c, src=src, dst=dst, t12x=t12x):
                    v.tensor_mul(dst, src, cosb)
                    self.chain(v, c, v.tensor_mul(t12x, t12x, sinb))
                    return v.tensor_add(dst, dst, t12x)
                w("dve", rope, waits=[swap_val, cs0, cs1, k_rel], inc="dv")
                rope_dv = p.cnt("dv")
                self.ev_rot.release(ev_lane, "dv", rope_dv)
                self.t12_rot.release(t_lane, "dv", rope_dv)

                sq_lane, sq_rel = self.sqs_rot.acquire()

                def nsq(s, c, k_lane=k_lane, sq_lane=sq_lane):
                    return s.activation(out=sqs[sq_lane], in_=self.ksb[k_lane],
                                        func=AF.Square)
                w("act", nsq, waits=[("dv", rope_dv), sq_rel], inc="ac")
                nsq_ac = p.cnt("ac")
                nrow_tgts = [self.psS0[0:1, 0:512], self.psS0[0:1, 512:1024],
                             self.psS1[0:1, 0:512], self.psS1[0:1, 512:1024]]
                for sp in range(4):
                    def nmm(t, c, sp=sp, sq_lane=sq_lane):
                        return t.matmul(nrow_tgts[sp], self.onecol_b[:, :],
                                        sqs[sq_lane][:, sp * 512:(sp + 1) * 512],
                                        start=True, stop=True)
                    wt_ = []
                    if sp == 0:
                        wt_ = [("ac", nsq_ac), self.psS1row_free]
                        wt_ += [self.stat_free[k] for k in range(4)]
                        for k in range(4):
                            self.stat_free[k] = None
                    w("pe", nmm, waits=wt_, inc="pe")
                self.sqs_rot.release(sq_lane, "pe", p.cnt("pe"))
                nmm_pe = p.cnt("pe")

                def nred(v, c, which=which, fc=fc, half=half):
                    ch = self.chain
                    if which == "q":
                        for sp in range(4):
                            i = v.tensor_reduce(self.mrow[0:1, 4 * fc + sp:4 * fc + sp + 1],
                                                nrow_tgts[sp], axis=AX, op=ALU.max)
                        return i
                    ch(v, c, v.tensor_reduce(self.mrow[0:1, 80:81],
                                             self.psS0[0:1, 0:1024],
                                             axis=AX, op=ALU.max))
                    ch(v, c, v.tensor_reduce(self.mrow[0:1, 81:82],
                                             self.psS1[0:1, 0:1024],
                                             axis=AX, op=ALU.max))
                    ch(v, c, v.tensor_max(self.mrow[0:1, 80:81],
                                          self.mrow[0:1, 80:81],
                                          self.mrow[0:1, 81:82]))
                    tgt = self.mrow[0:1, 64 + fc:64 + fc + 1]
                    if half == 1:
                        return v.tensor_max(tgt, tgt, self.mrow[0:1, 80:81])
                    return v.tensor_copy(tgt, self.mrow[0:1, 80:81])
                w("dve", nred, waits=[("pe", nmm_pe)], inc="dv")
                self.psS1row_free = ("dv", p.cnt("dv"))

                if which == "q":
                    def qwr(s_, c, fc=fc, k_lane=k_lane):
                        return s_.dma_start(
                            out=self.q_dram[fc * 128:(fc + 1) * 128, :],
                            in_=self.ksb[k_lane])
                    val = self.swrite(qwr, ("qw0", "qw1"), fc % 2,
                                      waits=[("dv", rope_dv)])
                    self.ksb_rot.release(k_lane, *val)
                else:
                    KG = self.KG

                    def kwr(s_, j, fc=fc, half=half, k_lane=k_lane):
                        jj = j if half == 0 else OTHER[j]
                        out = []
                        for blk in range(16):
                            g_ = 4 * blk + jj
                            out.append(s_.dma_start(
                                out=KG[fc * 128:(fc + 1) * 128,
                                       g_ * 128:(g_ + 1) * 128],
                                in_=self.ksb[k_lane][:, blk * 128:(blk + 1) * 128]))
                        return out
                    val = self.bwrite(kwr, ("kw0", "kw1"), fc % 2, 16,
                                      waits=[("dv", rope_dv)])
                    self.ksb_rot.release(k_lane, *val)
                self.emit_gamma(4)

        for fs in range(4):
            wv_lane = fs % 2
            wv_rel = cell.get(("wv_rel", wv_lane))

            def ldv(g, c, fs=fs, wv_lane=wv_lane):
                return g.dma_start(
                    out=self.wv_sp[wv_lane].rearrange("p (c f) -> p c f", f=512),
                    in_=self.wvT[:, fs * 512:(fs + 1) * 512]
                    .rearrange("(c p) f -> p c f", p=128))
            vl = self.gp_st2.load(ldv, free=[wv_rel])
            for tt in range(16):
                a_lane, a_rel = self.acc_rot.acquire()
                acc = self.ps_acc[a_lane]
                for e_ in range(NE):
                    def vmm(t, c, acc=acc, wv_lane=wv_lane, e_=e_, tt=tt):
                        return t.matmul(
                            acc[:, :], xb[e_][:, tt * 128:(tt + 1) * 128],
                            self.wv_sp[wv_lane][:, e_ * 512:(e_ + 1) * 512],
                            start=(e_ == 0), stop=(e_ == NE - 1))
                    wt_ = [vl, ("dv", xn_dv), a_rel] if e_ == 0 else []
                    w("pe", vmm, waits=wt_, inc="pe")
                vmm_pe = p.cnt("pe")

                vt_lane, vt_rel = self.vt_rot.acquire()

                def vev(s, c, acc=acc, vt_lane=vt_lane):
                    return s.copy(out=self.vtile[vt_lane], in_=acc[:, :])
                w("act", vev, waits=[("pe", vmm_pe), vt_rel], inc="ac")
                vev_ac = p.cnt("ac")
                self.acc_rot.release(a_lane, "ac", vev_ac)

                VG = self.VG

                def vwr(s_, j, tt=tt, half=half, fs=fs, vt_lane=vt_lane):
                    g_ = local_blocks(j)[tt + 16 * half]
                    return s_.dma_start(
                        out=VG[g_ * 128:(g_ + 1) * 128, fs * 512:(fs + 1) * 512],
                        in_=self.vtile[vt_lane])
                val = self.bwrite(vwr, ("vw0", "vw1"), tt % 2, 1,
                                  waits=[("ac", vev_ac)])
                self.vt_rot.release(vt_lane, *val)
            cell[("wv_rel", wv_lane)] = ("pe", p.cnt("pe"))
            self.emit_gamma(8)

        if half == 0:
            cell["h0_xn_users_pe"] = p.cnt("pe")
            cell["h0_rope_dv"] = p.cnt("dv")

    def gamma_final(self):
        p, w = self.p, self.w
        first_waits = [("dv", self.gred_dv[-1]), self.stat_free[7],
                       self.psS1row_free]
        self.stat_free[7] = None
        for wi in range(3):
            cols = self.gw_cols[wi]
            c0, c1 = min(cols), max(cols) + 1
            assert cols == list(range(c0, c1))

            def gmm(t, c, c0=c0, c1=c1):
                return t.matmul(self.psD[0:1, 0:c1 - c0], self.onecol_f[:, :],
                                self.gstat[:, c0:c1], start=True, stop=True)
            w("pe", gmm, waits=first_waits if wi == 0 else [("dv", p.cnt("dv"))],
              inc="pe")

            def gsum(v, c, wi=wi, c0=c0, c1=c1):
                return v.tensor_reduce(self.mrow[0:1, 84 + wi:85 + wi],
                                       self.psD[0:1, 0:c1 - c0], axis=AX, op=ALU.add)
            w("dve", gsum, waits=[("pe", p.cnt("pe"))], inc="dv")
        self.gam_dv = p.cnt("dv")
        self.psDrow_free = ("dv", self.gam_dv)

    def barrier(self):
        p, w, sems = self.p, self.w, self.sems
        self.ph1_pe = p.cnt("pe")

        def gp_stage(v, c):
            v.tensor_copy(self.gprow[0:1, 0:3], self.mrow[0:1, 84:87])
            return v.tensor_copy(self.gprow[0:1, 16:32], self.mrow[0:1, 64:80])
        w("dve", gp_stage, waits=[("dv", self.gam_dv)], inc="dv")
        stage_dv = p.cnt("dv")

        GP = self.GP

        def gpwr(s_, j):
            base = 0 if j in (0, 1) else 64
            return s_.dma_start(out=GP[0:1, base:base + 32], in_=self.gprow[0:1, 0:32])
        self.branch4(gpwr, "sw", waits=[("dv", stage_dv)])

        flag_waits = [(s, p.cnt(s))
                      for s in ("kw0", "kw1", "vw0", "vw1", "qw0", "qw1", "sw")
                      if p.cnt(s) > 0]

        def pre_flag(s_, c):
            for s, v in flag_waits:
                s_.wait_ge(sems[s], v)
            return None
        p.prog["sp"].append(((), pre_flag, None, 0))

        FLAG = self.FLAG

        def flagwr(s_, j):
            slot = 0 if j in (0, 1) else 1
            return s_.dma_start(out=FLAG[0:1, slot:slot + 1], in_=self.cidt[0:1, 1:2])
        self.branch4(flagwr, "sw")

        sw_done = p.cnt("sw")

        def spin(s_, c):
            s_.wait_ge(sems["sw"], sw_done)
            with s_.register("r_nd") as rnd, s_.register("r_f0") as rf0, \
                    s_.register("r_f1") as rf1, s_.register("r_non") as rnon:
                s_.reg_load(rnon, self.cidt[0:1, 1:2])
                s_.reg_mov(rnd, 1)
                with s_.While(rnd):
                    s_.sem_clear(sems["poll"])
                    s_.dma_start(out=self.flagt[0:1, 0:2],
                                 in_=FLAG[0:1, 0:2]).then_inc(sems["poll"], 16)
                    s_.wait_ge(sems["poll"], 16)
                    s_.reg_load(rf0, self.flagt[0:1, 0:1])
                    s_.reg_load(rf1, self.flagt[0:1, 1:2])
                    s_.reg_alu(rf0, rf0, rnon, ALU.is_equal)
                    s_.reg_alu(rf1, rf1, rnon, ALU.is_equal)
                    s_.reg_alu(rnd, rf0, rf1, ALU.add)
                    s_.reg_alu(rnd, rnd, 2, ALU.is_lt)
            s_.sem_clear(sems["poll"])
            return None
        p.prog["sp"].append(((), spin, None, 0))
        w("sp", lambda s_, c: s_.nop(), inc="bar")
        self.BAR = p.cnt("bar")

        def thrc(v, c):
            ch = self.chain
            m = self.mrow
            ch(v, c, v.tensor_scalar(m[0:1, 88:91], m[0:1, 84:87], 1.0 / QCNT,
                                     None, op0=ALU.mult))
            ch(v, c, v.tensor_scalar(m[0:1, 88:91], m[0:1, 88:91], Q_EPS, 0.5,
                                     op0=ALU.add, op1=ALU.mult))
            return v.tensor_scalar(m[0:1, 92:95], m[0:1, 88:91], -1.0, None,
                                   op0=ALU.mult)
        w("dve", thrc, waits=[("dv", self.gam_dv)], inc="dv")
        thr_dv = p.cnt("dv")

        def thrmm(t, c):
            return t.matmul(self.psS1[:, 0:8], self.onerow_f[:, :],
                            self.mrow[0:1, 88:96], start=True, stop=True)
        w("pe", thrmm, waits=[("dv", thr_dv)], inc="pe")
        thr_pe = p.cnt("pe")

        def thrcp(s, c):
            return s.copy(out=self.misc_f[:, 64:72], in_=self.psS1[:, 0:8])
        w("act", thrcp, waits=[("pe", thr_pe)], inc="ac")
        self.thr_ac = p.cnt("ac")
        self.thr_ap = [self.misc_f[:, 64 + i:65 + i] for i in range(3)]
        self.nthr_ap = [self.misc_f[:, 68 + i:69 + i] for i in range(3)]
        self.quant_setup()

    def quant_setup(self):
        steps = []
        for wi in (0, 1):
            srcT = [self.w1T_t, self.w2T_t][wi]
            shAB = [(self.qwA1_sh, self.qwB1_sh),
                    (self.qwA2_sh, self.qwB2_sh)][wi]
            for i in range(NH // 2):
                src = srcT[i].rearrange("(c p) f -> p c f", p=128)
                plane = (wi * (NH // 2) + i) % 2
                for h2 in range(2):
                    steps.append(dict(kind="w12", wi=wi,
                                      src=src[:, 8 * h2:8 * h2 + 8, :],
                                      ln=1024, off=h2 * 1024, plane=plane,
                                      first_sub=(h2 == 0), write=None))
                steps[-1]["write"] = (shAB[0][i], shAB[1][i], "qe", plane)
        for fc in range(NE // 2):
            src = self.w3T_t[fc].rearrange("(c p) f -> p c f", p=128)
            for si, (s0, ln) in enumerate(
                    ((0, 1024), (1024, 1024), (2048, 1024),
                     (3072, 1024), (4096, 768), (4864, 768))):
                steps.append(dict(kind="w3", wi=2,
                                  src=src[:, s0 // 128:(s0 + ln) // 128, :],
                                  ln=ln, off=s0, plane=None,
                                  first_sub=(si == 0), write=None))
            steps[-1]["write"] = (self.qwA3_sh[fc], self.qwB3_sh[fc],
                                  "fb", fc % 2)
        self._qsteps = steps
        self._q_i = 0
        self._q_raw_free = [None, None]
        self._q_eb_free = [None, None]
        self._q_fb_free = None
        self._q_first = True
        self._q_first_w3 = True
        self._q_flag_done = False

    def emit_quant(self, k):
        p, w, sems = self.p, self.w, self.sems
        Fb, Eb = self.Fb, self.Eb
        for j in range(self._q_i, min(self._q_i + k, len(self._qsteps))):
            st = self._qsteps[j]
            lane = j % 2

            def qld(s, c, st=st, lane=lane):
                return s.dma_start(
                    out=self.rawq[lane][:, 0:st["ln"]].rearrange(
                        "p (c f) -> p c f", f=128),
                    in_=st["src"])
            free = [self._q_raw_free[lane]]
            if j < 2:
                free.append(("dv", self.gam_dv))
            lv = self.sc_m.load(qld, free=free)

            wt_ = [lv]
            if self._q_first:
                wt_.append(("ac", self.thr_ac))
                self._q_first = False
            if st["kind"] == "w12":
                if st["first_sub"]:
                    wt_.append(self._q_eb_free[st["plane"]])

                def qz(v, c, st=st, lane=lane):
                    ch = self.chain
                    ln = st["ln"]
                    ch(v, c, v.tensor_scalar(Fb[:, 0:ln],
                                             self.rawq[lane][:, 0:ln],
                                             self.nthr_ap[st["wi"]], None,
                                             op0=ALU.is_le))
                    return v.scalar_tensor_tensor(
                        self.qe[st["plane"]][:, st["off"]:st["off"] + ln],
                        self.rawq[lane][:, 0:ln], self.thr_ap[st["wi"]],
                        Fb[:, 0:ln], op0=ALU.is_ge, op1=ALU.subtract)
            else:
                if st["first_sub"]:
                    wt_.append(self._q_fb_free)
                    if self._q_first_w3:
                        wt_ += [self._q_eb_free[0], self._q_eb_free[1]]
                        self._q_first_w3 = False

                def qz(v, c, st=st, lane=lane):
                    ch = self.chain
                    ln = st["ln"]
                    ch(v, c, v.tensor_scalar(Eb[:, 0:ln],
                                             self.rawq[lane][:, 0:ln],
                                             self.nthr_ap[2], None,
                                             op0=ALU.is_le))
                    return v.scalar_tensor_tensor(
                        Fb[:, st["off"]:st["off"] + st["ln"]],
                        self.rawq[lane][:, 0:ln], self.thr_ap[2],
                        Eb[:, 0:ln], op0=ALU.is_ge, op1=ALU.subtract)
            w("dve", qz, waits=wt_, inc="dv")
            qz_dv = p.cnt("dv")
            self._q_raw_free[lane] = ("dv", qz_dv)

            if st["write"] is not None:
                outA, outB, which, wlane = st["write"]
                wwl = ("ww0", "ww1")[wlane]
                src_ap = self.qe[st["plane"]] if which == "qe" else Fb[:, :]
                wt2 = [("dv", qz_dv)]
                if p.cnt(wwl) > 0:
                    wt2.append((wwl, p.cnt(wwl)))

                def qwr(s, c, outA=outA, outB=outB, src_ap=src_ap, wwl=wwl):
                    with s.If_eq(c["a_par"], 0):
                        s.dma_start(out=outA, in_=src_ap).then_inc(sems[wwl], 16)
                    with s.Else():
                        s.dma_start(out=outB, in_=src_ap).then_inc(sems[wwl], 16)
                    return None
                p.bump(wwl, 16, static_n=32)
                p.prog["act"].append((tuple(wt2), qwr, None, 0))
                if which == "qe":
                    self._q_eb_free[st["plane"]] = (wwl, p.cnt(wwl))
                else:
                    self._q_fb_free = (wwl, p.cnt(wwl))
        self._q_i = min(self._q_i + k, len(self._qsteps))

        if self._q_i == len(self._qsteps) and not self._q_flag_done:
            self._q_flag_done = True
            self.quant_last_dv = p.cnt("dv")
            self.quant_fb_free = self._q_fb_free
            ww_tot = [("ww0", p.cnt("ww0")), ("ww1", p.cnt("ww1"))]
            self.quant_ww_done = ww_tot

            def f2(s, c):
                with s.If_eq(c["a_par"], 0):
                    s.dma_start(out=self.FLAG[0:1, 2:3],
                                in_=self.cidt[0:1, 1:2]).then_inc(sems["qf0"], 16)
                with s.Else():
                    s.dma_start(out=self.FLAG[0:1, 3:4],
                                in_=self.cidt[0:1, 1:2]).then_inc(sems["qf0"], 16)
                return None
            p.bump("qf0", 16, static_n=32)
            p.prog["act"].append((tuple(ww_tot), f2, None, 0))

    def attn_phase(self):
        p, w, cell, sems = self.p, self.w, self.cell, self.sems
        BAR = self.BAR
        kh, vh, qhb, pbuf, at_t = self.kh, self.vh, self.qhb, self.pbuf, self.at_t
        psO = [self.psA, self.psB]
        psN = [self.psC, self.psD]
        masksb, rbig, rrow = self.masksb, self.rbig, self.rrow
        mrow, gprow, misc_f = self.mrow, self.gprow, self.misc_f

        ml = self.sc_m.load(
            lambda s, c: s.dma_start(out=masksb[:, :], in_=self.masks_in[:, :]),
            free=[("pe", self.ph1_pe)])
        self.emit_quant(60)
        gl0 = self.sc_m.load(
            lambda s, c: s.dma_start(out=gprow[0:1, 0:32], in_=self.GP[0:1, 0:32]),
            free=[("bar", BAR)])
        gl = self.sc_m.load(
            lambda s, c: s.dma_start(out=gprow[0:1, 64:96], in_=self.GP[0:1, 64:96]),
            free=[("bar", BAR)])

        rows = self.rows
        kn2r = rows[0:1, 512:528]
        lnq = rows[0:1, 528:592]
        brow = rows[0:1, 592:656]

        def knmax(v, c):
            return v.tensor_max(kn2r, gprow[0:1, 16:32], gprow[0:1, 80:96])
        w("dve", knmax, waits=[gl0, gl, self.rows_free], inc="dv")
        kn_dv = p.cnt("dv")

        def lns(s, c):
            ch = self.chain
            ch(s, c, s.activation(out=kn2r, in_=kn2r, func=AF.Ln), key="cac")
            return s.activation(out=lnq, in_=mrow[0:1, 0:64], func=AF.Ln)
        w("act", lns, waits=[("dv", kn_dv)], inc="ac")
        ln_ac = p.cnt("ac")

        def bsum(v, c):
            ch = self.chain
            i = None
            for cc4 in range(4):
                i = v.tensor_add(brow[0:1, cc4::4], lnq[0:1, cc4::4],
                                 kn2r[0:1, 0:16])
            return i
        w("dve", bsum, waits=[("ac", ln_ac)], inc="dv")
        bs_dv = p.cnt("dv")

        def bexp(s, c):
            return s.activation(out=brow, in_=brow, func=AF.Exp, scale=0.5)
        w("act", bexp, waits=[("dv", bs_dv)], inc="ac")
        be_ac = p.cnt("ac")

        def bneg(v, c):
            return v.tensor_scalar(brow, brow, -SCALE, None, op0=ALU.mult)
        w("dve", bneg, waits=[("ac", be_ac)], inc="dv")
        br_dv = p.cnt("dv")

        def bmm(t, c):
            return t.matmul(self.psA[:, 0:64], self.onerow_f[:, :],
                            brow, start=True, stop=True)
        w("pe", bmm, waits=[("dv", br_dv), self.acc_rot.rel[0]], inc="pe")

        def bcp(s, c):
            return s.copy(out=misc_f[:, 0:64], in_=self.psA[:, 0:64])
        w("act", bcp, waits=[("pe", p.cnt("pe"))], inc="ac")
        bias_ac = p.cnt("ac")

        s_free = [self.psS1row_free, ("ac", self.thr_ac)]
        pb_free = [None, None]
        o_free = [("ac", bias_ac), self.acc_rot.rel[1]]
        n_free = [self.stat_free[6], ("dv", self.gam_dv)]
        at_free = [None, None]
        cons = {}
        pending_epi = []
        first_epi = [True]

        for h in range(H):
            lane = h % 2

            def lq(s, c, h=h, lane=lane):
                return s.dma_start(out=qhb[lane],
                                   in_=self.q_dram[h * 128:(h + 1) * 128, :])

            def lk(s, c, h=h, lane=lane):
                return s.dma_start(out=kh[lane],
                                   in_=self.KG[h * 128:(h + 1) * 128, :])

            def lv(s, c, h=h, lane=lane):
                return s.dma_start(
                    out=vh[lane].rearrange("p (a d) -> p a d", d=128),
                    in_=self.VG[:, h * 128:(h + 1) * 128]
                    .rearrange("(a p) d -> p a d", p=128))
            fw = [("bar", BAR)] if h < 2 else [("pe", cons[h - 2])]
            qld = self.sc_q.load(lq, free=fw)
            kld = self.sc_k.load(lk, free=fw)
            vld = self.sc_v.load(lv, free=fw)

            for cch in range(4):
                nkt = 16 * (cch + 1)
                ngrp = nkt // 2
                p_ready_map = {}

                def emit_S(g, cch=cch, lane=lane, h=h, kld=kld, qld=qld):
                    kts = (2 * g, 2 * g + 1)
                    sb_ = self.psS[g % 2]
                    for i_, kt in enumerate(kts):
                        def smm(t, c, sb_=sb_, i_=i_, kt=kt, lane=lane, cch=cch):
                            return t.matmul(
                                sb_[:, i_ * 512:(i_ + 1) * 512],
                                kh[lane][:, kt * 128:(kt + 1) * 128],
                                qhb[lane][:, cch * 512:(cch + 1) * 512],
                                start=True, stop=True)
                        wt_ = [kld, qld, s_free[g % 2]] if i_ == 0 else []
                        w("pe", smm, waits=wt_, inc="pe")
                    smm_pe = p.cnt("pe")

                    def expf(s, c, sb_=sb_, g=g, h=h, cch=cch):
                        return s.activation(out=pbuf[g % 2], in_=sb_[:, :],
                                            func=AF.Exp,
                                            bias=misc_f[:, 4 * h + cch:4 * h + cch + 1],
                                            scale=SCALE)
                    wt_ = [("pe", smm_pe), pb_free[g % 2]]
                    if h == 0 and cch == 0 and g == 0:
                        wt_.append(("ac", bias_ac))
                    w("act", expf, waits=wt_, inc="ac")
                    exp_ac = p.cnt("ac")
                    s_free[g % 2] = ("ac", exp_ac)

                    mk_dv = None
                    for i_, kt in enumerate(kts):
                        beta = kt - 16 * cch
                        if 0 <= beta < 16:
                            def mk(v, c, g=g, i_=i_, beta=beta):
                                sl = pbuf[g % 2][:, i_ * 512:(i_ + 1) * 512]
                                return v.tensor_mul(
                                    sl, sl, masksb[:, beta * 512:(beta + 1) * 512])
                            w("dve", mk, waits=[("ac", exp_ac), ml], inc="dv")
                            mk_dv = p.cnt("dv")
                    p_ready_map[g] = (("dv", mk_dv) if mk_dv is not None
                                      else ("ac", exp_ac))

                def emit_PV(g, cch=cch, lane=lane, vld=vld, ngrp=ngrp):
                    p_ready = p_ready_map[g]
                    kts = (2 * g, 2 * g + 1)
                    for i_, kt in enumerate(kts):
                        last = (g == ngrp - 1 and i_ == 1)

                        def pv(t, c, kt=kt, i_=i_, g=g, lane=lane, cch=cch,
                               last=last):
                            return t.matmul(
                                psO[cch % 2][:, :],
                                vh[lane][:, kt * 128:(kt + 1) * 128],
                                pbuf[g % 2][:, i_ * 512:(i_ + 1) * 512],
                                start=(g == 0 and i_ == 0), stop=last)
                        wt_ = [p_ready] if i_ == 0 else []
                        if g == 0 and i_ == 0:
                            wt_ += [vld, o_free[cch % 2]]
                        w("pe", pv, waits=wt_, inc="pe")
                    for i_, kt in enumerate(kts):
                        last = (g == ngrp - 1 and i_ == 1)

                        def omm(t, c, i_=i_, g=g, cch=cch, last=last):
                            return t.matmul(
                                psN[cch % 2][0:1, 0:512], self.onecol_b[:, :],
                                pbuf[g % 2][:, i_ * 512:(i_ + 1) * 512],
                                start=(g == 0 and i_ == 0), stop=last)
                        wt_ = [n_free[cch % 2]] if (g == 0 and i_ == 0) else []
                        w("pe", omm, waits=wt_, inc="pe")
                    pb_free[g % 2] = ("pe", p.cnt("pe"))

                emit_S(0)
                if ngrp > 1:
                    emit_S(1)
                for g in range(ngrp):
                    if g == 1:
                        while pending_epi:
                            pending_epi.pop(0)()
                    emit_PV(g)
                    if g + 2 < ngrp:
                        emit_S(g + 2)
                last_pe = p.cnt("pe")
                cons[h] = last_pe

                def make_epi(h=h, cch=cch, last_pe=last_pe):
                    def epi():
                        def recip(v, c):
                            ch = self.chain
                            ch(v, c, v.tensor_copy(rrow, psN[cch % 2][0:1, 0:512]))
                            return v.reciprocal(rrow, rrow)
                        wt_ = [("pe", last_pe)]
                        if first_epi[0]:
                            wt_.append(self.rows_free)
                        w("dve", recip, waits=wt_, inc="dv")
                        rec_dv = p.cnt("dv")

                        def rbc(t, c):
                            return t.matmul(psN[cch % 2][:, :], self.onerow_f[:, :],
                                            rrow, start=True, stop=True)
                        w("pe", rbc, waits=[("dv", rec_dv)], inc="pe")

                        def rcp(s, c):
                            return s.copy(out=rbig, in_=psN[cch % 2][:, :])
                        wt_ = [("pe", p.cnt("pe"))]
                        if first_epi[0]:
                            wt_.append(("dv", self.gred_dv[-1]))
                            first_epi[0] = False
                        w("act", rcp, waits=wt_, inc="ac")
                        rb_ac = p.cnt("ac")
                        n_free[cch % 2] = ("ac", rb_ac)

                        al = cch % 2
                        arel = at_free[al]

                        def amul(v, c, al=al, cch=cch):
                            return v.tensor_mul(at_t[al], psO[cch % 2][:, :], rbig)
                        w("dve", amul, waits=[("ac", rb_ac), arel], inc="dv")
                        am_dv = p.cnt("dv")
                        o_free[cch % 2] = ("dv", am_dv)

                        def awr(s_, c, h=h, cch=cch, al=al):
                            return s_.dma_start(
                                out=self.attn_dram[h * 128:(h + 1) * 128,
                                                   cch * 512:(cch + 1) * 512],
                                in_=at_t[al])
                        at_free[al] = self.swrite(awr, ("aw0", "aw1"), al,
                                                  waits=[("dv", am_dv)])
                    return epi
                pending_epi.append(make_epi())
            self.emit_quant(5)
        while pending_epi:
            pending_epi.pop(0)()
        self.emit_quant(len(self._qsteps))
        assert self._q_flag_done
        self.attn_done = [("aw0", p.cnt("aw0")), ("aw1", p.cnt("aw1"))]
        self.attn_sfree = list(s_free)
        self.attn_nfree = list(n_free)
        self.attn_ofree = list(o_free)
        self.attn_cons_last = ("pe", p.cnt("pe"))

    def emit_ln(self, src_tiles, tok, in_ready, sq_scr, ab_A, ab_B,
                rX, rQ, row_ps, row_ps_free, scr_free=None, nfeat=E):
        p, w = self.p, self.w
        nsp = tok // 512
        assert nsp <= 2
        sq_rel = [None, None]
        for cc in range(NE):
            lane = cc % 2
            rdy = in_ready[cc]

            def sq(s, c, cc=cc, lane=lane):
                return s.activation(out=sq_scr[lane][:, 0:tok],
                                    in_=src_tiles[cc][:, 0:tok], func=AF.Square)
            wt0 = [rdy, sq_rel[lane]]
            if cc < 2:
                wt0.append(scr_free)
            w("act", sq, waits=wt0, inc="ac")
            sq_ac = p.cnt("ac")
            for sp in range(nsp):
                def mmx(t, c, cc=cc, sp=sp):
                    return t.matmul(row_ps[sp], self.onecol_b[:, :],
                                    src_tiles[cc][:, sp * 512:(sp + 1) * 512],
                                    start=(cc == 0), stop=(cc == NE - 1))
                wt_ = []
                if sp == 0:
                    wt_.append(rdy)
                if cc == 0:
                    wt_.append(row_ps_free[sp])
                w("pe", mmx, waits=wt_, inc="pe")
            for sp in range(nsp):
                def mmq(t, c, cc=cc, sp=sp, lane=lane):
                    return t.matmul(row_ps[2 + sp], self.onecol_b[:, :],
                                    sq_scr[lane][:, sp * 512:(sp + 1) * 512],
                                    start=(cc == 0), stop=(cc == NE - 1))
                wt_ = []
                if sp == 0:
                    wt_.append(("ac", sq_ac))
                if cc == 0:
                    wt_.append(row_ps_free[2 + sp])
                w("pe", mmq, waits=wt_, inc="pe")
            sq_rel[lane] = ("pe", p.cnt("pe"))
        stats_pe = p.cnt("pe")

        def rows_calc(v, c):
            ch = self.chain
            for sp in range(nsp):
                v.tensor_copy(rX[0:1, sp * 512:(sp + 1) * 512], row_ps[sp])
            for sp in range(nsp):
                i = v.tensor_copy(rQ[0:1, sp * 512:(sp + 1) * 512], row_ps[2 + sp])
            ch(v, c, i)
            ch(v, c, v.tensor_scalar(rX, rX, 1.0 / nfeat, None, op0=ALU.mult))
            ch(v, c, v.tensor_scalar(rQ, rQ, 1.0 / nfeat, None, op0=ALU.mult))
            ch(v, c, v.tensor_mul(self.rowM[0:1, 0:tok], rX, rX))
            ch(v, c, v.tensor_sub(rQ, rQ, self.rowM[0:1, 0:tok]))
            return v.tensor_scalar(rQ, rQ, LN_EPS, None, op0=ALU.add)
        w("dve", rows_calc, waits=[("pe", stats_pe), self.rowm_free], inc="dv")
        rc_dv = p.cnt("dv")

        def rows_ln(s, c):
            ch = self.chain
            ch(s, c, s.activation(out=rQ, in_=rQ, func=AF.Ln), key="cac")
            return s.activation(out=rQ, in_=rQ, func=AF.Exp, scale=-0.5)
        w("act", rows_ln, waits=[("dv", rc_dv)], inc="ac")
        rstd_ac = p.cnt("ac")

        def rows_fin(v, c):
            ch = self.chain
            ch(v, c, v.tensor_mul(rX, rX, rQ))
            return v.tensor_scalar(rX, rX, -1.0, None, op0=ALU.mult)
        w("dve", rows_fin, waits=[("ac", rstd_ac)], inc="dv")
        rows_dv = p.cnt("dv")

        ab_ac = None
        bc_ps = row_ps[0].tensor if False else None
        for src_row, dst in ((rQ, ab_A), (rX, ab_B)):
            for sp in range(nsp):
                def bmm(t, c, src_row=src_row, sp=sp):
                    return t.matmul(self.bcast_bank[:, 0:512], self.onerow_f[:, :],
                                    src_row[0:1, sp * 512:(sp + 1) * 512],
                                    start=True, stop=True)
                wt_ = [("dv", rows_dv)]
                if ab_ac is not None:
                    wt_.append(("ac", ab_ac))
                else:
                    wt_.append(self.bcast_bank_free)
                w("pe", bmm, waits=wt_, inc="pe")

                def bcp(s, c, dst=dst, sp=sp):
                    return s.copy(out=dst[:, sp * 512:(sp + 1) * 512],
                                  in_=self.bcast_bank[:, 0:512])
                wt2 = [("pe", p.cnt("pe"))]
                if ab_ac is None:
                    wt2.append(scr_free)
                w("act", bcp, waits=wt2, inc="ac")
                ab_ac = p.cnt("ac")
        self.bcast_bank_free = ("ac", ab_ac)
        self.rowm_free = ("dv", rc_dv)
        return ab_ac, ("pe", p.cnt("pe"))

    def wo_phase(self):
        p, w, sems = self.p, self.w, self.sems
        Bb, C = self.Bb, self.C
        worhs4 = [Bb[:, i * 8192:(i + 1) * 8192] for i in range(4)]
        wt3 = [Bb[:, 32768 + i * 2048:32768 + (i + 1) * 2048] for i in range(2)]
        evo = [Bb[:, 36864 + i * 512:36864 + (i + 1) * 512] for i in range(2)]
        asq3 = [Bb[:, 37888 + i * 512:37888 + (i + 1) * 512] for i in range(2)]
        ev2 = [Bb[:, i * 2048:(i + 1) * 2048] for i in range(2)]
        ab3_A = Bb[:, 40960 + 2048 * 0:43008]
        ab3_B = Bb[:, 43008:45056]
        xre = C[:, 0:2048]
        xmo = C[:, 2048:4096]
        rowX = self.rows[0:1, 0:2048]
        rowQ = self.rows[0:1, 2048:4096]

        ad = self.attn_done
        rhs_ld = []
        for t4 in range(4):
            def lr(s, c, t4=t4):
                return s.dma_start(
                    out=worhs4[t4].rearrange("p (c t) -> p c t", t=512),
                    in_=self.attn_dram[:, t4 * 512:(t4 + 1) * 512]
                    .rearrange("(c p) t -> p c t", p=128))
            fw = (list(ad) if t4 == 0 else []) + [self.attn_cons_last]
            rhs_ld.append(self.sc_m.load(lr, free=fw))

        def zrows(v, c):
            ch = self.chain
            ch(v, c, v.memset(rowX, 0.0))
            return v.memset(rowQ, 0.0)
        w("dve", zrows, waits=[self.attn_cons_last], inc="dv")
        rz_dv = p.cnt("dv")

        wo_acc = Rot(2)
        wo_acc.rel = [self.attn_ofree[0], self.attn_ofree[1]]
        evo_rot = Rot(2)
        asq_rot = Rot(2)
        ev_wr = {}
        prow_free = [self.attn_nfree[0], self.attn_nfree[1]]
        acc_dv = rz_dv
        for fc in range(NE):
            wl_lane = fc % 2
            wl_rel = self.cell.get(("wt3_rel", wl_lane))

            def ldw(g, c, fc=fc, wl_lane=wl_lane):
                return g.dma_start(
                    out=wt3[wl_lane].rearrange("p (c f) -> p c f", f=128),
                    in_=self.woT_t[fc].rearrange("(c p) f -> p c f", p=128))
            wl = self.gp_st.load(ldw, free=[wl_rel])

            for t4 in range(4):
                a_lane, a_rel = wo_acc.acquire()
                acc = self.ps_acc[a_lane]
                for hd in range(NE):
                    def mm(t, c, acc=acc, wl_lane=wl_lane, hd=hd, t4=t4):
                        return t.matmul(
                            acc[:, :], wt3[wl_lane][:, hd * 128:(hd + 1) * 128],
                            worhs4[t4][:, hd * 512:(hd + 1) * 512],
                            start=(hd == 0), stop=(hd == NE - 1))
                    wt_ = []
                    if hd == 0:
                        wt_ = [wl, a_rel]
                        if fc == 0:
                            wt_.append(rhs_ld[t4])
                    w("pe", mm, waits=wt_, inc="pe")
                mm_pe = p.cnt("pe")

                e_lane, e_rel = evo_rot.acquire()
                e_rel_dv = self.cell.get(("evo_dv", e_lane))

                def ev(s, c, acc=acc, e_lane=e_lane):
                    return s.copy(out=evo[e_lane], in_=acc[:, :])
                w("act", ev, waits=[("pe", mm_pe), e_rel, e_rel_dv], inc="ac")
                ev_ac = p.cnt("ac")
                wo_acc.release(a_lane, "ac", ev_ac)

                q_lane, q_rel = asq_rot.acquire()

                def sq(s, c, e_lane=e_lane, q_lane=q_lane):
                    return s.activation(out=asq3[q_lane], in_=evo[e_lane],
                                        func=AF.Square)
                w("act", sq, waits=[q_rel], inc="ac")
                sq_ac = p.cnt("ac")

                def mmx(t, c, e_lane=e_lane):
                    return t.matmul(self.psC[0:1, 0:512], self.onecol_b[:, :],
                                    evo[e_lane], start=True, stop=True)
                w("pe", mmx, waits=[("ac", ev_ac), prow_free[0]], inc="pe")
                prow_free[0] = None

                def mmq(t, c, q_lane=q_lane):
                    return t.matmul(self.psD[0:1, 0:512], self.onecol_b[:, :],
                                    asq3[q_lane], start=True, stop=True)
                w("pe", mmq, waits=[("ac", sq_ac), prow_free[1]], inc="pe")
                prow_free[1] = None
                rows_pe = p.cnt("pe")

                def racc(v, c, t4=t4):
                    ch = self.chain
                    sl = slice(t4 * 512, (t4 + 1) * 512)
                    ch(v, c, v.tensor_add(rowX[0:1, sl], rowX[0:1, sl],
                                          self.psC[0:1, 0:512]))
                    return v.tensor_add(rowQ[0:1, sl], rowQ[0:1, sl],
                                        self.psD[0:1, 0:512])
                w("dve", racc, waits=[("pe", rows_pe), ("dv", acc_dv)], inc="dv")
                acc_dv = p.cnt("dv")
                asq_rot.release(q_lane, "dv", acc_dv)
                prow_free = [("dv", acc_dv), ("dv", acc_dv)]
                self.cell[("evo_dv", e_lane)] = ("dv", acc_dv)

                def awr(s_, c, fc=fc, t4=t4, e_lane=e_lane):
                    return s_.dma_start(
                        out=self.attn2_dram[fc * 128:(fc + 1) * 128,
                                            t4 * 512:(t4 + 1) * 512],
                        in_=evo[e_lane])
                val = self.swrite(awr, ("aw0", "aw1"), e_lane,
                                  waits=[("ac", ev_ac)])
                evo_rot.release(e_lane, *val)
            self.cell[("wt3_rel", wl_lane)] = ("pe", p.cnt("pe"))
        a2_done = [("aw0", p.cnt("aw0")), ("aw1", p.cnt("aw1"))]
        stats_dv = acc_dv

        def rows_calc(v, c):
            ch = self.chain
            ch(v, c, v.tensor_scalar(rowX, rowX, 1.0 / E, None, op0=ALU.mult))
            ch(v, c, v.tensor_scalar(rowQ, rowQ, 1.0 / E, None, op0=ALU.mult))
            ch(v, c, v.tensor_mul(self.rowM, rowX, rowX))
            ch(v, c, v.tensor_sub(rowQ, rowQ, self.rowM))
            return v.tensor_scalar(rowQ, rowQ, LN_EPS, None, op0=ALU.add)
        w("dve", rows_calc, waits=[("dv", stats_dv)], inc="dv")

        def rows_ln(s, c):
            ch = self.chain
            ch(s, c, s.activation(out=rowQ, in_=rowQ, func=AF.Ln), key="cac")
            return s.activation(out=rowQ, in_=rowQ, func=AF.Exp, scale=-0.5)
        w("act", rows_ln, waits=[("dv", p.cnt("dv"))], inc="ac")

        def rows_fin(v, c):
            ch = self.chain
            ch(v, c, v.tensor_mul(rowX, rowX, rowQ))
            return v.tensor_scalar(rowX, rowX, -1.0, None, op0=ALU.mult)
        w("dve", rows_fin, waits=[("ac", p.cnt("ac"))], inc="dv")
        rows_dv = p.cnt("dv")

        ab_ac = None
        for src_row, dst in ((rowQ, ab3_A), (rowX, ab3_B)):
            for sp in range(4):
                def bmm(t, c, src_row=src_row, sp=sp):
                    return t.matmul(self.psS0[:, 0:512], self.onerow_f[:, :],
                                    src_row[0:1, sp * 512:(sp + 1) * 512],
                                    start=True, stop=True)
                wt_ = [("dv", rows_dv)]
                wt_.append(("ac", ab_ac) if ab_ac is not None else self.attn_sfree[0])
                w("pe", bmm, waits=wt_, inc="pe")

                def bcp(s, c, dst=dst, sp=sp):
                    return s.copy(out=dst[:, sp * 512:(sp + 1) * 512],
                                  in_=self.psS0[:, 0:512])
                w("act", bcp, waits=[("pe", p.cnt("pe"))], inc="ac")
                ab_ac = p.cnt("ac")
        ab3_ac = ab_ac
        self.bcast_s0_free = ("ac", ab_ac)

        xre_free = None
        ev2_rot = Rot(2)
        xmo_free = None
        for fc in range(NE):
            e_lane, e_rel = ev2_rot.acquire()

            def lda(s, c, fc=fc, e_lane=e_lane):
                return s.dma_start(out=ev2[e_lane],
                                   in_=self.attn2_dram[fc * 128:(fc + 1) * 128, :])
            fw = [e_rel]
            if fc < 2:
                fw += a2_done
            al = self.sc_v.load(lda, free=fw)

            def ldx(s, c, fc=fc):
                return s.dma_start(out=xre[:, :],
                                   in_=self.x_in[fc * 128:(fc + 1) * 128, 0:TQ])
            fwx = [xre_free]
            if fc < 2:
                fwx.append(("dv", self.quant_last_dv))
            xl = self.sc_k.load(ldx, free=fwx)

            def xmcalc(v, c, fc=fc, e_lane=e_lane):
                ch = self.chain
                ch(v, c, v.tensor_mul(xmo, ev2[e_lane], ab3_A))
                ch(v, c, v.tensor_add(xmo, xmo, ab3_B))
                return v.tensor_add(xmo, xmo, xre)
            w("dve", xmcalc, waits=[al, xl, ("ac", ab3_ac), xmo_free], inc="dv")
            xm_dv = p.cnt("dv")
            ev2_rot.release(e_lane, "dv", xm_dv)
            xre_free = ("dv", xm_dv)

            def xwr(s_, c, fc=fc):
                return s_.dma_start(
                    out=self.xmid_dram[fc * 128:(fc + 1) * 128, :], in_=xmo)
            xmo_free = self.swrite(xwr, ("xw0", "xw1"), fc % 2,
                                   waits=[("dv", xm_dv)])
        self.xmid_done = [("xw0", p.cnt("xw0")), ("xw1", p.cnt("xw1"))]
        self.wo_rel = list(wo_acc.rel)
        self.wo_psC_free = prow_free[0]
        self.wo_psD_free = prow_free[1]
        self.wo_worhs_users = ("pe", p.cnt("pe"))
        self.rows_free = ("pe", p.cnt("pe"))

    def debug_dump3(self):
        p, sems = self.p, self.sems
        dbg = self.dbg
        xd = self.xmid_done

        def d3(s_, c):
            for s, v in xd:
                s_.wait_ge(sems[s], v)
            s_.dma_start(out=dbg["xm"][:, :],
                         in_=self.xmid_dram[:, :]).then_inc(sems["ow"], 16)
            s_.wait_ge(sems["ow"], 16)
            return None
        p.prog["sp"].append(((), d3, None, 0))

    def debug_dump2(self):
        p, sems = self.p, self.sems
        dbg = self.dbg
        ad = self.attn_done

        def d2(s_, c):
            for s, v in ad:
                s_.wait_ge(sems[s], v)
            s_.dma_start(out=dbg["at"][:, :],
                         in_=self.attn_dram[:, :]).then_inc(sems["ow"], 16)
            s_.wait_ge(sems["ow"], 16)
            return None
        p.prog["sp"].append(((), d2, None, 0))

    def ffn_phase(self):
        p, w, sems = self.p, self.w, self.sems
        Bb, C, Eb, Fb = self.Bb, self.C, self.Eb, self.Fb
        xn2, y3b, hidv = self.xn2, self.y3b, self.hidv
        qw1v, qw2v, qw3v = self.qw1v, self.qw2v, self.qw3v
        raw1, raw2, sig4 = self.raw1, self.raw2, self.sig4
        misc_f = self.misc_f
        row4X, row4Q = self.row4X, self.row4Q
        self.rowM = C[0:1, 2048:4096]
        self.rowm_free = None
        self.bcast_bank = self.psS0[:, 0:512]
        self.bcast_bank_free = self.bcast_s0_free

        FLAG = self.FLAG

        def spin2(s_, c):
            s_.wait_ge(sems["qf0"], 16)
            with s_.register("r_nd2") as rnd, s_.register("r_f2") as rf0, \
                    s_.register("r_f3") as rf1, s_.register("r_non2") as rnon:
                s_.reg_load(rnon, self.cidt[0:1, 1:2])
                s_.reg_mov(rnd, 1)
                with s_.While(rnd):
                    s_.sem_clear(sems["poll"])
                    s_.dma_start(out=self.flagt[0:1, 2:4],
                                 in_=FLAG[0:1, 2:4]).then_inc(sems["poll"], 16)
                    s_.wait_ge(sems["poll"], 16)
                    s_.reg_load(rf0, self.flagt[0:1, 2:3])
                    s_.reg_load(rf1, self.flagt[0:1, 3:4])
                    s_.reg_alu(rf0, rf0, rnon, ALU.is_equal)
                    s_.reg_alu(rf1, rf1, rnon, ALU.is_equal)
                    s_.reg_alu(rnd, rf0, rf1, ALU.add)
                    s_.reg_alu(rnd, rnd, 2, ALU.is_lt)
            s_.sem_clear(sems["poll"])
            return None
        p.prog["sp"].append(((), spin2, None, 0))
        w("sp", lambda s_, c: s_.nop(), inc="bar2")
        BAR2 = p.cnt("bar2")
        psS1_free = self.attn_sfree[1]

        y_acc = Rot(2)
        y_acc.rel = list(self.wo_rel)
        y2_rel = [self.wo_psC_free, self.wo_psD_free]
        qwst = {}
        hf_state = {}

        for hf in range(2):
            hcols = slice(hf * 1024, (hf + 1) * 1024)

            x2ld = []
            for cc in range(NE):
                def ldm(g, c, cc=cc, hcols=hcols):
                    return g.dma_start(
                        out=xn2[cc],
                        in_=self.xmid_dram[cc * 128:(cc + 1) * 128, hcols])
                if hf == 0:
                    fw = list(self.xmid_done) + [self.attn_cons_last] \
                        if cc == 0 else []
                else:
                    fw = [("pe", hf_state[0]["xn2_users"])] if cc == 0 else []
                x2ld.append(self.gp_st.load(ldm, free=fw))
            row_ps = [self.psC[0:1, 0:512], self.psD[0:1, 0:512],
                      self.psS1[0:1, 0:512], self.psS1[0:1, 512:1024]]
            if hf == 0:
                row_ps_free = [y2_rel[0], y2_rel[1], psS1_free, psS1_free]
                scr_free = self.wo_worhs_users
            else:
                hd0 = hf_state[0]
                row_ps_free = [hd0["y2_free"][0], hd0["y2_free"][1],
                               hd0["psS1_free"], hd0["psS1_free"]]
                scr_free = hd0["ocalc_done"]
            self.rowm_free = ("dv", p.cnt("dv"))
            ab_ac, _bc_pe = self.emit_ln(
                xn2, 1024, x2ld, [hidv[0], hidv[1]], hidv[2], hidv[3],
                row4X, row4Q, row_ps, row_ps_free, scr_free=scr_free)
            psS1_free = ("dv", self.rowm_free[1])
            ap_dv = []
            for cc in range(NE):
                lane = cc % 2

                def ap1(v, c, cc=cc, lane=lane):
                    return v.tensor_mul(hidv[4 + lane], xn2[cc], hidv[2])

                def ap2(v, c, cc=cc, lane=lane):
                    return v.tensor_add(xn2[cc], hidv[4 + lane], hidv[3])
                wt_ = [("ac", ab_ac)]
                if cc < 2:
                    wt_.append(scr_free)
                if cc >= 2:
                    wt_.append(("dv", ap_dv[cc - 2]))
                w("dve", ap1, waits=wt_, inc="dv")
                w("dve", ap2, waits=[("dv", p.cnt("dv"))], inc="dv")
                ap_dv.append(p.cnt("dv"))
            xn2_dv = p.cnt("dv")
            if self.debug == 4 and hf == 0:
                for cc in range(NE):
                    def dxw(s_, c, cc=cc):
                        return s_.dma_start(
                            out=self.dbg["xn2"][cc * 128:(cc + 1) * 128, :],
                            in_=xn2[cc]).then_inc(sems["ow"], 16) and None
                    p.bump("ow", 16)
                    p.prog["sp"].append(((("dv", xn2_dv),), dxw, None, 0))

            first_hm = True
            for hc in range(NH):
                src1 = (self.qwA1_sh[hc] if hc < NH // 2
                        else self.qwB1_sh[hc - NH // 2])
                src2 = (self.qwA2_sh[hc] if hc < NH // 2
                        else self.qwB2_sh[hc - NH // 2])

                def lq1(s, c, src1=src1):
                    return s.dma_start(out=qw1v, in_=src1)

                def lq2(s, c, src2=src2):
                    return s.dma_start(out=qw2v, in_=src2)
                fw1 = [qwst.get(("mm1",))]
                fw2 = [qwst.get(("mm2",))]
                if hf == 0 and hc == 0:
                    ebf = [("bar2", BAR2), ("dv", self.quant_last_dv)] \
                        + list(self.quant_ww_done)
                    fw1 += ebf
                    fw2 += ebf
                qw_rdy1 = self.sc_k.load(lq1, free=fw1)
                qw_rdy2 = self.sc_v.load(lq2, free=fw2)

                for ts_ in range(2):
                    a_lane, a_rel = y_acc.acquire()
                    acc1 = self.ps_acc[a_lane]
                    acc2 = [self.psC, self.psD][a_lane]
                    for e_ in range(NE):
                        def m1(t, c, acc1=acc1, e_=e_, ts_=ts_):
                            return t.matmul(
                                acc1[:, :], qw1v[:, e_ * 128:(e_ + 1) * 128],
                                xn2[e_][:, ts_ * 512:(ts_ + 1) * 512],
                                start=(e_ == 0), stop=(e_ == NE - 1))
                        wt_ = [qw_rdy1, ("dv", xn2_dv), a_rel] if e_ == 0 else []
                        w("pe", m1, waits=wt_, inc="pe")
                    for e_ in range(NE):
                        def m2(t, c, acc2=acc2, e_=e_, ts_=ts_):
                            return t.matmul(
                                acc2[:, :], qw2v[:, e_ * 128:(e_ + 1) * 128],
                                xn2[e_][:, ts_ * 512:(ts_ + 1) * 512],
                                start=(e_ == 0), stop=(e_ == NE - 1))
                        wt_ = [qw_rdy2, y2_rel[a_lane]] if e_ == 0 else []
                        y2_rel[a_lane] = None
                        w("pe", m2, waits=wt_, inc="pe")
                    mm_pe = p.cnt("pe")

                    def sg(s, c, acc1=acc1, ts_=ts_):
                        return s.activation(out=sig4[:, ts_ * 512:(ts_ + 1) * 512],
                                            in_=acc1[:, :], func=AF.Sigmoid)
                    wt_ = [("pe", mm_pe), qwst.get(("hidmul", a_lane))]
                    if hf == 0 and hc == 0 and ts_ == 0:
                        wt_.append(self.quant_fb_free)
                    w("act", sg, waits=wt_, inc="ac")
                    sg_ac = p.cnt("ac")

                    def hm(v, c, acc2=acc2, hc=hc, ts_=ts_):
                        sl = slice(ts_ * 512, (ts_ + 1) * 512)
                        return v.tensor_mul(hidv[hc][:, sl], sig4[:, sl],
                                            acc2[:, :])
                    wt_ = [("ac", sg_ac)]
                    if first_hm:
                        wt_ += [self.wo_worhs_users, ("ac", ab_ac)]
                        if hf == 1:
                            wt_.append(hf_state[0]["y3_users"])
                        first_hm = False
                    w("dve", hm, waits=wt_, inc="dv")
                    rel = ("dv", p.cnt("dv"))
                    y_acc.release(a_lane, *rel)
                    qwst[("hidmul", a_lane)] = rel
                if ts_ == 1:
                    qwst[("mm1",)] = ("pe", mm_pe)
                    qwst[("mm2",)] = ("pe", mm_pe)
            hid_dv = p.cnt("dv")
            hf_state[hf] = {"xn2_users": p.cnt("pe"),
                            "y2_free": [qwst.get(("hidmul", 0)),
                                        qwst.get(("hidmul", 1))],
                            "psS1_free": psS1_free}

            first_ev3 = True
            for fc in range(NE):
                src3 = (self.qwA3_sh[fc] if fc < NE // 2
                        else self.qwB3_sh[fc - NE // 2])

                def lq3(s, c, src3=src3):
                    return s.dma_start(out=qw3v, in_=src3)
                fw = [qwst.get(("mm3",))]
                if fc == 0:
                    fw.append(("dv", hid_dv))
                if hf == 0 and fc == 0:
                    fw += [("bar2", BAR2), self.quant_fb_free,
                           ("dv", self.quant_last_dv)]
                qw3_rdy = self.sc_q.load(lq3, free=fw)

                for ts_ in range(2):
                    a_lane, a_rel = y_acc.acquire()
                    acc = self.ps_acc[a_lane]
                    for hc in range(NH):
                        def m3(t, c, acc=acc, hc=hc, ts_=ts_):
                            return t.matmul(
                                acc[:, :], qw3v[:, hc * 128:(hc + 1) * 128],
                                hidv[hc][:, ts_ * 512:(ts_ + 1) * 512],
                                start=(hc == 0), stop=(hc == NH - 1))
                        wt_ = [qw3_rdy, ("dv", hid_dv), a_rel] if hc == 0 else []
                        w("pe", m3, waits=wt_, inc="pe")
                    mm_pe = p.cnt("pe")
                    qwst[("mm3",)] = ("pe", mm_pe)

                    def ev3(s, c, acc=acc, fc=fc, ts_=ts_):
                        return s.copy(out=y3b[fc][:, ts_ * 512:(ts_ + 1) * 512],
                                      in_=acc[:, :])
                    wt_ = [("pe", mm_pe)]
                    if first_ev3:
                        if hf == 0:
                            wt_.append(self.attn_cons_last)
                        else:
                            wt_.append(hf_state[0]["ocalc_done"])
                        first_ev3 = False
                    w("act", ev3, waits=wt_, inc="ac")
                    y_acc.release(a_lane, "ac", p.cnt("ac"))
            y3_ac = p.cnt("ac")
            hf_state[hf]["y3_users"] = ("pe", p.cnt("pe"))
            if self.debug == 4 and hf == 0:
                for cc in range(NE):
                    def dyw(s_, c, cc=cc):
                        return s_.dma_start(
                            out=self.dbg["y3"][cc * 128:(cc + 1) * 128, :],
                            in_=y3b[cc]).then_inc(sems["ow"], 16) and None
                    p.bump("ow", 16)
                    p.prog["sp"].append(((("ac", y3_ac),), dyw, None, 0))

                def dsw(s_, c):
                    s_.dma_start(out=self.dbg["sig"][:, :],
                                 in_=sig4[:, :]).then_inc(sems["ow"], 16)
                    s_.dma_start(out=self.dbg["hid"][:, :],
                                 in_=hidv[NH - 1][:, :]).then_inc(sems["ow"], 16)
                    return None
                p.bump("ow", 32)
                p.prog["sp"].append(((("ac", y3_ac),), dsw, None, 0))

            row_ps = [self.psC[0:1, 0:512], self.psD[0:1, 0:512],
                      self.psS1[0:1, 0:512], self.psS1[0:1, 512:1024]]
            row_ps_free = [hf_state[hf]["y2_free"][0],
                           hf_state[hf]["y2_free"][1],
                           psS1_free, psS1_free]
            in_rdy = [("ac", y3_ac)] * NE
            self.rowm_free = ("dv", p.cnt("dv"))
            abf_ac, _ = self.emit_ln(
                y3b, 1024, in_rdy, [hidv[4], hidv[5]], hidv[0], hidv[1],
                row4X, row4Q, row_ps, row_ps_free,
                scr_free=("ac", y3_ac))
            psS1_free = ("dv", self.rowm_free[1])

            out_free = None
            ld_free = None
            last_q_dv = ("dv", p.cnt("dv"))
            for cc in range(NE):
                def ldm2(s, c, cc=cc, hcols=hcols):
                    return s.dma_start(
                        out=raw1[:, 0:1024],
                        in_=self.xmid_dram[cc * 128:(cc + 1) * 128, hcols])
                fw = [ld_free]
                if cc == 0 and hf == 0:
                    fw.append(("dv", self.quant_last_dv))
                    fw += list(self.xmid_done)
                if cc == 0:
                    fw.append(("dv", self.rowm_free[1]))
                xl = self.sc_m.load(ldm2, free=fw)

                def ocalc(v, c, cc=cc):
                    ch = self.chain
                    ch(v, c, v.tensor_mul(raw2[:, 0:1024], y3b[cc], hidv[0]))
                    ch(v, c, v.tensor_add(raw2[:, 0:1024], raw2[:, 0:1024],
                                          hidv[1]))
                    return v.tensor_add(raw2[:, 0:1024], raw2[:, 0:1024],
                                        raw1[:, 0:1024])
                w("dve", ocalc, waits=[xl, ("ac", abf_ac), out_free], inc="dv")
                oc_dv = p.cnt("dv")

                def owr(s_, c, cc=cc, hcols=hcols):
                    return s_.dma_start(
                        out=self.out_d[cc * 128:(cc + 1) * 128, hcols],
                        in_=raw2[:, 0:1024])
                val = self.swrite(owr, ("ow0", "ow1"), cc % 2,
                                  waits=[("dv", oc_dv)])
                out_free = val
                ld_free = ("dv", oc_dv)
            hf_state[hf]["ocalc_done"] = ("dv", p.cnt("dv"))
            if hf == 0:
                psCD_free = ("dv", self.rowm_free[1])
                qwst[("mm1",)] = ("pe", hf_state[0]["xn2_users"])
                qwst[("mm2",)] = ("pe", hf_state[0]["xn2_users"])
                qwst[("mm3",)] = hf_state[0]["y3_users"]
                qwst[("hidmul", 0)] = psCD_free
                qwst[("hidmul", 1)] = psCD_free
                y2_rel = [psCD_free, psCD_free]
                hf_state[0]["y2_free"] = [psCD_free, psCD_free]

        ow_tot = [("ow0", p.cnt("ow0")), ("ow1", p.cnt("ow1"))]

        def fin(s_, c):
            for s, v in ow_tot:
                s_.wait_ge(sems[s], v)
            return None
        p.prog["sp"].append(((), fin, None, 0))

    def debug_dump1(self):
        p, sems = self.p, self.sems
        dbg = self.dbg

        def d1(s_, c):
            s_.dma_start(out=dbg["kg"][:, :], in_=self.KG[:, :]).then_inc(sems["ow"], 16)
            s_.dma_start(out=dbg["vg"][:, :], in_=self.VG[:, :]).then_inc(sems["ow"], 16)
            s_.dma_start(out=dbg["gp"][0:1, 0:32], in_=self.GP[0:1, 0:32]).then_inc(sems["ow"], 16)
            s_.dma_start(out=dbg["gp"][0:1, 64:96], in_=self.GP[0:1, 64:96]).then_inc(sems["ow"], 16)
            s_.dma_start(out=dbg["q"][:, :], in_=self.q_dram[:, :]).then_inc(sems["ow"], 16)
            s_.wait_ge(sems["ow"], 80)
            return None
        p.prog["sp"].append(((("bar", self.BAR),), d1, None, 0))

    def finish(self):
        nc, p, sems, cell = self.nc, self.p, self.sems, self.cell
        cid_ap = self.cidt[0:1, 0:1]
        with nc.Block() as block:
            @block.tensor
            def _(t):
                run_prog(t, p.prog["pe"], sems, cell)

            @block.scalar
            def _(s_):
                with s_.register("a_par") as rp:
                    cell["a_par"] = rp
                    s_.wait_ge(sems["gl0"], self.cid_ld)
                    s_.reg_load(rp, cid_ap)
                    s_.reg_alu(rp, rp, 1, ALU.arith_shift_right)
                    run_prog(s_, p.prog["act"], sems, cell)

            @block.vector
            def _(v):
                run_prog(v, p.prog["dve"], sems, cell)

            @block.gpsimd
            def _(g):
                run_prog(g, p.prog["gp"], sems, cell)

            @block.sync
            def _(s_):
                with s_.register("s_cls") as rc:
                    cell["s_cls"] = rc
                    s_.wait_ge(sems["gl0"], self.cid_ld)
                    s_.reg_load(rc, cid_ap)
                    run_prog(s_, p.prog["sp"], sems, cell)
        self.st.close()
        self.verify_sems()

    def verify_sems(self):
        import re
        from collections import defaultdict
        totals = defaultdict(int)
        pat = re.compile(r"update:S\[(\w+)\]\+[+=]?(\d+)")
        for inst in self.nc.all_instructions():
            for m in pat.finditer(str(inst)):
                totals[m.group(1)] += int(m.group(2))
        bad = []
        for name, expect in self.p.sem_static.items():
            actual = totals.get(name, 0)
            if actual != expect and name not in ("poll",):
                bad.append((name, expect, actual))
        if bad:
            raise AssertionError(f"sem count mismatches (plan vs built): {bad}")


def run_prog(eng, prog, sems, cell):
    for waits, fn, inc, n in prog:
        for s, v in waits:
            eng.wait_ge(sems[s], v)
        instr = fn(eng, cell)
        if inc is not None and instr is not None:
            instr.then_inc(sems[inc], n)



def build_nc(debug=0):
    mk = MK(debug)
    mk.phase0()
    mk.phase1()
    if debug == 1:
        mk.debug_dump1()
        mk.finish()
        return mk.nc, mk.dbg
    mk.attn_phase()
    if debug == 2:
        mk.debug_dump2()
        mk.finish()
        return mk.nc, mk.dbg
    mk.wo_phase()
    if debug == 3:
        mk.debug_dump3()
        mk.finish()
        return mk.nc, mk.dbg
    mk.ffn_phase()
    mk.finish()
    return mk.nc, mk.dbg



def _tile_w(wT):
    ein, f = wT.shape
    return np.ascontiguousarray(wT.reshape(ein, f // 128, 128).transpose(1, 0, 2))


def _rope_tables():
    inv = (1.0 / (np.float32(ROPE_BASE) **
                  (np.arange(0, D, 2, dtype=np.float32) / np.float32(D)))).astype(np.float32)
    t = np.arange(T, dtype=np.float32)
    f = t[:, None] * inv[None, :]
    cos = np.concatenate([np.cos(f), np.cos(f)], axis=-1).astype(np.float32)
    sin = np.concatenate([np.sin(f), np.sin(f)], axis=-1).astype(np.float32)
    return np.ascontiguousarray(cos.T), np.ascontiguousarray(sin.T)


def _masks(j):
    import ml_dtypes
    out = np.zeros((128, 16, 4, 128), dtype=np.float32)
    tri = (np.arange(128)[:, None] <= np.arange(128)[None, :]).astype(np.float32)
    for beta in range(16):
        for m in range(4):
            d = beta - 4 * m - j
            if d < 0:
                out[:, beta, m, :] = 1.0
            elif d == 0:
                out[:, beta, m, :] = tri
    return np.ascontiguousarray(out.reshape(128, 16 * 512)).astype(ml_dtypes.bfloat16)


_CACHE = {}


def prep(args):
    _CACHE["rope"] = _rope_tables()
    _CACHE["wt"] = {
        "q": _tile_w(np.ascontiguousarray(np.asarray(args["L0_attn_wq"]).T)),
        "k": _tile_w(np.ascontiguousarray(np.asarray(args["L0_attn_wk"]).T)),
        "o": _tile_w(np.ascontiguousarray(np.asarray(args["L0_attn_wo"]).T)),
        "v": np.ascontiguousarray(np.asarray(args["L0_attn_wv"]).T),
        "1": _tile_w(np.ascontiguousarray(np.asarray(args["L0_ffn_w1"]).T)),
        "2": _tile_w(np.ascontiguousarray(np.asarray(args["L0_ffn_w2"]).T)),
        "3": _tile_w(np.ascontiguousarray(np.asarray(args["L0_ffn_w3"]).T)),
    }
    wt = _CACHE["wt"]
    wt["1r"] = np.ascontiguousarray(np.roll(wt["1"], -(NH // 2), axis=0))
    wt["2r"] = np.ascontiguousarray(np.roll(wt["2"], -(NH // 2), axis=0))
    wt["3r"] = np.ascontiguousarray(np.roll(wt["3"], -(NE // 2), axis=0))


def make_inputs(core, args, nonce):
    b = core // 4
    j = CLASS_OF_L[core % 4]
    par = core % 2
    toks = token_ids(j)
    x = np.asarray(args["L0_attn_x"])[b]
    cos, sin = _CACHE["rope"]
    wt = _CACHE["wt"]
    if par == 0:
        w1s, w2s, w3s = wt["1"], wt["2"], wt["3"]
    else:
        w1s, w2s, w3s = wt["1r"], wt["2r"], wt["3r"]
    return {
        "x": np.ascontiguousarray(x[toks].T),
        "wqT_t": wt["q"], "wkT_t": wt["k"], "woT_t": wt["o"], "wvT": wt["v"],
        "w1T_t": w1s, "w2T_t": w2s, "w3T_t": w3s,
        "cosT": np.ascontiguousarray(cos[:, toks]),
        "sinT": np.ascontiguousarray(
            np.concatenate([-sin[0:D // 2, toks], sin[D // 2:, toks]], axis=0)),
        "masks": _masks(j),
        "cls": np.array([[j]], dtype=np.int32),
        "nonce": np.array([[nonce]], dtype=np.int32),
    }


def kernel(**inputs):
    prep(inputs)
    nonce = int(np.random.randint(1, 2**31 - 1))
    nc, _ = build_nc(debug=0)
    in_maps = [make_inputs(c, inputs, nonce) for c in range(8)]
    res = run_bass_kernel_spmd(nc, in_maps, core_ids=list(range(8)))
    out = np.zeros((B, T, E), dtype=np.float32)
    for c in range(8):
        b = c // 4
        j = CLASS_OF_L[c % 4]
        toks = token_ids(j)[:TQ]
        out[b, toks, :] = np.asarray(res.results[c]["out"]).T
    return out
